# version 05
# baseline (speedup 1.0000x reference)
import numpy as np
import bass_rust
from contextlib import ExitStack
import concourse.bass as bass
import concourse.mybir as mybir
from concourse.bass_utils import run_bass_kernel_spmd

F32 = mybir.dt.float32
BF16 = mybir.dt.bfloat16
AF = mybir.ActivationFunctionType
ALU = mybir.AluOpType
AX = mybir.AxisListType.X

NCORES = 8
D = 1024
TTOT = 16384
T = TTOT // NCORES
TT = 512
NTILES = T // TT
NB = TT // 128
HEADS = 4
DH = 256
FF = 4096
E = 8
EPS = 1e-5
SCALE = DH ** (-0.25)
NCH = D // 128
NFC = FF // 128
GELU_A = 0.7978845608028654
GELU_B = 0.044715

DEBUG = {}

ENGINES = ("sync", "gpsimd", "scalar", "vector", "tensor")


NDMALANES = 8


class Op:
    __slots__ = ("idx", "eng", "fn", "reads", "writes", "kind", "tag",
                 "needs_event", "event_no", "waits", "bar_state", "lane")

    def __init__(self, idx, eng, fn, reads, writes, kind, tag):
        self.idx = idx
        self.eng = eng
        self.fn = fn
        self.reads = tuple(reads)
        self.writes = tuple(writes)
        self.kind = kind
        self.tag = tag
        self.needs_event = False
        self.event_no = None
        self.waits = []
        self.lane = None


class TK:

    def __init__(self):
        self.ops = []

    def comp(self, eng, fn, reads=(), writes=(), tag=""):
        op = Op(len(self.ops), eng, fn, reads, writes, "comp", tag)
        self.ops.append(op)
        return op

    def dma(self, queue, fn, reads=(), writes=(), tag=""):
        op = Op(len(self.ops), queue, fn, reads, writes, "dma", tag)
        self.ops.append(op)
        return op

    def cc(self, fn, reads=(), writes=(), tag=""):
        op = Op(len(self.ops), "gpsimd", fn, reads, writes, "cc", tag)
        self.ops.append(op)
        return op

    def barrier(self, tag="bar"):
        op = Op(len(self.ops), None, None, (), (), "barrier", tag)
        self.ops.append(op)
        return op

    def resolve(self):
        last_writer = {}
        readers = {}
        edges = []
        last_op_per_eng = {}
        for op in self.ops:
            if op.kind == "barrier":
                for e, lop in last_op_per_eng.items():
                    if lop.kind == "comp":
                        lop.needs_event = True
                continue
            deps = set()
            for r in op.reads:
                w = last_writer.get(r)
                if w is not None:
                    deps.add(w)
            for w in op.writes:
                lw = last_writer.get(w)
                if lw is not None:
                    deps.add(lw)
                for rd in readers.get(w, {}).values():
                    deps.add(rd)
            for d in deps:
                if d is op:
                    continue
                if d.eng == op.eng == "tensor" and d.kind == "comp" and op.kind == "comp":
                    continue
                edges.append((d, op))
            for r in op.reads:
                readers.setdefault(r, {})[op.eng] = op
            for w in op.writes:
                last_writer[w] = op
                readers[w] = {}
            last_op_per_eng[op.eng] = op
        for p, c in edges:
            if p.kind == "comp":
                p.needs_event = True
        ev_count = {e: 0 for e in ENGINES}
        dma_lane_count = {}
        dma_queue_idx = {e: 0 for e in ENGINES}
        cc_count = 0
        chan_of, val_of = {}, {}
        for op in self.ops:
            if op.kind == "comp":
                if op.needs_event:
                    ev_count[op.eng] += 1
                    op.event_no = ev_count[op.eng]
                chan_of[op] = ("ev", op.eng)
                val_of[op] = op.event_no
            elif op.kind == "dma":
                lane = dma_queue_idx[op.eng] % NDMALANES
                dma_queue_idx[op.eng] += 1
                op.lane = lane
                key = (op.eng, lane)
                dma_lane_count[key] = dma_lane_count.get(key, 0) + 16
                op.event_no = dma_lane_count[key]
                chan_of[op] = ("dma", op.eng, lane)
                val_of[op] = op.event_no
            elif op.kind == "cc":
                cc_count += 1
                op.event_no = cc_count
                chan_of[op] = ("cc", "cc")
                val_of[op] = op.event_no
        edges_by_consumer = {}
        for p, c in edges:
            edges_by_consumer.setdefault(c.idx, []).append(p)
        frontier = {e: {} for e in ENGINES}
        run_ev = {e: 0 for e in ENGINES}
        run_dma = {}
        run_cc = 0
        for op in self.ops:
            if op.kind == "barrier":
                op.bar_state = (dict(run_ev), dict(run_dma), run_cc)
                continue
            need = {}
            for p in edges_by_consumer.get(op.idx, ()):
                v = val_of[p]
                if v is None:
                    continue
                ch = chan_of[p]
                if need.get(ch, 0) < v:
                    need[ch] = v
            if op.kind == "dma" and op.event_no > 16:
                ch = chan_of[op]
                if need.get(ch, 0) < op.event_no - 16:
                    need[ch] = op.event_no - 16
            fr = frontier[op.eng]
            for ch, v in sorted(need.items(), key=lambda kv: str(kv[0])):
                if fr.get(ch, 0) >= v:
                    continue
                op.waits.append((ch, v))
                fr[ch] = v
            if op.kind == "comp":
                if op.needs_event:
                    run_ev[op.eng] = op.event_no
            elif op.kind == "dma":
                run_dma[chan_of[op]] = op.event_no
            elif op.kind == "cc":
                run_cc = op.event_no
        self.stats = dict(n_ops=len(self.ops), ev=dict(ev_count),
                          dma=dict(dma_queue_idx), edges=len(edges))

    def emit(self, block, sems):
        per_eng = {e: [] for e in ENGINES}
        for op in self.ops:
            if op.kind == "barrier":
                ev, dma, cc = op.bar_state
                for e in ENGINES:
                    items = []
                    for e2 in ENGINES:
                        if ev[e2] > 0:
                            items.append((("ev", e2), ev[e2]))
                    for ch, v in dma.items():
                        items.append((ch, v))
                    if cc > 0:
                        items.append((("cc", "cc"), cc))
                    per_eng[e].append(("waits", items))
                continue
            per_eng[op.eng].append(("op", op))

        def emitter(eng_name, eng):
            frontier = {}
            for kind, payload in per_eng[eng_name]:
                if kind == "waits":
                    for ch, v in payload:
                        if frontier.get(ch, 0) >= v:
                            continue
                        eng.wait_ge(sems[ch], v)
                        frontier[ch] = v
                    continue
                op = payload
                for ch, v in op.waits:
                    if frontier.get(ch, 0) >= v:
                        continue
                    eng.wait_ge(sems[ch], v)
                    frontier[ch] = v
                inst = op.fn(eng)
                if op.kind == "dma":
                    inst.then_inc(sems[("dma", op.eng, op.lane)], 16)
                elif op.kind == "cc":
                    inst.then_inc(sems[("cc", "cc")])
                elif op.needs_event:
                    inst.then_inc(sems[("ev", op.eng)], 1)

        @block.sync
        def _(eng):
            emitter("sync", eng)

        @block.gpsimd
        def _(eng):
            emitter("gpsimd", eng)

        @block.scalar
        def _(eng):
            emitter("scalar", eng)

        @block.vector
        def _(eng):
            emitter("vector", eng)

        @block.tensor
        def _(eng):
            emitter("tensor", eng)



def build_program():
    nc = bass.Bass("TRN2", target_bir_lowering=False)
    tk = TK()

    xT = nc.dram_tensor("xT", [D, T], F32, kind="ExternalInput").ap()
    wq = nc.dram_tensor("wq", [D, D], F32, kind="ExternalInput").ap()
    wk = nc.dram_tensor("wk", [D, D], F32, kind="ExternalInput").ap()
    wv = nc.dram_tensor("wv", [D, D], F32, kind="ExternalInput").ap()
    wo = nc.dram_tensor("wo", [D, D], F32, kind="ExternalInput").ap()
    wf1 = nc.dram_tensor("wf1", [D, FF], F32, kind="ExternalInput").ap()
    wf2 = nc.dram_tensor("wf2", [FF, D], F32, kind="ExternalInput").ap()
    wd = nc.dram_tensor("wd", [D, D], F32, kind="ExternalInput").ap()
    w1g = nc.dram_tensor("w1g", [E * 4 * 128, NCH * 1024], F32, kind="ExternalInput").ap()
    w2g = nc.dram_tensor("w2g", [E * 4 * 128, NFC * 256], F32, kind="ExternalInput").ap()
    iotas_in = nc.dram_tensor("iotas", [128, 8], F32, kind="ExternalInput").ap()
    lstr_in = nc.dram_tensor("lstrict", [128, 128], F32, kind="ExternalInput").ap()
    lincl_in = nc.dram_tensor("lincl8", [8, 8], F32, kind="ExternalInput").ap()
    eidx_in = nc.dram_tensor("eidx", [128, 8], F32, kind="ExternalInput").ap()
    rT = nc.dram_tensor("rT", [D, E], F32, kind="ExternalInput").ap()
    vg1 = nc.dram_tensor("vg1", [128, NCH], F32, kind="ExternalInput").ap()
    vb1 = nc.dram_tensor("vb1", [128, NCH], F32, kind="ExternalInput").ap()
    vg2 = nc.dram_tensor("vg2", [128, NCH], F32, kind="ExternalInput").ap()
    vb2 = nc.dram_tensor("vb2", [128, NCH], F32, kind="ExternalInput").ap()
    vbo = nc.dram_tensor("vbo", [128, NCH], F32, kind="ExternalInput").ap()
    vbf2 = nc.dram_tensor("vbf2", [128, NCH], F32, kind="ExternalInput").ap()
    vbd = nc.dram_tensor("vbd", [128, NCH], F32, kind="ExternalInput").ap()
    vbf1 = nc.dram_tensor("vbf1", [128, NFC], F32, kind="ExternalInput").ap()
    vesc = nc.dram_tensor("vesc", [1, E], F32, kind="ExternalInput").ap()
    idn = nc.dram_tensor("idn", [128, 128], F32, kind="ExternalInput").ap()
    outT = nc.dram_tensor("outT", [D, T], F32, kind="ExternalOutput").ap()

    ct_dram = nc.dram_tensor("ct_dram", [E, T], F32)
    rows_dram = nc.dram_tensor("rows_dram", [4, T], F32)
    xnb_dram = nc.dram_tensor("xnb_dram", [T, D], BF16)
    tok_dram = nc.dram_tensor("tok_dram", [16 * TT, 8], F32)
    ys_dram = nc.dram_tensor("ys_dram", [T, D], F32)
    off_dram = nc.dram_tensor("off_dram", [8, 16], F32)
    cum_dram = nc.dram_tensor("cum_dram", [8, 1], F32)
    cc_in = nc.dram_tensor("cc_in", [128, 8 * 257], F32)
    cc_out = nc.dram_tensor("cc_out", [128, 8 * 257], F32)

    taps = {}
    for name, shape in [("xn32", [D, T]), ("ctx", [128, 8 * 257]), ("cmb", [E, T]),
                        ("ctxn", [128, 8 * 256]), ("g", [D, T]), ("h1", [D, T]),
                        ("x2", [D, T]), ("s", [D, T]), ("moe", [D, T]),
                        ("cmdbg", [128, 64]), ("kv", [128, 1024 + 1028]),
                        ("hsb", [128, 2 * 8 * TT]), ("srt", [16 * TT, 8]),
                        ("ys", [T, D])]:
        if DEBUG.get(name):
            taps[name] = nc.dram_tensor("tap_" + name, shape, F32, kind="ExternalOutput").ap()

    stack = ExitStack()
    with stack:
        VEC = stack.enter_context(nc.sbuf_tensor("VEC", [128, 7 * NCH + NFC], F32))
        ESC = stack.enter_context(nc.sbuf_tensor("ESCb", [1, E], F32))
        ONESC = stack.enter_context(nc.sbuf_tensor("ONESC", [128, 1], F32))
        ONESB = stack.enter_context(nc.sbuf_tensor("ONESB", [128, 1], BF16))
        ONESR = stack.enter_context(nc.sbuf_tensor("ONESR", [1, 128], F32))
        EPSR = stack.enter_context(nc.sbuf_tensor("EPSR", [1, 1], F32))
        IDN = stack.enter_context(nc.sbuf_tensor("IDN", [128, 128], F32))
        CTX_BF = stack.enter_context(nc.sbuf_tensor("CTX_BF", [128, 8 * 256], BF16))
        WSL = {
            "WA": stack.enter_context(nc.sbuf_tensor("WA", [128, 8 * 1024], BF16)),
            "WB": stack.enter_context(nc.sbuf_tensor("WB", [128, 8 * 1024], BF16)),
            "WC": stack.enter_context(nc.sbuf_tensor("WC", [128, 8 * 1024], BF16)),
        }
        WROT = ["WA", "WB", "WC"]
        ROWS = stack.enter_context(nc.sbuf_tensor("ROWS", [1, 6 * TT], F32))
        CMB = stack.enter_context(nc.sbuf_tensor("CMBt", [128, 64], F32))
        EXPB = stack.enter_context(nc.sbuf_tensor("EXPB", [128, 16], F32))
        ESCB = stack.enter_context(nc.sbuf_tensor("ESCB", [128, 8], F32))
        IOTAS = stack.enter_context(nc.sbuf_tensor("IOTAS", [128, 8], F32))
        EIDX = stack.enter_context(nc.sbuf_tensor("EIDXs", [128, 8], F32))
        IDNB = stack.enter_context(nc.sbuf_tensor("IDNB", [128, 128], BF16))
        CTS = stack.enter_context(nc.sbuf_tensor("CTS", [8, 128], F32))
        TMPC = stack.enter_context(nc.sbuf_tensor("TMPC", [128, TT], F32))
        TMPC2 = stack.enter_context(nc.sbuf_tensor("TMPC2", [128, TT], F32))
        PS = [stack.enter_context(nc.psum_tensor(f"PS{i}", [128, 512], F32))
              for i in range(8)]

        def row(i):
            return ROWS[:, i * TT:(i + 1) * TT]

        G1 = VEC[:, 0 * NCH:1 * NCH]
        B1 = VEC[:, 1 * NCH:2 * NCH]
        G2 = VEC[:, 2 * NCH:3 * NCH]
        B2 = VEC[:, 3 * NCH:4 * NCH]
        BO = VEC[:, 4 * NCH:5 * NCH]
        BF2 = VEC[:, 5 * NCH:6 * NCH]
        BD = VEC[:, 6 * NCH:7 * NCH]
        BFF1 = VEC[:, 7 * NCH:7 * NCH + NFC]

        tk.comp("vector", lambda v: v.memset(ONESC[:, :], 1.0), writes=["ONESC"])
        tk.comp("vector", lambda v: v.memset(ONESB[:, :], 1.0), writes=["ONESB"])
        tk.comp("vector", lambda v: v.memset(ONESR[:, :], 1.0), writes=["ONESR"])
        tk.comp("vector", lambda v: v.memset(EPSR[:, :], EPS), writes=["EPSR"])
        tk.dma("sync", lambda s: s.dma_start(out=IDN[:, :], in_=idn), writes=["IDN"])
        for i, src in enumerate([vg1, vb1, vg2, vb2, vbo, vbf2, vbd]):
            tk.dma("sync", lambda s, i=i, src=src: s.dma_start(
                out=VEC[:, i * NCH:(i + 1) * NCH], in_=src), writes=["VEC"])
        tk.dma("sync", lambda s: s.dma_start(out=VEC[:, 7 * NCH:7 * NCH + NFC], in_=vbf1),
               writes=["VEC"])
        tk.dma("sync", lambda s: s.dma_start(out=ESC[:, :], in_=vesc), writes=["ESC"])
        tk.dma("sync", lambda s: s.dma_start(out=IOTAS[:, :], in_=iotas_in), writes=["IOTAS"])
        tk.dma("sync", lambda s: s.dma_start(out=EIDX[:, :], in_=eidx_in), writes=["EIDX"])
        tk.comp("scalar", lambda sc: sc.activation(IDNB[:, :], IDN[:, :], AF.Identity),
                reads=["IDN"], writes=["IDNB"], tag="idnb")
        tk.comp("tensor", lambda t: t.matmul(PS[0][:, 0:8], ONESR[:, :], ESC[:, :],
                                             start=True, stop=True),
                reads=["ONESR", "ESC"], writes=["PS0"], tag="escb_mm")
        tk.comp("vector", lambda v: v.tensor_copy(ESCB[:, :], PS[0][:, 0:8]),
                reads=["PS0"], writes=["ESCB"], tag="escb_cp")

        def wload(slot, src_ap3, tag, nchunks=8, width=1024):
            dst = WSL[slot][:, 0:nchunks * width].rearrange("p (c w) -> p c w", w=width)
            tk.dma("gpsimd", lambda g: g.dma_start(out=dst, in_=src_ap3),
                   writes=[slot], tag=tag)

        def w_chunk(slot, c, width=1024):
            return WSL[slot][:, c * width:(c + 1) * width]

        def mm_group(bankslice, terms, psname, reads, tag):
            nt = len(terms)
            for i, term in enumerate(terms):
                lf, rf = term[0], term[1]
                extra = list(term[2]) if len(term) > 2 else []
                tk.comp("tensor",
                        lambda t, lf=lf, rf=rf, i=i: t.matmul(
                            bankslice, lf(), rf(), start=(i == 0), stop=(i == nt - 1)),
                        reads=list(reads) + extra, writes=[psname], tag=tag)

        def bcast(psbank, src_row, src_reads, tag):
            tk.comp("tensor", lambda t: t.matmul(
                PS[psbank][:, :], ONESR[:, :], src_row, start=True, stop=True),
                reads=list(src_reads) + ["ONESR"], writes=[f"PS{psbank}"], tag=tag)

        def stats8(xf32_fn, src_key_fn, sq_fn, sq_key_fn, tag):
            terms = [(lambda c=c: ONESC[:, :], lambda c=c: xf32_fn(c), [src_key_fn(c)])
                     for c in range(NCH)]
            mm_group(PS[0][0:1, :], terms, "PS0", ["ONESC"], tag + "_sumx")
            for c in range(NCH):
                tk.comp("scalar", lambda s, c=c: s.activation(
                    sq_fn(c), xf32_fn(c), AF.Square),
                    reads=[src_key_fn(c)], writes=[sq_key_fn(c)], tag=tag + "_sq")
            terms = [(lambda c=c: ONESB[:, :], lambda c=c: sq_fn(c), [sq_key_fn(c)])
                     for c in range(NCH)]
            mm_group(PS[1][0:1, :], terms, "PS1", ["ONESB"], tag + "_sumsq")
            tk.comp("vector", lambda v: v.tensor_scalar(
                row(0), PS[0][0:1, :], 1.0 / D, None, ALU.mult),
                reads=["PS0"], writes=["ROW0"], tag=tag + "_mu")
            tk.comp("vector", lambda v: v.tensor_scalar(
                row(5), PS[1][0:1, :], 1.0 / D, None, ALU.mult),
                reads=["PS1"], writes=["ROW5"], tag=tag + "_msq")
            tk.comp("vector", lambda v: v.tensor_tensor(
                row(4), row(0), row(0), ALU.mult),
                reads=["ROW0"], writes=["ROW4"], tag=tag + "_mu2")
            tk.comp("vector", lambda v: v.tensor_tensor(
                row(5), row(5), row(4), ALU.subtract),
                reads=["ROW5", "ROW4"], writes=["ROW5"], tag=tag + "_var")

        def rstd_from(var_slot, out_slot, tag):
            tk.comp("scalar", lambda s: s.activation(
                row(4), row(var_slot), AF.Ln, bias=EPSR[0:1, 0:1]),
                reads=[f"ROW{var_slot}", "EPSR"], writes=["ROW4"], tag=tag + "_ln")
            tk.comp("scalar", lambda s: s.activation(
                row(out_slot), row(4), AF.Exp, scale=-0.5),
                reads=["ROW4"], writes=[f"ROW{out_slot}"], tag=tag + "_rstd")

        def ln01_rows(tag):
            rstd_from(5, 1, tag + "_r0")
            tk.comp("vector", lambda v: v.tensor_tensor(
                row(2), row(0), row(1), ALU.mult),
                reads=["ROW0", "ROW1"], writes=["ROW2"], tag=tag + "_Q0")
            tk.comp("vector", lambda v: v.tensor_tensor(
                row(3), row(5), row(1), ALU.mult),
                reads=["ROW5", "ROW1"], writes=["ROW3"], tag=tag + "_v1a")
            tk.comp("vector", lambda v: v.tensor_tensor(
                row(3), row(3), row(1), ALU.mult),
                reads=["ROW3", "ROW1"], writes=["ROW3"], tag=tag + "_v1b")
            rstd_from(3, 3, tag + "_r1")
            tk.comp("vector", lambda v: v.tensor_tensor(
                row(3), row(3), row(1), ALU.mult),
                reads=["ROW3", "ROW1"], writes=["ROW3"], tag=tag + "_P01")
            tk.comp("vector", lambda v: v.tensor_tensor(
                row(4), row(0), row(3), ALU.mult),
                reads=["ROW0", "ROW3"], writes=["ROW4"], tag=tag + "_Q01")

        def tap_featmajor(name, src_tensor, t0, reads, f32_chunks=NCH):
            if name not in taps:
                return
            tk.dma("sync", lambda s: s.dma_start(
                out=taps[name].rearrange("(c p) t -> p c t", p=128)[:, :, t0:t0 + TT],
                in_=src_tensor[:, 0:f32_chunks * TT].rearrange("p (c t) -> p c t", t=TT)),
                reads=list(reads), tag="tap_" + name)

        p0_scope = ExitStack()
        XBD = p0_scope.enter_context(nc.sbuf_tensor("XBD", [128, 2 * NCH * TT], F32))
        XN32 = p0_scope.enter_context(nc.sbuf_tensor("XN32", [128, NCH * TT], F32))
        H1P = p0_scope.enter_context(nc.sbuf_tensor("H1P", [128, NCH * TT], BF16))
        XSQ = p0_scope.enter_context(nc.sbuf_tensor("XSQ", [128, NCH * TT], BF16))
        KE = p0_scope.enter_context(nc.sbuf_tensor("KE", [128, 1024], BF16))
        VA = p0_scope.enter_context(nc.sbuf_tensor("VA", [128, 4 * 257], BF16))
        CTX32 = p0_scope.enter_context(nc.sbuf_tensor("CTX32", [128, 8 * 257], F32))
        RTB = p0_scope.enter_context(nc.sbuf_tensor("RTB", [128, NCH * E], F32))
        BCA = p0_scope.enter_context(nc.sbuf_tensor("BCA", [128, TT], F32))
        BCB = p0_scope.enter_context(nc.sbuf_tensor("BCB", [128, TT], F32))
        XNT = p0_scope.enter_context(nc.sbuf_tensor("XNT", [128, NCH * TT], BF16))
        XROW = p0_scope.enter_context(nc.sbuf_tensor("XROW", [128, 2 * D], BF16))
        LSTR = p0_scope.enter_context(nc.sbuf_tensor("LSTR", [128, 128], F32))
        LIN8 = p0_scope.enter_context(nc.sbuf_tensor("LIN8", [8, 8], F32))
        BCNTS = p0_scope.enter_context(nc.sbuf_tensor("BCNTS", [8, 16], F32))
        RNKSB = p0_scope.enter_context(nc.sbuf_tensor("RNKSB", [128, 16 * 8], F32))
        MSB = p0_scope.enter_context(nc.sbuf_tensor("MSB", [128, 16 * 8], F32))
        M1SB = p0_scope.enter_context(nc.sbuf_tensor("M1SB", [128, 16 * 8], F32))
        CMBSB = p0_scope.enter_context(nc.sbuf_tensor("CMBSB", [128, 16 * 8], F32))
        SRT = p0_scope.enter_context(nc.sbuf_tensor("SRT", [128, 16], F32))
        TOKR = p0_scope.enter_context(nc.sbuf_tensor("TOKR", [128, 8], F32))
        IDXI = p0_scope.enter_context(nc.sbuf_tensor("IDXI", [128, 1], mybir.dt.int32))
        ZZ = p0_scope.enter_context(nc.sbuf_tensor("ZZ", [128, 2048], F32))

        tk.comp("vector", lambda v: v.memset(VA[:, :], 1.0), writes=["VA"])
        tk.comp("vector", lambda v: v.memset(CTX32[:, :], 0.0), writes=["CTX32"])
        tk.comp("vector", lambda v: v.memset(ZZ[:, :], 0.0), writes=["ZZ"])
        tk.comp("vector", lambda v: v.memset(TOKR[:, :], 0.0), writes=["TOKR"])
        tk.dma("sync", lambda s: s.dma_start(out=LSTR[:, :], in_=lstr_in), writes=["LSTR"])
        tk.dma("sync", lambda s: s.dma_start(out=LIN8[:, :], in_=lincl_in), writes=["LIN8"])
        tk.dma("sync", lambda s: s.dma_start(
            out=tok_dram.ap().rearrange("(p r) c -> p (r c)", p=128), in_=ZZ[:, 0:512]),
            reads=["ZZ"], writes=["tok_dram"], tag="tokzero")
        for zz in range(8):
            tk.dma("sync", lambda s, zz=zz: s.dma_start(
                out=ys_dram.ap()[zz * 256:(zz + 1) * 256, :]
                .rearrange("(p r) c -> p (r c)", p=128), in_=ZZ[:, :]),
                reads=["ZZ"], writes=["ys_dram"], tag=f"yszero{zz}")

        wload("WA", wk.rearrange("(c p) n -> p c n", p=128), "wk")
        wload("WB", wv.rearrange("(c p) n -> p c n", p=128), "wv")
        tk.dma("sync", lambda s: s.dma_start(
            out=RTB[:, :].rearrange("p (c e) -> p c e", e=E),
            in_=rT.rearrange("(c p) e -> p c e", p=128)), writes=["RTB"], tag="rt")

        def combine_block(tau, b):
            def V(fn, reads, writes, tag):
                tk.comp("vector", fn, reads=reads, writes=writes, tag=f"{tag}{tau}_{b}")

            V(lambda v: v.tensor_reduce(CMB[:, 8:9], PS[4][:, 0:E], AX, ALU.max),
              ["PS4"], ["CMB"], "cm_max")
            V(lambda v: v.tensor_scalar(CMB[:, 0:E], PS[4][:, 0:E], CMB[:, 8:9],
                                        None, ALU.subtract),
              ["PS4", "CMB"], ["CMB"], "cm_sh")
            tk.comp("scalar", lambda s: s.activation(CMB[:, 0:E], CMB[:, 0:E], AF.Exp),
                    reads=["CMB"], writes=["CMB"], tag=f"cm_exp{tau}_{b}")
            V(lambda v: v.tensor_reduce(CMB[:, 9:10], CMB[:, 0:E], AX, ALU.add),
              ["CMB"], ["CMB"], "cm_sum")
            V(lambda v: v.reciprocal(CMB[:, 10:11], CMB[:, 9:10]),
              ["CMB"], ["CMB"], "cm_rcp")
            V(lambda v: v.tensor_scalar(CMB[:, 0:E], CMB[:, 0:E], CMB[:, 10:11],
                                        None, ALU.mult),
              ["CMB"], ["CMB"], "cm_p")
            V(lambda v: v.tensor_reduce(CMB[:, 11:12], CMB[:, 0:E], AX, ALU.max),
              ["CMB"], ["CMB"], "cm_m1")
            V(lambda v: v.tensor_scalar(CMB[:, 32:32 + E], CMB[:, 0:E], CMB[:, 11:12],
                                        None, ALU.is_ge),
              ["CMB"], ["CMB"], "cm_g1")
            V(lambda v: v.tensor_scalar(CMB[:, 40:40 + E], CMB[:, 32:32 + E], 2.0,
                                        None, ALU.mult),
              ["CMB"], ["CMB"], "cm_g2")
            V(lambda v: v.tensor_tensor(CMB[:, 24:24 + E], CMB[:, 0:E],
                                        CMB[:, 40:40 + E], ALU.subtract),
              ["CMB"], ["CMB"], "cm_q8")
            V(lambda v: v.tensor_reduce(CMB[:, 12:13], CMB[:, 24:24 + E], AX, ALU.max),
              ["CMB"], ["CMB"], "cm_m2")
            V(lambda v: v.tensor_scalar(CMB[:, 16:16 + E], CMB[:, 0:E], CMB[:, 12:13],
                                        None, ALU.is_ge),
              ["CMB"], ["CMB"], "cm_msk")
            tk.comp("scalar", lambda s: s.activation(CMB[:, 0:E], CMB[:, 0:E], AF.Exp),
                    reads=["CMB"], writes=["CMB"], tag=f"cm_ep{tau}_{b}")
            V(lambda v: v.tensor_tensor(CMB[:, 0:E], CMB[:, 0:E], CMB[:, 16:16 + E],
                                        ALU.mult),
              ["CMB"], ["CMB"], "cm_em")
            V(lambda v: v.tensor_reduce(CMB[:, 13:14], CMB[:, 0:E], AX, ALU.add),
              ["CMB"], ["CMB"], "cm_ds")
            V(lambda v: v.reciprocal(CMB[:, 14:15], CMB[:, 13:14]),
              ["CMB"], ["CMB"], "cm_dr")
            V(lambda v: v.tensor_scalar(CMB[:, 0:E], CMB[:, 0:E], CMB[:, 14:15],
                                        None, ALU.mult),
              ["CMB"], ["CMB"], "cm_c")
            tk.comp("tensor", lambda t: t.matmul(
                PS[5][0:E, 0:128], CMB[:, 0:E], IDN[:, :], start=True, stop=True),
                reads=["CMB", "IDN"], writes=["PS5"], tag=f"cm_tr{tau}_{b}")
            V(lambda v: v.tensor_copy(CTS[:, :], PS[5][0:E, 0:128]),
              ["PS5"], ["CTS"], "cm_ct")

        def p0_tile(tau):
            t0 = tau * TT
            par = tau % 2
            xbk = f"XB{par}"
            XBt = XBD[:, par * NCH * TT:(par + 1) * NCH * TT]

            def xb(c):
                return XBt[:, c * TT:(c + 1) * TT]

            def xn32(c):
                return XN32[:, c * TT:(c + 1) * TT]

            def h1c(c):
                return H1P[:, c * TT:(c + 1) * TT]

            def xsq(c):
                return XSQ[:, c * TT:(c + 1) * TT]

            tk.dma("sync", lambda s: s.dma_start(
                out=XBt.rearrange("p (c t) -> p c t", t=TT),
                in_=xT.rearrange("(c p) t -> p c t", p=128)[:, :, t0:t0 + TT]),
                writes=[xbk], tag=f"xload{tau}")

            stats8(xb, lambda c: xbk, xsq, lambda c: f"XSQ_{c}", f"ln0_{tau}")
            ln01_rows(f"ln01_{tau}")
            tk.dma("sync", lambda s: s.dma_start(
                out=rows_dram.ap()[:, t0:t0 + TT], in_=ROWS[0:1, TT:5 * TT]),
                reads=["ROW1", "ROW2", "ROW3", "ROW4"], writes=["rows_dram"],
                tag=f"rsave{tau}")
            bcast(2, row(1), ["ROW1"], f"b_r0_{tau}")
            bcast(3, row(2), ["ROW2"], f"b_q0_{tau}")
            bcast(4, row(3), ["ROW3"], f"b_p01_{tau}")
            bcast(5, row(4), ["ROW4"], f"b_q01_{tau}")
            tk.comp("scalar", lambda s: s.activation(
                BCA[:, :], PS[2][:, :], AF.Identity),
                reads=["PS2"], writes=["BCA"], tag=f"stg_r0_{tau}")
            tk.comp("scalar", lambda s: s.activation(
                BCB[:, :], PS[3][:, :], AF.Identity),
                reads=["PS3"], writes=["BCB"], tag=f"stg_q0_{tau}")
            for c in range(NCH):
                tk.comp("gpsimd", lambda g, c=c: g.tensor_tensor(
                    xn32(c), xb(c), BCA[:, :], ALU.mult),
                    reads=[xbk, "BCA"], writes=[f"XN32_{c}"], tag=f"xnm{tau}_{c}")
                tk.comp("gpsimd", lambda g, c=c: g.tensor_tensor(
                    xn32(c), xn32(c), BCB[:, :], ALU.subtract),
                    reads=[f"XN32_{c}", "BCB"], writes=[f"XN32_{c}"], tag=f"xns{tau}_{c}")
                tk.comp("gpsimd", lambda g, c=c: g.tensor_copy(
                    XNT[:, c * TT:(c + 1) * TT], xn32(c)),
                    reads=[f"XN32_{c}"], writes=[f"XNT_{c}"], tag=f"xnc{tau}_{c}")
                tk.comp("vector", lambda v, c=c: v.tensor_tensor(
                    TMPC[:, :], xb(c), PS[4][:, :], ALU.mult),
                    reads=[xbk, "PS4"], writes=["TMPC"], tag=f"h1m{tau}_{c}")
                tk.comp("vector", lambda v, c=c: v.tensor_tensor(
                    TMPC[:, :], TMPC[:, :], PS[5][:, :], ALU.subtract),
                    reads=["TMPC", "PS5"], writes=["TMPC"], tag=f"h1s{tau}_{c}")
                tk.comp("vector", lambda v, c=c: v.tensor_scalar(
                    h1c(c), TMPC[:, :], G1[:, c:c + 1], B1[:, c:c + 1],
                    ALU.mult, ALU.add),
                    reads=["TMPC", "VEC"], writes=[f"H1_{c}"], tag=f"h1f{tau}_{c}")
            tap_featmajor("xn32", XN32, t0, [f"XN32_{c}" for c in range(NCH)])
            if "h1" in taps:
                for c in range(NCH):
                    tk.comp("vector", lambda v, c=c: v.tensor_copy(TMPC2[:, :], h1c(c)),
                            reads=[f"H1_{c}"], writes=["TMPC2"], tag="taph1c")
                    tk.dma("sync", lambda s, c=c: s.dma_start(
                        out=taps["h1"].rearrange("(c p) t -> p c t", p=128)[:, c, t0:t0 + TT],
                        in_=TMPC2[:, :]), reads=["TMPC2"], tag="taph1")

            def p0_block(b):
                tb = b * 128
                terms = [(lambda c=c: xn32(c)[:, tb:tb + 128],
                          lambda c=c: RTB[:, c * E:(c + 1) * E], [f"XN32_{c}"])
                         for c in range(NCH)]
                mm_group(PS[4][:, 0:E], terms, "PS4", ["RTB"], f"rt{tau}_{b}")

                for proj, slot, pk in (("k", "WA", 5), ("v", "WB", 6)):
                    for hh in range(2):
                        terms = [(lambda c=c: h1c(c)[:, tb:tb + 128],
                                  lambda c=c, hh=hh, slot=slot: w_chunk(slot, c)[:, hh * 512:(hh + 1) * 512],
                                  [f"H1_{c}"])
                                 for c in range(NCH)]
                        mm_group(PS[pk][:, :], terms, f"PS{pk}", [slot],
                                 f"{proj}{tau}_{b}_{hh}")
                        if proj == "k":
                            tk.comp("scalar", lambda s, hh=hh: s.activation(
                                KE[:, hh * 512:(hh + 1) * 512], PS[5][:, :],
                                AF.Exp, scale=SCALE),
                                reads=["PS5"], writes=["KE"], tag=f"ke{tau}_{b}_{hh}")
                        else:
                            for hq in range(2):
                                h = hh * 2 + hq
                                tk.comp("vector", lambda v, hq=hq, h=h: v.tensor_copy(
                                    VA[:, h * 257:h * 257 + 256],
                                    PS[6][:, hq * 256:(hq + 1) * 256]),
                                    reads=["PS6"], writes=["VA"], tag=f"va{tau}_{b}_{h}")
                for j in range(8):
                    h, db = j // 2, j % 2
                    pb = 7 if j % 2 == 0 else 6
                    tk.comp("tensor", lambda t, h=h, db=db, pb=pb: t.matmul(
                        PS[pb][:, 0:257],
                        KE[:, h * 256 + db * 128:h * 256 + db * 128 + 128],
                        VA[:, h * 257:(h + 1) * 257],
                        start=True, stop=True),
                        reads=["KE", "VA"], writes=[f"PS{pb}"], tag=f"ctx{tau}_{b}_{j}")
                    tk.comp("vector", lambda v, j=j, pb=pb: v.tensor_tensor(
                        CTX32[:, j * 257:(j + 1) * 257],
                        CTX32[:, j * 257:(j + 1) * 257],
                        PS[pb][:, 0:257], ALU.add),
                        reads=["CTX32", f"PS{pb}"], writes=["CTX32"],
                        tag=f"ctxa{tau}_{b}_{j}")

                combine_block(tau, b)
                if "cmb" in taps:
                    tk.dma("sync", lambda s: s.dma_start(
                        out=ct_dram.ap()[:, t0 + tb:t0 + tb + 128], in_=CTS[:, :]),
                        reads=["CTS"], writes=["ct_dram"], tag=f"ctd{tau}_{b}")
                gb = 4 * tau + b
                V8 = lambda a: a
                tk.comp("vector", lambda v, gb=gb: v.tensor_copy(
                    MSB[:, gb * 8:(gb + 1) * 8], CMB[:, 16:24]),
                    reads=["CMB"], writes=["MSB"], tag=f"svm{gb}")
                tk.comp("vector", lambda v, gb=gb: v.tensor_copy(
                    M1SB[:, gb * 8:(gb + 1) * 8], CMB[:, 32:40]),
                    reads=["CMB"], writes=["M1SB"], tag=f"svm1{gb}")
                tk.comp("vector", lambda v, gb=gb: v.tensor_copy(
                    CMBSB[:, gb * 8:(gb + 1) * 8], CMB[:, 0:8]),
                    reads=["CMB"], writes=["CMBSB"], tag=f"svc{gb}")
                tk.comp("tensor", lambda t, gb=gb: t.matmul(
                    PS[7][:, 0:8], LSTR[:, :], MSB[:, gb * 8:(gb + 1) * 8],
                    start=True, stop=True),
                    reads=["LSTR", "MSB"], writes=["PS7"], tag=f"rnk{gb}")
                tk.comp("vector", lambda v, gb=gb: v.tensor_copy(
                    RNKSB[:, gb * 8:(gb + 1) * 8], PS[7][:, 0:8]),
                    reads=["PS7"], writes=["RNKSB"], tag=f"rnke{gb}")
                tk.comp("tensor", lambda t, gb=gb: t.matmul(
                    PS[7][0:8, 8:9], MSB[:, gb * 8:(gb + 1) * 8], ONESC[:, :],
                    start=True, stop=True),
                    reads=["MSB", "ONESC"], writes=["PS7"], tag=f"cnt{gb}")
                tk.comp("vector", lambda v, gb=gb: v.tensor_copy(
                    BCNTS[:, gb:gb + 1], PS[7][0:8, 8:9]),
                    reads=["PS7"], writes=["BCNTS"], tag=f"cnte{gb}")

            for b in range(NB):
                p0_block(b)

            for b in range(NB):
                rpar = b % 2
                for c in range(NCH):
                    pb = 2 + c % 2
                    tk.comp("tensor", lambda t, b=b, c=c, pb=pb: t.matmul(
                        PS[pb][:, 0:128], XNT[:, c * TT + b * 128:c * TT + b * 128 + 128],
                        IDNB[:, :], start=True, stop=True),
                        reads=[f"XNT_{c}", "IDNB"], writes=[f"PS{pb}"],
                        tag=f"xtr{tau}_{b}_{c}")
                    tk.comp("vector", lambda v, b=b, c=c, pb=pb, rpar=rpar: v.tensor_copy(
                        XROW[:, rpar * D + c * 128:rpar * D + c * 128 + 128],
                        PS[pb][:, 0:128]),
                        reads=[f"PS{pb}"], writes=[f"XROW{rpar}"],
                        tag=f"xtre{tau}_{b}_{c}")
                tk.dma("sync", lambda s, b=b, rpar=rpar: s.dma_start(
                    out=xnb_dram.ap()[t0 + b * 128:t0 + b * 128 + 128, :],
                    in_=XROW[:, rpar * D:(rpar + 1) * D]),
                    reads=[f"XROW{rpar}"], writes=["xnb_dram"], tag=f"xrow{tau}_{b}")

        for tau in range(NTILES):
            p0_tile(tau)

        def sort_post():
            tk.comp("vector", lambda v: v.tensor_reduce(
                SRT[0:8, 0:1], BCNTS[:, :], AX, ALU.add),
                reads=["BCNTS"], writes=["SRT"], tag="sp_N")
            for j in range(4):
                tk.comp("vector", lambda v, j=j: v.tensor_scalar(
                    SRT[0:8, 1 + j:2 + j], SRT[0:8, 0:1], 512.0 * j, None, ALU.is_gt),
                    reads=["SRT"], writes=["SRT"], tag=f"sp_nt{j}")
            tk.comp("vector", lambda v: v.tensor_tensor(
                SRT[0:8, 1:2], SRT[0:8, 1:2], SRT[0:8, 2:3], ALU.add),
                reads=["SRT"], writes=["SRT"], tag="sp_nta")
            tk.comp("vector", lambda v: v.tensor_tensor(
                SRT[0:8, 3:4], SRT[0:8, 3:4], SRT[0:8, 4:5], ALU.add),
                reads=["SRT"], writes=["SRT"], tag="sp_ntb")
            tk.comp("vector", lambda v: v.tensor_tensor(
                SRT[0:8, 1:2], SRT[0:8, 1:2], SRT[0:8, 3:4], ALU.add),
                reads=["SRT"], writes=["SRT"], tag="sp_ntc")
            tk.comp("tensor", lambda t: t.matmul(
                PS[7][0:8, 0:1], LIN8[:, :], SRT[0:8, 1:2], start=True, stop=True),
                reads=["LIN8", "SRT"], writes=["PS7"], tag="sp_cum")
            tk.comp("vector", lambda v: v.tensor_copy(SRT[0:8, 5:6], PS[7][0:8, 0:1]),
                    reads=["PS7"], writes=["SRT"], tag="sp_cume")
            tk.comp("vector", lambda v: v.tensor_tensor(
                SRT[0:8, 6:7], SRT[0:8, 5:6], SRT[0:8, 1:2], ALU.subtract),
                reads=["SRT"], writes=["SRT"], tag="sp_sb0")
            tk.comp("vector", lambda v: v.tensor_scalar(
                SRT[0:8, 6:7], SRT[0:8, 6:7], 512.0, None, ALU.mult),
                reads=["SRT"], writes=["SRT"], tag="sp_sb1")
            tk.comp("vector", lambda v: v.tensor_scalar(
                BCNTS[:, 0:1], BCNTS[:, 0:1], 0.0, None, ALU.mult),
                reads=["BCNTS"], writes=["BCNTS"], tag="sp_b0") if False else None
            for gb in range(15, 0, -1):
                pass
            tk.comp("vector", lambda v: v.tensor_scalar(
                SRT[0:8, 7:8], SRT[0:8, 0:1], 0.0, None, ALU.mult),
                reads=["SRT"], writes=["SRT"], tag="sp_acc0")
            for gb in range(16):
                tk.comp("vector", lambda v, gb=gb: v.tensor_copy(
                    SRT[0:8, 8:9], BCNTS[:, gb:gb + 1]),
                    reads=["BCNTS"], writes=["SRT"], tag=f"sp_sv{gb}")
                tk.comp("vector", lambda v, gb=gb: v.tensor_tensor(
                    BCNTS[:, gb:gb + 1], SRT[0:8, 7:8], SRT[0:8, 6:7], ALU.add),
                    reads=["SRT"], writes=["BCNTS"], tag=f"sp_off{gb}")
                tk.comp("vector", lambda v, gb=gb: v.tensor_tensor(
                    SRT[0:8, 7:8], SRT[0:8, 7:8], SRT[0:8, 8:9], ALU.add),
                    reads=["SRT"], writes=["SRT"], tag=f"sp_run{gb}")
            tk.dma("sync", lambda s: s.dma_start(out=off_dram.ap(), in_=BCNTS[:, :]),
                   reads=["BCNTS"], writes=["off_dram"], tag="sp_offd")
            tk.dma("sync", lambda s: s.dma_start(out=cum_dram.ap(), in_=SRT[0:8, 5:6]),
                   reads=["SRT"], writes=["cum_dram"], tag="sp_cumd")
            tk.dma("sync", lambda s: s.dma_start(
                out=SRT[0:1, 0:8], in_=cum_dram.ap().rearrange("e one -> one e")),
                reads=["cum_dram"], writes=["SRT"], tag="sp_cumr")
            for k in range(16):
                tk.comp("vector", lambda v, k=k: v.tensor_scalar(
                    SRT[0:1, 8:16], SRT[0:1, 0:8], float(k) + 0.5, None, ALU.is_lt),
                    reads=["SRT"], writes=["SRT"], tag=f"sp_ek{k}")
                tk.comp("vector", lambda v, k=k: v.tensor_reduce(
                    EXPB[0:1, k:k + 1], SRT[0:1, 8:16], AX, ALU.add),
                    reads=["SRT"], writes=["EXPB"], tag=f"sp_ekr{k}")
            tk.comp("vector", lambda v: v.tensor_scalar(
                EXPB[0:1, 0:16], EXPB[0:1, 0:16], 7.0, None, ALU.min),
                reads=["EXPB"], writes=["EXPB"], tag="sp_clamp")
            tk.comp("tensor", lambda t: t.matmul(
                PS[7][:, 0:16], ONESR[:, :], EXPB[0:1, 0:16], start=True, stop=True),
                reads=["EXPB", "ONESR"], writes=["PS7"], tag="sp_eb")
            tk.comp("vector", lambda v: v.tensor_copy(EXPB[:, :], PS[7][:, 0:16]),
                    reads=["PS7"], writes=["EXPB"], tag="sp_ebc")
            for gb in range(16):
                tk.dma("sync", lambda s, gb=gb: s.dma_start(
                    out=SRT[0:1, 0:8], in_=off_dram.ap()[:, gb:gb + 1]
                    .rearrange("e one -> one e"),
                    allow_slow_non_contiguous=True),
                    reads=["off_dram"], writes=["SRT"], tag=f"sp_ofl{gb}")
                tk.comp("tensor", lambda t, gb=gb: t.matmul(
                    PS[7][:, 0:8], ONESR[:, :], SRT[0:1, 0:8], start=True, stop=True),
                    reads=["SRT", "ONESR"], writes=["PS7"], tag=f"sp_ofb{gb}")
                tk.comp("vector", lambda v, gb=gb: v.tensor_tensor(
                    SRT[:, 0:8], PS[7][:, 0:8], RNKSB[:, gb * 8:(gb + 1) * 8], ALU.add),
                    reads=["PS7", "RNKSB"], writes=["SRT"], tag=f"sp_sl{gb}")
                for j in range(2):
                    mk = "M1SB" if j == 0 else "MSB"
                    if j == 0:
                        tk.comp("vector", lambda v, gb=gb: v.tensor_copy(
                            SRT[:, 8:16], M1SB[:, gb * 8:(gb + 1) * 8]),
                            reads=["M1SB"], writes=["SRT"], tag=f"sp_mj{gb}_{j}")
                    else:
                        tk.comp("vector", lambda v, gb=gb: v.tensor_tensor(
                            SRT[:, 8:16], MSB[:, gb * 8:(gb + 1) * 8],
                            M1SB[:, gb * 8:(gb + 1) * 8], ALU.subtract),
                            reads=["MSB", "M1SB"], writes=["SRT"], tag=f"sp_mj{gb}_{j}")
                    tk.comp("vector", lambda v: v.tensor_tensor(
                        TMPC[:, 0:8], SRT[:, 8:16], SRT[:, 0:8], ALU.mult),
                        reads=["SRT"], writes=["TMPC"], tag=f"sp_ms{gb}_{j}")
                    tk.comp("vector", lambda v: v.tensor_reduce(
                        TOKR[:, 2:3], TMPC[:, 0:8], AX, ALU.add),
                        reads=["TMPC"], writes=["TOKR"], tag=f"sp_ix{gb}_{j}")
                    tk.comp("vector", lambda v, gb=gb: v.tensor_tensor(
                        TMPC[:, 0:8], CMBSB[:, gb * 8:(gb + 1) * 8], ESCB[:, :],
                        ALU.mult),
                        reads=["CMBSB", "ESCB"], writes=["TMPC"], tag=f"sp_ce{gb}_{j}")
                    tk.comp("vector", lambda v: v.tensor_tensor(
                        TMPC[:, 0:8], TMPC[:, 0:8], SRT[:, 8:16], ALU.mult),
                        reads=["TMPC", "SRT"], writes=["TMPC"], tag=f"sp_cm{gb}_{j}")
                    tk.comp("vector", lambda v: v.tensor_reduce(
                        TOKR[:, 1:2], TMPC[:, 0:8], AX, ALU.add),
                        reads=["TMPC"], writes=["TOKR"], tag=f"sp_cer{gb}_{j}")
                    tk.comp("vector", lambda v, gb=gb: v.tensor_scalar(
                        TOKR[:, 0:1], IOTAS[:, 0:1], 1.0, 128.0 * gb, ALU.mult, ALU.add),
                        reads=["IOTAS"], writes=["TOKR"], tag=f"sp_tk{gb}_{j}")
                    tk.comp("vector", lambda v: v.tensor_copy(IDXI[:, :], TOKR[:, 2:3]),
                            reads=["TOKR"], writes=["IDXI"], tag=f"sp_ii{gb}_{j}")
                    tk.dma("gpsimd", lambda g: g.indirect_dma_start(
                        tok_dram.ap(), bass_rust.IndirectOffsetOnAxis(ap=IDXI[:, :], axis=0),
                        TOKR[:, :], None),
                        reads=["TOKR", "IDXI"], writes=["tok_dram"],
                        tag=f"sp_sc{gb}_{j}")

        sort_post()
        if "srt" in taps:
            tk.dma("sync", lambda s: s.dma_start(out=taps["srt"], in_=tok_dram.ap()),
                   reads=["tok_dram"], tag="tapsrt")

        if "cmdbg" in taps:
            tk.dma("sync", lambda s: s.dma_start(out=taps["cmdbg"][:, 0:64],
                                                 in_=CMB[:, 0:64]),
                   reads=["CMB"], tag="tapcmdbg")
        if "kv" in taps:
            tk.dma("gpsimd", lambda g: g.dma_start(out=taps["kv"][:, 0:1024],
                                                   in_=KE[:, :]),
                   reads=["KE"], tag="tapkv1")
            tk.dma("gpsimd", lambda g: g.dma_start(out=taps["kv"][:, 1024:2052],
                                                   in_=VA[:, :]),
                   reads=["VA"], tag="tapkv2")
        tk.dma("sync", lambda s: s.dma_start(out=cc_in.ap(), in_=CTX32[:, :]),
               reads=["CTX32"], writes=["cc_in"], tag="ccout")
        if "ctx" in taps:
            tk.dma("sync", lambda s: s.dma_start(out=taps["ctx"], in_=CTX32[:, :]),
                   reads=["CTX32"], tag="tapctx")
        if "cmb" in taps:
            tk.dma("sync", lambda s: s.dma_start(out=taps["cmb"], in_=ct_dram.ap()),
                   reads=["ct_dram"], tag="tapcmb")

        p0_scope.close()
        tk.barrier("p0_to_p3")
        tk.cc(lambda g: g.collective_compute(
            "AllReduce", ALU.add, replica_groups=[list(range(NCORES))],
            ins=[cc_in.ap().opt()], outs=[cc_out.ap().opt()]),
            reads=["cc_in"], writes=["cc_out"], tag="allreduce")
        p3_scope = ExitStack()
        HFULLD = p3_scope.enter_context(nc.sbuf_tensor("HFULLD", [128, 2 * NFC * TT], BF16))
        XNSD = p3_scope.enter_context(nc.sbuf_tensor("XNSD", [128, 2 * NCH * TT], BF16))
        XG = p3_scope.enter_context(nc.sbuf_tensor("XG", [128, 2 * D], BF16))
        YTOKD = p3_scope.enter_context(nc.sbuf_tensor("YTOKD", [128, 2 * 4 * D], F32))
        TOKT = p3_scope.enter_context(nc.sbuf_tensor("TOKT", [128, 2 * 4 * 8], F32))
        TKI = p3_scope.enter_context(nc.sbuf_tensor("TKI", [128, 2 * 4], mybir.dt.int32))
        WIX = p3_scope.enter_context(nc.sbuf_tensor("WIX", [128, 16], F32))
        WIXI = p3_scope.enter_context(nc.sbuf_tensor("WIXI", [128, 16], mybir.dt.int32))
        p3b_scope = ExitStack()
        CCB = p3b_scope.enter_context(nc.sbuf_tensor("CCB", [128, 8 * 257], BF16))
        RECJ = p3b_scope.enter_context(nc.sbuf_tensor("RECJ", [128, 8], F32))

        def ctx_normalize():
            tk.dma("gpsimd", lambda g: g.dma_start(out=CCB[:, :], in_=cc_out.ap()),
                   reads=["cc_out"], writes=["CCB"], tag="ccload")
            for j in range(8):
                tk.comp("vector", lambda v, j=j: v.reciprocal(
                    RECJ[:, j:j + 1], CCB[:, j * 257 + 256:j * 257 + 257]),
                    reads=["CCB"], writes=["RECJ"], tag=f"nrec{j}")
                tk.comp("vector", lambda v, j=j: v.tensor_scalar(
                    CTX_BF[:, j * 256:(j + 1) * 256], CCB[:, j * 257:j * 257 + 256],
                    RECJ[:, j:j + 1], None, ALU.mult),
                    reads=["CCB", "RECJ"], writes=["CTX_BF"], tag=f"nmul{j}")
            if "ctxn" in taps:
                tk.dma("gpsimd", lambda g: g.dma_start(
                    out=taps["ctxn"], in_=CTX_BF[:, :]),
                    reads=["CTX_BF"], tag="tapctxn")

        def sp_tile(k):
            kp = 0 if DEBUG.get("kp0") else k % 2
            HFULL = HFULLD[:, kp * NFC * TT:(kp + 1) * NFC * TT]
            XNS = XNSD[:, kp * NCH * TT:(kp + 1) * NCH * TT]
            YTOK = YTOKD[:, kp * 4 * D:(kp + 1) * 4 * D]
            hk = lambda fc: f"HF{kp}_{fc}"
            xk = lambda c: f"XNS{kp}_{c}"
            yk = lambda sb: f"YT{kp}_{sb}"
            tkk = lambda: f"TOK{kp}"

            def widx(col, g, tag):
                col = col + kp * 8
                tk.comp("vector", lambda v, col=col, g=g: v.scalar_tensor_tensor(
                    WIX[:, col:col + 1], EXPB[:, k:k + 1], 512.0, IOTAS[:, g:g + 1],
                    ALU.mult, ALU.add),
                    reads=["EXPB", "IOTAS"], writes=[f"WIX{col}"], tag=tag + "a")
                tk.comp("vector", lambda v, col=col: v.tensor_copy(
                    WIXI[:, col:col + 1], WIX[:, col:col + 1]),
                    reads=[f"WIX{col}"], writes=[f"WIXI{col}"], tag=tag + "b")

            for b in range(4):
                bq = kp * 4 + b
                tk.dma("sync", lambda s, bq=bq, b=b: s.dma_start(
                    out=TOKT[:, bq * 8:(bq + 1) * 8],
                    in_=tok_dram.ap()[k * TT + b * 128:k * TT + b * 128 + 128, :]),
                    reads=["tok_dram"], writes=[tkk()], tag=f"tok{k}_{b}")
                tk.comp("vector", lambda v, bq=bq: v.tensor_copy(
                    TKI[:, bq:bq + 1], TOKT[:, bq * 8:bq * 8 + 1]),
                    reads=[tkk()], writes=[f"TKI{kp}"], tag=f"tki{k}_{b}")
                tk.dma("gpsimd", lambda g, b=b, bq=bq: g.indirect_dma_start(
                    XG[:, (b % 2) * D:(b % 2 + 1) * D], None, xnb_dram.ap(),
                    bass_rust.IndirectOffsetOnAxis(ap=TKI[:, bq:bq + 1], axis=0)),
                    reads=["xnb_dram", f"TKI{kp}"], writes=[f"XG{b % 2}"], tag=f"xg{k}_{b}")
                for c in range(NCH):
                    pb = 2 + c % 2
                    tk.comp("tensor", lambda t, b=b, c=c, pb=pb: t.matmul(
                        PS[pb][:, 0:128],
                        XG[:, (b % 2) * D + c * 128:(b % 2) * D + c * 128 + 128],
                        IDNB[:, :], start=True, stop=True),
                        reads=[f"XG{b % 2}", "IDNB"], writes=[f"PS{pb}"],
                        tag=f"sptr{k}_{b}_{c}")
                    tk.comp("vector", lambda v, b=b, c=c, pb=pb: v.tensor_copy(
                        XNS[:, c * TT + b * 128:c * TT + b * 128 + 128],
                        PS[pb][:, 0:128]),
                        reads=[f"PS{pb}"], writes=[xk(c)],
                        tag=f"sptre{k}_{b}_{c}")

            for s4 in range(4):
                w1slot = WROT[(k * 8 + s4) % 3]
                widx(s4, s4, f"wi1_{k}_{s4}")
                tk.dma("gpsimd", lambda g, s4=s4, w1slot=w1slot, kp=kp: g.indirect_dma_start(
                    WSL[w1slot][:, :], None, w1g,
                    bass_rust.IndirectOffsetOnAxis(ap=WIXI[:, kp * 8 + s4:kp * 8 + s4 + 1], axis=0),
                    oob_is_err=False),
                    reads=[f"WIXI{kp * 8 + s4}"], writes=[w1slot], tag=f"w1g{k}_{s4}")
                for fb in range(8):
                    fbg = s4 * 8 + fb
                    pb = 2 + fbg % 2
                    terms = [(lambda c=c, fb=fb, w1slot=w1slot: w_chunk(w1slot, c)[:, fb * 128:(fb + 1) * 128],
                              lambda c=c, XNS=XNS: XNS[:, c * TT:(c + 1) * TT], [xk(c)])
                             for c in range(NCH)]
                    mm_group(PS[pb][:, :], terms, f"PS{pb}", [w1slot],
                             f"sph{k}_{fbg}")
                    tk.comp("scalar", lambda sc, fbg=fbg, pb=pb, HFULL=HFULL: sc.activation(
                        HFULL[:, fbg * TT:(fbg + 1) * TT], PS[pb][:, :], AF.Silu),
                        reads=[f"PS{pb}"], writes=[hk(fbg)], tag=f"sphs{k}_{fbg}")

            for dq in range(4):
                w2slot = WROT[(k * 8 + 4 + dq) % 3]
                widx(4 + dq, dq, f"wi2_{k}_{dq}")
                tk.dma("gpsimd", lambda g, dq=dq, w2slot=w2slot, kp=kp: g.indirect_dma_start(
                    WSL[w2slot][:, :], None, w2g,
                    bass_rust.IndirectOffsetOnAxis(ap=WIXI[:, kp * 8 + 4 + dq:kp * 8 + 5 + dq], axis=0),
                    oob_is_err=False),
                    reads=[f"WIXI{kp * 8 + 4 + dq}"], writes=[w2slot], tag=f"w2g{k}_{dq}")
                for sb in range(4):
                    pb = 4 + sb % 2
                    terms = [(lambda fc=fc, sb=sb, HFULL=HFULL: HFULL[:, fc * TT + sb * 128:fc * TT + sb * 128 + 128],
                              lambda fc=fc, w2slot=w2slot: WSL[w2slot][:, fc * 256:(fc + 1) * 256],
                              [hk(fc)]) for fc in range(NFC)]
                    mm_group(PS[pb][:, 0:256], terms, f"PS{pb}", [w2slot],
                             f"spy{k}_{dq}_{sb}")
                    tk.comp("vector", lambda v, dq=dq, sb=sb, pb=pb, kp=kp, YTOK=YTOK: v.tensor_scalar(
                        YTOK[:, sb * D + dq * 256:sb * D + (dq + 1) * 256],
                        PS[pb][:, 0:256], TOKT[:, (kp * 4 + sb) * 8 + 1:(kp * 4 + sb) * 8 + 2], None,
                        ALU.mult),
                        reads=[f"PS{pb}", tkk()], writes=[yk(sb)],
                        tag=f"spye{k}_{dq}_{sb}")
            for sb in range(4):
                tk.dma("gpsimd", lambda g, sb=sb, kp=kp, YTOK=YTOK: g.indirect_dma_start(
                    ys_dram.ap(),
                    bass_rust.IndirectOffsetOnAxis(ap=TKI[:, kp * 4 + sb:kp * 4 + sb + 1], axis=0),
                    YTOK[:, sb * D:(sb + 1) * D], None, compute_op=ALU.add),
                    reads=[yk(sb), f"TKI{kp}", "ys_dram"], writes=["ys_dram"],
                    tag=f"spsc{k}_{sb}")

        for k in range(16):
            sp_tile(k)
            if DEBUG.get("spbar"):
                tk.barrier(f"spbar{k}")
            if k == 3:
                ctx_normalize()

        if "ys" in taps:
            for zz in range(8):
                tk.dma("sync", lambda s, zz=zz: s.dma_start(
                    out=taps["ys"][zz * 256:(zz + 1) * 256, :]
                    .rearrange("(p r) c -> p (r c)", p=128),
                    in_=ZZ[:, :]), reads=["ys_dram"], tag="tapysX") if False else None
            tk.dma("sync", lambda s: s.dma_start(
                out=taps["ys"].rearrange("(a p) d -> p (a d)", p=128),
                in_=ys_dram.ap().rearrange("(a p) d -> p (a d)", p=128)),
                reads=["ys_dram"], tag="tapys") if False else None
            for b16 in range(16):
                tk.dma("sync", lambda s, b16=b16: s.dma_start(
                    out=taps["ys"][b16 * 128:(b16 + 1) * 128, :],
                    in_=ys_dram.ap()[b16 * 128:(b16 + 1) * 128, :]),
                    reads=["ys_dram"], tag=f"tapys{b16}")

        p3b_scope.close()
        p3_scope.close()
        g_scope = ExitStack()
        GBF = g_scope.enter_context(nc.sbuf_tensor("GBF", [128, NCH * T], BF16,
                                                   side="right"))
        gl_scope = ExitStack()
        YSROW = gl_scope.enter_context(nc.sbuf_tensor("YSROW", [128, 2 * D], F32))
        tk.barrier("p3_to_g")

        def g_sl(db, t0):
            return GBF[:, db * T + t0:db * T + t0 + TT]

        for tb in range(16):
            rpar = tb % 2
            tk.dma("sync", lambda s, tb=tb, rpar=rpar: s.dma_start(
                out=YSROW[:, rpar * D:(rpar + 1) * D],
                in_=ys_dram.ap()[tb * 128:(tb + 1) * 128, :]),
                reads=["ys_dram"], writes=[f"YSR{rpar}"], tag=f"gld{tb}")
            for c in range(NCH):
                pb = 6 if c % 2 == 0 else 2
                tk.comp("tensor", lambda t, tb=tb, c=c, pb=pb, rpar=rpar: t.matmul(
                    PS[pb][:, 0:128],
                    YSROW[:, rpar * D + c * 128:rpar * D + c * 128 + 128],
                    IDN[:, :], start=True, stop=True),
                    reads=[f"YSR{rpar}", "IDN"], writes=[f"PS{pb}"],
                    tag=f"gtr{tb}_{c}")
                tk.comp("scalar", lambda sc, tb=tb, c=c, pb=pb: sc.activation(
                    GBF[:, c * T + tb * 128:c * T + tb * 128 + 128],
                    PS[pb][:, 0:128], AF.Silu),
                    reads=[f"PS{pb}"], writes=[f"G{c}"], tag=f"gsil{tb}_{c}")
        if "g" in taps:
            for db in range(NCH):
                for tau in range(NTILES):
                    t0g = tau * TT
                    tk.comp("vector", lambda v, db=db, t0g=t0g: v.tensor_copy(
                        TMPC[:, :], g_sl(db, t0g)),
                        reads=[f"G{db}"], writes=["TMPC"], tag="tapgc")
                    tk.dma("sync", lambda s, db=db, t0g=t0g: s.dma_start(
                        out=taps["g"].rearrange("(c p) t -> p c t", p=128)[:, db, t0g:t0g + TT],
                        in_=TMPC[:, :]), reads=["TMPC"], tag="tapg")
        gl_scope.close()

        p2s = ExitStack()
        X2D = p2s.enter_context(nc.sbuf_tensor("X2D", [128, 2 * NCH * TT], F32))
        RES = p2s.enter_context(nc.sbuf_tensor("RES", [128, NCH * TT], F32))
        H1B = p2s.enter_context(nc.sbuf_tensor("H1B", [128, NCH * TT], BF16))
        QS = p2s.enter_context(nc.sbuf_tensor("QS", [128, NCH * TT], BF16))
        UB = p2s.enter_context(nc.sbuf_tensor("UB", [128, NFC * TT], BF16))
        BCA2 = p2s.enter_context(nc.sbuf_tensor("BCA2", [128, TT], F32))
        BCB2 = p2s.enter_context(nc.sbuf_tensor("BCB2", [128, TT], F32))
        tk.barrier("p3_to_p2")

        def p2_tile(tau):
            t0 = tau * TT
            par = tau % 2
            X2 = X2D[:, par * NCH * TT:(par + 1) * NCH * TT]
            xk = lambda c: f"X2_{par}_{c}"
            rk = lambda c: f"RES_{c}"

            def x2c(c):
                return X2[:, c * TT:(c + 1) * TT]

            def resc(c):
                return RES[:, c * TT:(c + 1) * TT]

            def h1b(c):
                return H1B[:, c * TT:(c + 1) * TT]

            def qsc(c):
                return QS[:, c * TT:(c + 1) * TT]

            def ub(fb):
                return UB[:, fb * TT:(fb + 1) * TT]

            tk.dma("sync", lambda s: s.dma_start(
                out=X2[:, :].rearrange("p (c t) -> p c t", t=TT),
                in_=xT.rearrange("(c p) t -> p c t", p=128)[:, :, t0:t0 + TT]),
                writes=[xk(c) for c in range(NCH)], tag=f"x2load{tau}")
            tk.dma("sync", lambda s: s.dma_start(
                out=ROWS[0:1, TT:5 * TT], in_=rows_dram.ap()[:, t0:t0 + TT]),
                reads=["rows_dram"], writes=["ROW1", "ROW2", "ROW3", "ROW4"],
                tag=f"rload{tau}")
            bcast(6, row(1), ["ROW1"], f"p2r0_{tau}")
            bcast(7, row(2), ["ROW2"], f"p2q0_{tau}")
            bcast(4, row(3), ["ROW3"], f"p2p01_{tau}")
            bcast(5, row(4), ["ROW4"], f"p2q01_{tau}")
            tk.comp("scalar", lambda s: s.activation(
                BCA2[:, :], PS[6][:, :], AF.Identity),
                reads=["PS6"], writes=["BCA2"], tag=f"stg_r0_{tau}")
            tk.comp("scalar", lambda s: s.activation(
                BCB2[:, :], PS[7][:, :], AF.Identity),
                reads=["PS7"], writes=["BCB2"], tag=f"stg_q0_{tau}")
            for c in range(NCH):
                tk.comp("gpsimd", lambda g, c=c: g.tensor_tensor(
                    resc(c), x2c(c), BCA2[:, :], ALU.mult),
                    reads=[xk(c), "BCA2"], writes=[rk(c)], tag=f"q0m{tau}_{c}")
                tk.comp("gpsimd", lambda g, c=c: g.tensor_tensor(
                    resc(c), resc(c), BCB2[:, :], ALU.subtract),
                    reads=[rk(c), "BCB2"], writes=[rk(c)], tag=f"q0s{tau}_{c}")
                tk.comp("vector", lambda v, c=c: v.tensor_tensor(
                    TMPC[:, :], x2c(c), PS[4][:, :], ALU.mult),
                    reads=[xk(c), "PS4"], writes=["TMPC"], tag=f"qh1m{tau}_{c}")
                tk.comp("vector", lambda v, c=c: v.tensor_tensor(
                    TMPC[:, :], TMPC[:, :], PS[5][:, :], ALU.subtract),
                    reads=["TMPC", "PS5"], writes=["TMPC"], tag=f"qh1s{tau}_{c}")
                tk.comp("vector", lambda v, c=c: v.tensor_scalar(
                    h1b(c), TMPC[:, :], G1[:, c:c + 1], B1[:, c:c + 1],
                    ALU.mult, ALU.add),
                    reads=["TMPC", "VEC"], writes=[f"H1B_{c}"], tag=f"qh1f{tau}_{c}")

            wqslot = WROT[(tau * 4) % 3]
            wload(wqslot, wq.rearrange("(c p) n -> p c n", p=128), f"wq{tau}")
            for qb in range(NCH):
                pb = 2 + qb % 2
                terms = [(lambda c=c, qb=qb: w_chunk(wqslot, c)[:, qb * 128:(qb + 1) * 128],
                          lambda c=c: h1b(c), [f"H1B_{c}"]) for c in range(NCH)]
                mm_group(PS[pb][:, :], terms, f"PS{pb}", [wqslot], f"q{tau}_{qb}")
                tk.comp("scalar", lambda s, qb=qb, pb=pb: s.activation(
                    qsc(qb), PS[pb][:, :], AF.Exp, scale=SCALE),
                    reads=[f"PS{pb}"], writes=[f"QS_{qb}"], tag=f"qe{tau}_{qb}")
            for h in range(HEADS):
                terms = [(lambda: ONESB[:, :], lambda c=c: qsc(c), [f"QS_{c}"])
                         for c in (2 * h, 2 * h + 1)]
                mm_group(PS[0][0:1, :], terms, "PS0", ["ONESB"], f"qsum{tau}_{h}")
                tk.comp("vector", lambda v: v.tensor_copy(row(0), PS[0][0:1, :]),
                        reads=["PS0"], writes=["ROW0"], tag=f"qsc{tau}_{h}")
                tk.comp("vector", lambda v: v.reciprocal(row(5), row(0)),
                        reads=["ROW0"], writes=["ROW5"], tag=f"qrec{tau}_{h}")
                pbq = 6 + h % 2
                bcast(pbq, row(5), ["ROW5"], f"qb{tau}_{h}")
                for c in (2 * h, 2 * h + 1):
                    tk.comp("vector", lambda v, c=c, pbq=pbq: v.tensor_tensor(
                        qsc(c), qsc(c), PS[pbq][:, :], ALU.mult),
                        reads=[f"QS_{c}", f"PS{pbq}"], writes=[f"QS_{c}"],
                        tag=f"qn{tau}_{c}")

            for h in range(HEADS):
                for eb in range(2):
                    ac = 2 * h + eb
                    pb = 2 + ac % 2
                    terms = [(lambda dc=dc, h=h, eb=eb: CTX_BF[:, (2 * h + dc) * 256 + eb * 128:
                                                                 (2 * h + dc) * 256 + eb * 128 + 128],
                              lambda dc=dc, h=h: qsc(2 * h + dc),
                              [f"QS_{2 * h + dc}"]) for dc in range(2)]
                    mm_group(PS[pb][:, :], terms, f"PS{pb}", ["CTX_BF"],
                             f"at{tau}_{ac}")
                    tk.comp("vector", lambda v, ac=ac, pb=pb: v.tensor_copy(
                        ub(ac), PS[pb][:, :]),
                        reads=[f"PS{pb}"], writes=[f"UB_{ac}"], tag=f"atc{tau}_{ac}")

            woslot = WROT[(tau * 4 + 1) % 3]
            wload(woslot, wo.rearrange("(c p) n -> p c n", p=128), f"wo{tau}")
            for db in range(NCH):
                pb = 4 + db % 2
                terms = [(lambda ac=ac, db=db: w_chunk(woslot, ac)[:, db * 128:(db + 1) * 128],
                          lambda ac=ac: ub(ac), [f"UB_{ac}"]) for ac in range(NCH)]
                mm_group(PS[pb][:, :], terms, f"PS{pb}", [woslot], f"wo{tau}_{db}")
                tk.comp("vector", lambda v, db=db, pb=pb: v.scalar_tensor_tensor(
                    resc(db), PS[pb][:, :], BO[:, db:db + 1], resc(db),
                    ALU.add, ALU.add),
                    reads=[f"PS{pb}", "VEC", rk(db)], writes=[rk(db)],
                    tag=f"woe{tau}_{db}")
            tap_featmajor("x2", RES, t0, [rk(c) for c in range(NCH)])

            def qsq(c):
                return QS[:, c * TT:(c + 1) * TT]

            stats8(resc, rk, qsq, lambda c: f"QS_{c}", f"l2_{tau}")
            rstd_from(5, 1, f"l2r_{tau}")
            tk.comp("vector", lambda v: v.tensor_tensor(
                row(2), row(0), row(1), ALU.mult),
                reads=["ROW0", "ROW1"], writes=["ROW2"], tag=f"l2q_{tau}")
            bcast(6, row(1), ["ROW1"], f"l2P_{tau}")
            bcast(7, row(2), ["ROW2"], f"l2Q_{tau}")
            for c in range(NCH):
                tk.comp("vector", lambda v, c=c: v.tensor_tensor(
                    TMPC[:, :], resc(c), PS[6][:, :], ALU.mult),
                    reads=[rk(c), "PS6"], writes=["TMPC"], tag=f"h2m{tau}_{c}")
                tk.comp("vector", lambda v, c=c: v.tensor_tensor(
                    TMPC[:, :], TMPC[:, :], PS[7][:, :], ALU.subtract),
                    reads=["TMPC", "PS7"], writes=["TMPC"], tag=f"h2s{tau}_{c}")
                tk.comp("vector", lambda v, c=c: v.tensor_scalar(
                    h1b(c), TMPC[:, :], G2[:, c:c + 1], B2[:, c:c + 1],
                    ALU.mult, ALU.add),
                    reads=["TMPC", "VEC"], writes=[f"H1B_{c}"], tag=f"h2f{tau}_{c}")

            for s4 in range(4):
                f0 = s4 * 1024
                wfslot = WROT[(tau * 4 + 2 + s4) % 3]
                wload(wfslot,
                      wf1.rearrange("(c p) f -> p c f", p=128)[:, :, f0:f0 + 1024],
                      f"wf1_{tau}_{s4}")
                for fb in range(8):
                    fbg = s4 * 8 + fb
                    pb = 2 + fbg % 2
                    terms = [(lambda c=c, fb=fb, wfslot=wfslot: w_chunk(wfslot, c)[:, fb * 128:(fb + 1) * 128],
                              lambda c=c: h1b(c), [f"H1B_{c}"]) for c in range(NCH)]
                    mm_group(PS[pb][:, :], terms, f"PS{pb}", [wfslot],
                             f"f1_{tau}_{fbg}")
                    tk.comp("scalar", lambda s, pb=pb, fbg=fbg: s.activation(
                        TMPC2[:, :], PS[pb][:, :], AF.Identity,
                        bias=BFF1[:, fbg:fbg + 1]),
                        reads=[f"PS{pb}", "VEC"], writes=["TMPC2"],
                        tag=f"f1b{tau}_{fbg}")
                    tk.comp("scalar", lambda s: s.activation(
                        TMPC[:, :], TMPC2[:, :], AF.Square),
                        reads=["TMPC2"], writes=["TMPC"], tag=f"f1sq{tau}_{fbg}")
                    tk.comp("vector", lambda v: v.tensor_scalar(
                        TMPC[:, :], TMPC[:, :], GELU_A * GELU_B, GELU_A,
                        ALU.mult, ALU.add),
                        reads=["TMPC"], writes=["TMPC"], tag=f"f1p{tau}_{fbg}")
                    tk.comp("vector", lambda v: v.tensor_tensor(
                        TMPC[:, :], TMPC[:, :], TMPC2[:, :], ALU.mult),
                        reads=["TMPC", "TMPC2"], writes=["TMPC"],
                        tag=f"f1i{tau}_{fbg}")
                    tk.comp("scalar", lambda s: s.activation(
                        TMPC[:, :], TMPC[:, :], AF.Tanh),
                        reads=["TMPC"], writes=["TMPC"], tag=f"f1t{tau}_{fbg}")
                    tk.comp("vector", lambda v: v.tensor_scalar(
                        TMPC[:, :], TMPC[:, :], 0.5, 0.5, ALU.mult, ALU.add),
                        reads=["TMPC"], writes=["TMPC"], tag=f"f1q{tau}_{fbg}")
                    tk.comp("vector", lambda v, fbg=fbg: v.tensor_tensor(
                        ub(fbg), TMPC[:, :], TMPC2[:, :], ALU.mult),
                        reads=["TMPC", "TMPC2"], writes=[f"UB_{fbg}"],
                        tag=f"f1u{tau}_{fbg}")

            for dbp in range(4):
                wfslot2 = WROT[(tau * 4 + dbp) % 3]
                dst = WSL[wfslot2][:, 0:NFC * 256].rearrange("p (c w) -> p c w", w=256)
                tk.dma("gpsimd", lambda g, dbp=dbp, dst=dst, wfslot2=wfslot2: g.dma_start(
                    out=dst,
                    in_=wf2.rearrange("(c p) d -> p c d", p=128)[:, :, dbp * 256:(dbp + 1) * 256]),
                    writes=[wfslot2], tag=f"wf2_{tau}_{dbp}")
                for dq in range(2):
                    db = dbp * 2 + dq
                    pb = 4 + db % 2
                    terms = [(lambda fc=fc, dq=dq, wfslot2=wfslot2:
                              WSL[wfslot2][:, fc * 256 + dq * 128:fc * 256 + dq * 128 + 128],
                              lambda fc=fc: ub(fc), [f"UB_{fc}"]) for fc in range(NFC)]
                    mm_group(PS[pb][:, :], terms, f"PS{pb}", [wfslot2],
                             f"f2_{tau}_{db}")
                    tk.comp("vector", lambda v, db=db, pb=pb: v.scalar_tensor_tensor(
                        resc(db), PS[pb][:, :], BF2[:, db:db + 1], resc(db),
                        ALU.add, ALU.add),
                        reads=[f"PS{pb}", "VEC", rk(db)], writes=[rk(db)],
                        tag=f"f2e{tau}_{db}")

            for c in range(NCH):
                tk.comp("vector", lambda v, c=c: v.tensor_copy(qsc(c), resc(c)),
                        reads=[rk(c)], writes=[f"QS_{c}"], tag=f"x3c{tau}_{c}")
            wdslot = WROT[(tau * 4 + 3) % 3]
            wload(wdslot, wd.rearrange("(c p) n -> p c n", p=128), f"wd{tau}")
            for db in range(NCH):
                pb = 2 + db % 2
                terms = [(lambda c=c, db=db: w_chunk(wdslot, c)[:, db * 128:(db + 1) * 128],
                          lambda c=c: qsc(c), [f"QS_{c}"]) for c in range(NCH)]
                mm_group(PS[pb][:, :], terms, f"PS{pb}", [wdslot],
                         f"sd{tau}_{db}")
                if "s" in taps:
                    tk.comp("vector", lambda v, db=db, pb=pb: v.tensor_scalar(
                        TMPC[:, :], PS[pb][:, :], BD[:, db:db + 1], None, ALU.add),
                        reads=[f"PS{pb}", "VEC"], writes=["TMPC"], tag="taps1")
                    tk.dma("sync", lambda s, db=db: s.dma_start(
                        out=taps["s"].rearrange("(c p) t -> p c t", p=128)[:, db, t0:t0 + TT],
                        in_=TMPC[:, :]), reads=["TMPC"], tag="taps")
                tk.comp("scalar", lambda sc, db=db, pb=pb: sc.activation(
                    TMPC2[:, :], PS[pb][:, :], AF.Identity, bias=BD[:, db:db + 1]),
                    reads=[f"PS{pb}", "VEC"], writes=["TMPC2"], tag=f"sb{tau}_{db}")
                tk.comp("vector", lambda v, db=db: v.tensor_tensor(
                    TMPC[:, :], TMPC2[:, :], g_sl(db, t0), ALU.mult),
                    reads=["TMPC2", f"G{db}"], writes=["TMPC"], tag=f"sg{tau}_{db}")
                tk.comp("gpsimd", lambda g, db=db: g.tensor_tensor(
                    x2c(db), x2c(db), TMPC[:, :], ALU.add),
                    reads=[xk(db), "TMPC"], writes=[xk(db)], tag=f"out{tau}_{db}")
            tk.dma("sync", lambda s: s.dma_start(
                out=outT.rearrange("(c p) t -> p c t", p=128)[:, :, t0:t0 + TT],
                in_=X2[:, :].rearrange("p (c t) -> p c t", t=TT)),
                reads=[xk(c) for c in range(NCH)], writes=["outT"], tag=f"store{tau}")

        for tau in range(NTILES):
            p2_tile(tau)

        p2s.close()
        g_scope.close()

        tk.barrier("final")

        tk.resolve()
        sems = {}
        for e in ENGINES:
            sems[("ev", e)] = stack.enter_context(nc.semaphore(f"ev_{e}"))
        for e in ("sync", "gpsimd"):
            for ln in range(NDMALANES):
                sems[("dma", e, ln)] = stack.enter_context(
                    nc.semaphore(f"dma_{e}_{ln}"))
        sems[("cc", "cc")] = stack.enter_context(nc.semaphore("ccsem"))
        block = stack.enter_context(nc.Block())
        tk.emit(block, sems)

    return nc, taps, tk.stats



def kernel(L0_attn_x, L0_attn_ln1_g, L0_attn_ln1_b, L0_attn_wq, L0_attn_wk,
           L0_attn_wv, L0_attn_wo, L0_attn_bo, L0_attn_ln2_g, L0_attn_ln2_b,
           L0_attn_w_ff1, L0_attn_b_ff1, L0_attn_w_ff2, L0_attn_b_ff2,
           L0_attn_w_d, L0_attn_b_d, L0_ffn_router_w, L0_ffn_w1, L0_ffn_w2,
           L0_ffn_expert_scale):
    f32 = np.float32

    def arr(a):
        return np.ascontiguousarray(np.asarray(a), dtype=f32)

    x = arr(L0_attn_x).reshape(TTOT, D)
    xTh = np.ascontiguousarray(x.T)
    w1_ = arr(L0_ffn_w1)
    w1g = np.ascontiguousarray(
        w1_.reshape(E, 4, 1024, NCH, 128).transpose(0, 1, 4, 3, 2)
    ).reshape(E * 4 * 128, NCH * 1024)
    w2_ = arr(L0_ffn_w2)
    w2g = np.ascontiguousarray(
        w2_.reshape(E, 4, 256, NFC, 128).transpose(0, 1, 4, 3, 2)
    ).reshape(E * 4 * 128, NFC * 256)
    rTh = np.ascontiguousarray(arr(L0_ffn_router_w).T)
    iotas = (np.arange(128, dtype=f32)[:, None]
             + 128.0 * np.arange(8, dtype=f32)[None, :])
    lstrict = np.tril(np.ones((128, 128), f32), -1).T
    lincl8 = np.tril(np.ones((8, 8), f32), 0).T
    eidx = np.broadcast_to(np.arange(8, dtype=f32)[None, :], (128, 8)).copy()

    def vecr(a, n):
        return np.ascontiguousarray(arr(a).reshape(n, 128).T)

    shared = {
        "wq": arr(L0_attn_wq), "wk": arr(L0_attn_wk), "wv": arr(L0_attn_wv),
        "wo": arr(L0_attn_wo), "wf1": arr(L0_attn_w_ff1), "wf2": arr(L0_attn_w_ff2),
        "wd": arr(L0_attn_w_d), "w1g": w1g, "w2g": w2g, "rT": rTh,
        "iotas": iotas, "lstrict": lstrict, "lincl8": lincl8, "eidx": eidx,
        "vg1": vecr(L0_attn_ln1_g, NCH), "vb1": vecr(L0_attn_ln1_b, NCH),
        "vg2": vecr(L0_attn_ln2_g, NCH), "vb2": vecr(L0_attn_ln2_b, NCH),
        "vbo": vecr(L0_attn_bo, NCH), "vbf2": vecr(L0_attn_b_ff2, NCH),
        "vbd": vecr(L0_attn_b_d, NCH), "vbf1": vecr(L0_attn_b_ff1, NFC),
        "vesc": arr(L0_ffn_expert_scale).reshape(1, E),
        "idn": np.eye(128, dtype=f32),
    }
    in_maps = []
    for c in range(NCORES):
        m = dict(shared)
        m["xT"] = np.ascontiguousarray(xTh[:, c * T:(c + 1) * T])
        in_maps.append(m)

    nc, taps, stats = build_program()
    res = run_bass_kernel_spmd(nc, in_maps, core_ids=list(range(NCORES)))

    out = np.empty((TTOT, D), dtype=f32)
    for c in range(NCORES):
        out[c * T:(c + 1) * T, :] = np.asarray(res.results[c]["outT"]).T
    if taps:
        kernel.tap_results = [
            {k: np.asarray(res.results[c]["tap_" + k]) for k in taps}
            for c in range(NCORES)]
        kernel.build_stats = stats
    return out.reshape(1, TTOT, D)


# version 06
# speedup vs baseline: 1.0270x; 1.0270x over previous
import numpy as np
import bass_rust
from contextlib import ExitStack
import concourse.bass as bass
import concourse.mybir as mybir
from concourse.bass_utils import run_bass_kernel_spmd

F32 = mybir.dt.float32
BF16 = mybir.dt.bfloat16
AF = mybir.ActivationFunctionType
ALU = mybir.AluOpType
AX = mybir.AxisListType.X

NCORES = 8
D = 1024
TTOT = 16384
T = TTOT // NCORES
TT = 512
NTILES = T // TT
NB = TT // 128
HEADS = 4
DH = 256
FF = 4096
E = 8
EPS = 1e-5
SCALE = DH ** (-0.25)
NCH = D // 128
NFC = FF // 128
GELU_A = 0.7978845608028654
GELU_B = 0.044715

DEBUG = {}

ENGINES = ("sync", "gpsimd", "scalar", "vector", "tensor")


NDMALANES = 8


class Op:
    __slots__ = ("idx", "eng", "fn", "reads", "writes", "kind", "tag",
                 "needs_event", "event_no", "waits", "bar_state", "lane")

    def __init__(self, idx, eng, fn, reads, writes, kind, tag):
        self.idx = idx
        self.eng = eng
        self.fn = fn
        self.reads = tuple(reads)
        self.writes = tuple(writes)
        self.kind = kind
        self.tag = tag
        self.needs_event = False
        self.event_no = None
        self.waits = []
        self.lane = None


class TK:

    def __init__(self):
        self.ops = []

    def comp(self, eng, fn, reads=(), writes=(), tag=""):
        op = Op(len(self.ops), eng, fn, reads, writes, "comp", tag)
        self.ops.append(op)
        return op

    def dma(self, queue, fn, reads=(), writes=(), tag=""):
        op = Op(len(self.ops), queue, fn, reads, writes, "dma", tag)
        self.ops.append(op)
        return op

    def cc(self, fn, reads=(), writes=(), tag=""):
        op = Op(len(self.ops), "gpsimd", fn, reads, writes, "cc", tag)
        self.ops.append(op)
        return op

    def barrier(self, tag="bar"):
        op = Op(len(self.ops), None, None, (), (), "barrier", tag)
        self.ops.append(op)
        return op

    def resolve(self):
        last_writer = {}
        readers = {}
        edges = []
        last_op_per_eng = {}
        for op in self.ops:
            if op.kind == "barrier":
                for e, lop in last_op_per_eng.items():
                    if lop.kind == "comp":
                        lop.needs_event = True
                continue
            deps = set()
            for r in op.reads:
                w = last_writer.get(r)
                if w is not None:
                    deps.add(w)
            for w in op.writes:
                lw = last_writer.get(w)
                if lw is not None:
                    deps.add(lw)
                for rd in readers.get(w, {}).values():
                    deps.add(rd)
            for d in deps:
                if d is op:
                    continue
                if d.eng == op.eng == "tensor" and d.kind == "comp" and op.kind == "comp":
                    continue
                edges.append((d, op))
            for r in op.reads:
                readers.setdefault(r, {})[op.eng] = op
            for w in op.writes:
                last_writer[w] = op
                readers[w] = {}
            last_op_per_eng[op.eng] = op
        for p, c in edges:
            if p.kind == "comp":
                p.needs_event = True
        ev_count = {e: 0 for e in ENGINES}
        dma_lane_count = {}
        dma_queue_idx = {e: 0 for e in ENGINES}
        cc_count = 0
        chan_of, val_of = {}, {}
        for op in self.ops:
            if op.kind == "comp":
                if op.needs_event:
                    ev_count[op.eng] += 1
                    op.event_no = ev_count[op.eng]
                chan_of[op] = ("ev", op.eng)
                val_of[op] = op.event_no
            elif op.kind == "dma":
                lane = dma_queue_idx[op.eng] % NDMALANES
                dma_queue_idx[op.eng] += 1
                op.lane = lane
                key = (op.eng, lane)
                dma_lane_count[key] = dma_lane_count.get(key, 0) + 16
                op.event_no = dma_lane_count[key]
                chan_of[op] = ("dma", op.eng, lane)
                val_of[op] = op.event_no
            elif op.kind == "cc":
                cc_count += 1
                op.event_no = cc_count
                chan_of[op] = ("cc", "cc")
                val_of[op] = op.event_no
        edges_by_consumer = {}
        for p, c in edges:
            edges_by_consumer.setdefault(c.idx, []).append(p)
        frontier = {e: {} for e in ENGINES}
        run_ev = {e: 0 for e in ENGINES}
        run_dma = {}
        run_cc = 0
        for op in self.ops:
            if op.kind == "barrier":
                op.bar_state = (dict(run_ev), dict(run_dma), run_cc)
                continue
            need = {}
            for p in edges_by_consumer.get(op.idx, ()):
                v = val_of[p]
                if v is None:
                    continue
                ch = chan_of[p]
                if need.get(ch, 0) < v:
                    need[ch] = v
            if op.kind == "dma" and op.event_no > 16:
                ch = chan_of[op]
                if need.get(ch, 0) < op.event_no - 16:
                    need[ch] = op.event_no - 16
            fr = frontier[op.eng]
            for ch, v in sorted(need.items(), key=lambda kv: str(kv[0])):
                if fr.get(ch, 0) >= v:
                    continue
                op.waits.append((ch, v))
                fr[ch] = v
            if op.kind == "comp":
                if op.needs_event:
                    run_ev[op.eng] = op.event_no
            elif op.kind == "dma":
                run_dma[chan_of[op]] = op.event_no
            elif op.kind == "cc":
                run_cc = op.event_no
        self.stats = dict(n_ops=len(self.ops), ev=dict(ev_count),
                          dma=dict(dma_queue_idx), edges=len(edges))

    def emit(self, block, sems):
        per_eng = {e: [] for e in ENGINES}
        for op in self.ops:
            if op.kind == "barrier":
                ev, dma, cc = op.bar_state
                for e in ENGINES:
                    items = []
                    for e2 in ENGINES:
                        if ev[e2] > 0:
                            items.append((("ev", e2), ev[e2]))
                    for ch, v in dma.items():
                        items.append((ch, v))
                    if cc > 0:
                        items.append((("cc", "cc"), cc))
                    per_eng[e].append(("waits", items))
                continue
            per_eng[op.eng].append(("op", op))

        def emitter(eng_name, eng):
            frontier = {}
            for kind, payload in per_eng[eng_name]:
                if kind == "waits":
                    for ch, v in payload:
                        if frontier.get(ch, 0) >= v:
                            continue
                        eng.wait_ge(sems[ch], v)
                        frontier[ch] = v
                    continue
                op = payload
                for ch, v in op.waits:
                    if frontier.get(ch, 0) >= v:
                        continue
                    eng.wait_ge(sems[ch], v)
                    frontier[ch] = v
                inst = op.fn(eng)
                if op.kind == "dma":
                    inst.then_inc(sems[("dma", op.eng, op.lane)], 16)
                elif op.kind == "cc":
                    inst.then_inc(sems[("cc", "cc")])
                elif op.needs_event:
                    inst.then_inc(sems[("ev", op.eng)], 1)

        @block.sync
        def _(eng):
            emitter("sync", eng)

        @block.gpsimd
        def _(eng):
            emitter("gpsimd", eng)

        @block.scalar
        def _(eng):
            emitter("scalar", eng)

        @block.vector
        def _(eng):
            emitter("vector", eng)

        @block.tensor
        def _(eng):
            emitter("tensor", eng)



def build_program():
    nc = bass.Bass("TRN2", target_bir_lowering=False)
    tk = TK()

    xT = nc.dram_tensor("xT", [D, T], F32, kind="ExternalInput").ap()
    wq = nc.dram_tensor("wq", [D, D], F32, kind="ExternalInput").ap()
    wk = nc.dram_tensor("wk", [D, D], F32, kind="ExternalInput").ap()
    wv = nc.dram_tensor("wv", [D, D], F32, kind="ExternalInput").ap()
    wo = nc.dram_tensor("wo", [D, D], F32, kind="ExternalInput").ap()
    wf1 = nc.dram_tensor("wf1", [D, FF], F32, kind="ExternalInput").ap()
    wf2 = nc.dram_tensor("wf2", [FF, D], F32, kind="ExternalInput").ap()
    wd = nc.dram_tensor("wd", [D, D], F32, kind="ExternalInput").ap()
    w1g = nc.dram_tensor("w1g", [E * 4 * 128, NCH * 1024], F32, kind="ExternalInput").ap()
    w2g = nc.dram_tensor("w2g", [E * 4 * 128, NFC * 256], F32, kind="ExternalInput").ap()
    iotas_in = nc.dram_tensor("iotas", [128, 8], F32, kind="ExternalInput").ap()
    lstr_in = nc.dram_tensor("lstrict", [128, 128], F32, kind="ExternalInput").ap()
    lincl_in = nc.dram_tensor("lincl8", [8, 8], F32, kind="ExternalInput").ap()
    eidx_in = nc.dram_tensor("eidx", [128, 8], F32, kind="ExternalInput").ap()
    rT = nc.dram_tensor("rT", [D, E], F32, kind="ExternalInput").ap()
    vg1 = nc.dram_tensor("vg1", [128, NCH], F32, kind="ExternalInput").ap()
    vb1 = nc.dram_tensor("vb1", [128, NCH], F32, kind="ExternalInput").ap()
    vg2 = nc.dram_tensor("vg2", [128, NCH], F32, kind="ExternalInput").ap()
    vb2 = nc.dram_tensor("vb2", [128, NCH], F32, kind="ExternalInput").ap()
    vbo = nc.dram_tensor("vbo", [128, NCH], F32, kind="ExternalInput").ap()
    vbf2 = nc.dram_tensor("vbf2", [128, NCH], F32, kind="ExternalInput").ap()
    vbd = nc.dram_tensor("vbd", [128, NCH], F32, kind="ExternalInput").ap()
    vbf1 = nc.dram_tensor("vbf1", [128, NFC], F32, kind="ExternalInput").ap()
    vesc = nc.dram_tensor("vesc", [1, E], F32, kind="ExternalInput").ap()
    idn = nc.dram_tensor("idn", [128, 128], F32, kind="ExternalInput").ap()
    outT = nc.dram_tensor("outT", [D, T], F32, kind="ExternalOutput").ap()

    ct_dram = nc.dram_tensor("ct_dram", [E, T], F32)
    rows_dram = nc.dram_tensor("rows_dram", [4, T], F32)
    xnb_dram = nc.dram_tensor("xnb_dram", [T, D], BF16)
    tok_dram = nc.dram_tensor("tok_dram", [16 * TT, 8], F32)
    ys_dram = nc.dram_tensor("ys_dram", [T, D], F32)
    off_dram = nc.dram_tensor("off_dram", [8, 16], F32)
    cum_dram = nc.dram_tensor("cum_dram", [8, 1], F32)
    cc_in = nc.dram_tensor("cc_in", [128, 8 * 257], F32)
    cc_out = nc.dram_tensor("cc_out", [128, 8 * 257], F32)

    taps = {}
    for name, shape in [("xn32", [D, T]), ("ctx", [128, 8 * 257]), ("cmb", [E, T]),
                        ("ctxn", [128, 8 * 256]), ("g", [D, T]), ("h1", [D, T]),
                        ("x2", [D, T]), ("s", [D, T]), ("moe", [D, T]),
                        ("cmdbg", [128, 64]), ("kv", [128, 1024 + 1028]),
                        ("hsb", [128, 2 * 8 * TT]), ("srt", [16 * TT, 8]),
                        ("ys", [T, D])]:
        if DEBUG.get(name):
            taps[name] = nc.dram_tensor("tap_" + name, shape, F32, kind="ExternalOutput").ap()

    stack = ExitStack()
    with stack:
        VEC = stack.enter_context(nc.sbuf_tensor("VEC", [128, 7 * NCH + NFC], F32))
        ESC = stack.enter_context(nc.sbuf_tensor("ESCb", [1, E], F32))
        ONESC = stack.enter_context(nc.sbuf_tensor("ONESC", [128, 1], F32))
        ONESB = stack.enter_context(nc.sbuf_tensor("ONESB", [128, 1], BF16))
        ONESR = stack.enter_context(nc.sbuf_tensor("ONESR", [1, 128], F32))
        EPSR = stack.enter_context(nc.sbuf_tensor("EPSR", [1, 1], F32))
        IDN = stack.enter_context(nc.sbuf_tensor("IDN", [128, 128], F32))
        CTX_BF = stack.enter_context(nc.sbuf_tensor("CTX_BF", [128, 8 * 256], BF16))
        WSL = {
            "WA": stack.enter_context(nc.sbuf_tensor("WA", [128, 8 * 1024], BF16)),
            "WB": stack.enter_context(nc.sbuf_tensor("WB", [128, 8 * 1024], BF16)),
            "WC": stack.enter_context(nc.sbuf_tensor("WC", [128, 8 * 1024], BF16)),
        }
        WROT = ["WA", "WB", "WC"]
        ROWS = stack.enter_context(nc.sbuf_tensor("ROWS", [1, 6 * TT], F32))
        CMB = stack.enter_context(nc.sbuf_tensor("CMBt", [128, 64], F32))
        EXPB = stack.enter_context(nc.sbuf_tensor("EXPB", [128, 16], F32))
        ESCB = stack.enter_context(nc.sbuf_tensor("ESCB", [128, 8], F32))
        IOTAS = stack.enter_context(nc.sbuf_tensor("IOTAS", [128, 8], F32))
        EIDX = stack.enter_context(nc.sbuf_tensor("EIDXs", [128, 8], F32))
        IDNB = stack.enter_context(nc.sbuf_tensor("IDNB", [128, 128], BF16))
        CTS = stack.enter_context(nc.sbuf_tensor("CTS", [8, 128], F32))
        TMPC = stack.enter_context(nc.sbuf_tensor("TMPC", [128, TT], F32))
        TMPC2 = stack.enter_context(nc.sbuf_tensor("TMPC2", [128, TT], F32))
        PS = [stack.enter_context(nc.psum_tensor(f"PS{i}", [128, 512], F32))
              for i in range(8)]

        def row(i):
            return ROWS[:, i * TT:(i + 1) * TT]

        G1 = VEC[:, 0 * NCH:1 * NCH]
        B1 = VEC[:, 1 * NCH:2 * NCH]
        G2 = VEC[:, 2 * NCH:3 * NCH]
        B2 = VEC[:, 3 * NCH:4 * NCH]
        BO = VEC[:, 4 * NCH:5 * NCH]
        BF2 = VEC[:, 5 * NCH:6 * NCH]
        BD = VEC[:, 6 * NCH:7 * NCH]
        BFF1 = VEC[:, 7 * NCH:7 * NCH + NFC]

        tk.comp("vector", lambda v: v.memset(ONESC[:, :], 1.0), writes=["ONESC"])
        tk.comp("vector", lambda v: v.memset(ONESB[:, :], 1.0), writes=["ONESB"])
        tk.comp("vector", lambda v: v.memset(ONESR[:, :], 1.0), writes=["ONESR"])
        tk.comp("vector", lambda v: v.memset(EPSR[:, :], EPS), writes=["EPSR"])
        tk.dma("sync", lambda s: s.dma_start(out=IDN[:, :], in_=idn), writes=["IDN"])
        for i, src in enumerate([vg1, vb1, vg2, vb2, vbo, vbf2, vbd]):
            tk.dma("sync", lambda s, i=i, src=src: s.dma_start(
                out=VEC[:, i * NCH:(i + 1) * NCH], in_=src), writes=["VEC"])
        tk.dma("sync", lambda s: s.dma_start(out=VEC[:, 7 * NCH:7 * NCH + NFC], in_=vbf1),
               writes=["VEC"])
        tk.dma("sync", lambda s: s.dma_start(out=ESC[:, :], in_=vesc), writes=["ESC"])
        tk.dma("sync", lambda s: s.dma_start(out=IOTAS[:, :], in_=iotas_in), writes=["IOTAS"])
        tk.dma("sync", lambda s: s.dma_start(out=EIDX[:, :], in_=eidx_in), writes=["EIDX"])
        tk.comp("scalar", lambda sc: sc.activation(IDNB[:, :], IDN[:, :], AF.Identity),
                reads=["IDN"], writes=["IDNB"], tag="idnb")
        tk.comp("tensor", lambda t: t.matmul(PS[0][:, 0:8], ONESR[:, :], ESC[:, :],
                                             start=True, stop=True),
                reads=["ONESR", "ESC"], writes=["PS0"], tag="escb_mm")
        tk.comp("vector", lambda v: v.tensor_copy(ESCB[:, :], PS[0][:, 0:8]),
                reads=["PS0"], writes=["ESCB"], tag="escb_cp")

        def wload(slot, src_ap3, tag, nchunks=8, width=1024):
            dst = WSL[slot][:, 0:nchunks * width].rearrange("p (c w) -> p c w", w=width)
            tk.dma("gpsimd", lambda g: g.dma_start(out=dst, in_=src_ap3),
                   writes=[slot], tag=tag)

        def w_chunk(slot, c, width=1024):
            return WSL[slot][:, c * width:(c + 1) * width]

        def mm_group(bankslice, terms, psname, reads, tag):
            nt = len(terms)
            for i, term in enumerate(terms):
                lf, rf = term[0], term[1]
                extra = list(term[2]) if len(term) > 2 else []
                tk.comp("tensor",
                        lambda t, lf=lf, rf=rf, i=i: t.matmul(
                            bankslice, lf(), rf(), start=(i == 0), stop=(i == nt - 1)),
                        reads=list(reads) + extra, writes=[psname], tag=tag)

        def bcast(psbank, src_row, src_reads, tag):
            tk.comp("tensor", lambda t: t.matmul(
                PS[psbank][:, :], ONESR[:, :], src_row, start=True, stop=True),
                reads=list(src_reads) + ["ONESR"], writes=[f"PS{psbank}"], tag=tag)

        def stats8(xf32_fn, src_key_fn, sq_fn, sq_key_fn, tag):
            terms = [(lambda c=c: ONESC[:, :], lambda c=c: xf32_fn(c), [src_key_fn(c)])
                     for c in range(NCH)]
            mm_group(PS[0][0:1, :], terms, "PS0", ["ONESC"], tag + "_sumx")
            for c in range(NCH):
                tk.comp("scalar", lambda s, c=c: s.activation(
                    sq_fn(c), xf32_fn(c), AF.Square),
                    reads=[src_key_fn(c)], writes=[sq_key_fn(c)], tag=tag + "_sq")
            terms = [(lambda c=c: ONESB[:, :], lambda c=c: sq_fn(c), [sq_key_fn(c)])
                     for c in range(NCH)]
            mm_group(PS[1][0:1, :], terms, "PS1", ["ONESB"], tag + "_sumsq")
            tk.comp("vector", lambda v: v.tensor_scalar(
                row(0), PS[0][0:1, :], 1.0 / D, None, ALU.mult),
                reads=["PS0"], writes=["ROW0"], tag=tag + "_mu")
            tk.comp("vector", lambda v: v.tensor_scalar(
                row(5), PS[1][0:1, :], 1.0 / D, None, ALU.mult),
                reads=["PS1"], writes=["ROW5"], tag=tag + "_msq")
            tk.comp("vector", lambda v: v.tensor_tensor(
                row(4), row(0), row(0), ALU.mult),
                reads=["ROW0"], writes=["ROW4"], tag=tag + "_mu2")
            tk.comp("vector", lambda v: v.tensor_tensor(
                row(5), row(5), row(4), ALU.subtract),
                reads=["ROW5", "ROW4"], writes=["ROW5"], tag=tag + "_var")

        def rstd_from(var_slot, out_slot, tag):
            tk.comp("scalar", lambda s: s.activation(
                row(4), row(var_slot), AF.Ln, bias=EPSR[0:1, 0:1]),
                reads=[f"ROW{var_slot}", "EPSR"], writes=["ROW4"], tag=tag + "_ln")
            tk.comp("scalar", lambda s: s.activation(
                row(out_slot), row(4), AF.Exp, scale=-0.5),
                reads=["ROW4"], writes=[f"ROW{out_slot}"], tag=tag + "_rstd")

        def ln01_rows(tag):
            rstd_from(5, 1, tag + "_r0")
            tk.comp("vector", lambda v: v.tensor_tensor(
                row(2), row(0), row(1), ALU.mult),
                reads=["ROW0", "ROW1"], writes=["ROW2"], tag=tag + "_Q0")
            tk.comp("vector", lambda v: v.tensor_tensor(
                row(3), row(5), row(1), ALU.mult),
                reads=["ROW5", "ROW1"], writes=["ROW3"], tag=tag + "_v1a")
            tk.comp("vector", lambda v: v.tensor_tensor(
                row(3), row(3), row(1), ALU.mult),
                reads=["ROW3", "ROW1"], writes=["ROW3"], tag=tag + "_v1b")
            rstd_from(3, 3, tag + "_r1")
            tk.comp("vector", lambda v: v.tensor_tensor(
                row(3), row(3), row(1), ALU.mult),
                reads=["ROW3", "ROW1"], writes=["ROW3"], tag=tag + "_P01")
            tk.comp("vector", lambda v: v.tensor_tensor(
                row(4), row(0), row(3), ALU.mult),
                reads=["ROW0", "ROW3"], writes=["ROW4"], tag=tag + "_Q01")

        def tap_featmajor(name, src_tensor, t0, reads, f32_chunks=NCH):
            if name not in taps:
                return
            tk.dma("sync", lambda s: s.dma_start(
                out=taps[name].rearrange("(c p) t -> p c t", p=128)[:, :, t0:t0 + TT],
                in_=src_tensor[:, 0:f32_chunks * TT].rearrange("p (c t) -> p c t", t=TT)),
                reads=list(reads), tag="tap_" + name)

        p0_scope = ExitStack()
        XBD = p0_scope.enter_context(nc.sbuf_tensor("XBD", [128, 2 * NCH * TT], F32))
        XN32 = p0_scope.enter_context(nc.sbuf_tensor("XN32", [128, NCH * TT], F32))
        H1P = p0_scope.enter_context(nc.sbuf_tensor("H1P", [128, NCH * TT], BF16))
        XSQ = p0_scope.enter_context(nc.sbuf_tensor("XSQ", [128, NCH * TT], BF16))
        KE = p0_scope.enter_context(nc.sbuf_tensor("KE", [128, 1024], BF16))
        VA = p0_scope.enter_context(nc.sbuf_tensor("VA", [128, 4 * 257], BF16))
        CTX32 = p0_scope.enter_context(nc.sbuf_tensor("CTX32", [128, 8 * 257], F32))
        RTB = p0_scope.enter_context(nc.sbuf_tensor("RTB", [128, NCH * E], F32))
        BCA = p0_scope.enter_context(nc.sbuf_tensor("BCA", [128, TT], F32))
        BCB = p0_scope.enter_context(nc.sbuf_tensor("BCB", [128, TT], F32))
        XNT = p0_scope.enter_context(nc.sbuf_tensor("XNT", [128, NCH * TT], BF16))
        XROW = p0_scope.enter_context(nc.sbuf_tensor("XROW", [128, 2 * D], BF16))
        LSTR = p0_scope.enter_context(nc.sbuf_tensor("LSTR", [128, 128], F32))
        LIN8 = p0_scope.enter_context(nc.sbuf_tensor("LIN8", [8, 8], F32))
        BCNTS = p0_scope.enter_context(nc.sbuf_tensor("BCNTS", [8, 16], F32))
        RNKSB = p0_scope.enter_context(nc.sbuf_tensor("RNKSB", [128, 16 * 8], F32))
        MSB = p0_scope.enter_context(nc.sbuf_tensor("MSB", [128, 16 * 8], F32))
        M1SB = p0_scope.enter_context(nc.sbuf_tensor("M1SB", [128, 16 * 8], F32))
        CMBSB = p0_scope.enter_context(nc.sbuf_tensor("CMBSB", [128, 16 * 8], F32))
        SRT = p0_scope.enter_context(nc.sbuf_tensor("SRT", [128, 16], F32))
        TOKR = p0_scope.enter_context(nc.sbuf_tensor("TOKR", [128, 8], F32))
        IDXI = p0_scope.enter_context(nc.sbuf_tensor("IDXI", [128, 1], mybir.dt.int32))
        ZZ = p0_scope.enter_context(nc.sbuf_tensor("ZZ", [128, 2048], F32))

        tk.comp("vector", lambda v: v.memset(VA[:, :], 1.0), writes=["VA"])
        tk.comp("vector", lambda v: v.memset(CTX32[:, :], 0.0), writes=["CTX32"])
        tk.comp("vector", lambda v: v.memset(ZZ[:, :], 0.0), writes=["ZZ"])
        tk.comp("vector", lambda v: v.memset(TOKR[:, :], 0.0), writes=["TOKR"])
        tk.dma("sync", lambda s: s.dma_start(out=LSTR[:, :], in_=lstr_in), writes=["LSTR"])
        tk.dma("sync", lambda s: s.dma_start(out=LIN8[:, :], in_=lincl_in), writes=["LIN8"])
        tk.dma("sync", lambda s: s.dma_start(
            out=tok_dram.ap().rearrange("(p r) c -> p (r c)", p=128), in_=ZZ[:, 0:512]),
            reads=["ZZ"], writes=["tok_dram"], tag="tokzero")
        for zz in range(8):
            tk.dma("sync", lambda s, zz=zz: s.dma_start(
                out=ys_dram.ap()[zz * 256:(zz + 1) * 256, :]
                .rearrange("(p r) c -> p (r c)", p=128), in_=ZZ[:, :]),
                reads=["ZZ"], writes=["ys_dram"], tag=f"yszero{zz}")

        wload("WA", wk.rearrange("(c p) n -> p c n", p=128), "wk")
        wload("WB", wv.rearrange("(c p) n -> p c n", p=128), "wv")
        tk.dma("sync", lambda s: s.dma_start(
            out=RTB[:, :].rearrange("p (c e) -> p c e", e=E),
            in_=rT.rearrange("(c p) e -> p c e", p=128)), writes=["RTB"], tag="rt")

        def combine_block(tau, b):
            def V(fn, reads, writes, tag):
                tk.comp("vector", fn, reads=reads, writes=writes, tag=f"{tag}{tau}_{b}")

            V(lambda v: v.tensor_reduce(CMB[:, 8:9], PS[4][:, 0:E], AX, ALU.max),
              ["PS4"], ["CMB"], "cm_max")
            V(lambda v: v.tensor_scalar(CMB[:, 0:E], PS[4][:, 0:E], CMB[:, 8:9],
                                        None, ALU.subtract),
              ["PS4", "CMB"], ["CMB"], "cm_sh")
            tk.comp("scalar", lambda s: s.activation(CMB[:, 0:E], CMB[:, 0:E], AF.Exp),
                    reads=["CMB"], writes=["CMB"], tag=f"cm_exp{tau}_{b}")
            V(lambda v: v.tensor_reduce(CMB[:, 9:10], CMB[:, 0:E], AX, ALU.add),
              ["CMB"], ["CMB"], "cm_sum")
            V(lambda v: v.reciprocal(CMB[:, 10:11], CMB[:, 9:10]),
              ["CMB"], ["CMB"], "cm_rcp")
            V(lambda v: v.tensor_scalar(CMB[:, 0:E], CMB[:, 0:E], CMB[:, 10:11],
                                        None, ALU.mult),
              ["CMB"], ["CMB"], "cm_p")
            V(lambda v: v.tensor_reduce(CMB[:, 11:12], CMB[:, 0:E], AX, ALU.max),
              ["CMB"], ["CMB"], "cm_m1")
            V(lambda v: v.tensor_scalar(CMB[:, 32:32 + E], CMB[:, 0:E], CMB[:, 11:12],
                                        None, ALU.is_ge),
              ["CMB"], ["CMB"], "cm_g1")
            V(lambda v: v.tensor_scalar(CMB[:, 40:40 + E], CMB[:, 32:32 + E], 2.0,
                                        None, ALU.mult),
              ["CMB"], ["CMB"], "cm_g2")
            V(lambda v: v.tensor_tensor(CMB[:, 24:24 + E], CMB[:, 0:E],
                                        CMB[:, 40:40 + E], ALU.subtract),
              ["CMB"], ["CMB"], "cm_q8")
            V(lambda v: v.tensor_reduce(CMB[:, 12:13], CMB[:, 24:24 + E], AX, ALU.max),
              ["CMB"], ["CMB"], "cm_m2")
            V(lambda v: v.tensor_scalar(CMB[:, 16:16 + E], CMB[:, 0:E], CMB[:, 12:13],
                                        None, ALU.is_ge),
              ["CMB"], ["CMB"], "cm_msk")
            tk.comp("scalar", lambda s: s.activation(CMB[:, 0:E], CMB[:, 0:E], AF.Exp),
                    reads=["CMB"], writes=["CMB"], tag=f"cm_ep{tau}_{b}")
            V(lambda v: v.tensor_tensor(CMB[:, 0:E], CMB[:, 0:E], CMB[:, 16:16 + E],
                                        ALU.mult),
              ["CMB"], ["CMB"], "cm_em")
            V(lambda v: v.tensor_reduce(CMB[:, 13:14], CMB[:, 0:E], AX, ALU.add),
              ["CMB"], ["CMB"], "cm_ds")
            V(lambda v: v.reciprocal(CMB[:, 14:15], CMB[:, 13:14]),
              ["CMB"], ["CMB"], "cm_dr")
            V(lambda v: v.tensor_scalar(CMB[:, 0:E], CMB[:, 0:E], CMB[:, 14:15],
                                        None, ALU.mult),
              ["CMB"], ["CMB"], "cm_c")
            tk.comp("tensor", lambda t: t.matmul(
                PS[5][0:E, 0:128], CMB[:, 0:E], IDN[:, :], start=True, stop=True),
                reads=["CMB", "IDN"], writes=["PS5"], tag=f"cm_tr{tau}_{b}")
            V(lambda v: v.tensor_copy(CTS[:, :], PS[5][0:E, 0:128]),
              ["PS5"], ["CTS"], "cm_ct")

        def p0_tile(tau):
            t0 = tau * TT
            par = tau % 2
            xbk = f"XB{par}"
            XBt = XBD[:, par * NCH * TT:(par + 1) * NCH * TT]

            def xb(c):
                return XBt[:, c * TT:(c + 1) * TT]

            def xn32(c):
                return XN32[:, c * TT:(c + 1) * TT]

            def h1c(c):
                return H1P[:, c * TT:(c + 1) * TT]

            def xsq(c):
                return XSQ[:, c * TT:(c + 1) * TT]

            tk.dma("sync", lambda s: s.dma_start(
                out=XBt.rearrange("p (c t) -> p c t", t=TT),
                in_=xT.rearrange("(c p) t -> p c t", p=128)[:, :, t0:t0 + TT]),
                writes=[xbk], tag=f"xload{tau}")

            stats8(xb, lambda c: xbk, xsq, lambda c: f"XSQ_{c}", f"ln0_{tau}")
            ln01_rows(f"ln01_{tau}")
            tk.dma("sync", lambda s: s.dma_start(
                out=rows_dram.ap()[:, t0:t0 + TT], in_=ROWS[0:1, TT:5 * TT]),
                reads=["ROW1", "ROW2", "ROW3", "ROW4"], writes=["rows_dram"],
                tag=f"rsave{tau}")
            bcast(2, row(1), ["ROW1"], f"b_r0_{tau}")
            bcast(3, row(2), ["ROW2"], f"b_q0_{tau}")
            bcast(4, row(3), ["ROW3"], f"b_p01_{tau}")
            bcast(5, row(4), ["ROW4"], f"b_q01_{tau}")
            tk.comp("scalar", lambda s: s.activation(
                BCA[:, :], PS[2][:, :], AF.Identity),
                reads=["PS2"], writes=["BCA"], tag=f"stg_r0_{tau}")
            tk.comp("scalar", lambda s: s.activation(
                BCB[:, :], PS[3][:, :], AF.Identity),
                reads=["PS3"], writes=["BCB"], tag=f"stg_q0_{tau}")
            for c in range(NCH):
                tk.comp("gpsimd", lambda g, c=c: g.tensor_tensor(
                    xn32(c), xb(c), BCA[:, :], ALU.mult),
                    reads=[xbk, "BCA"], writes=[f"XN32_{c}"], tag=f"xnm{tau}_{c}")
                tk.comp("gpsimd", lambda g, c=c: g.tensor_tensor(
                    xn32(c), xn32(c), BCB[:, :], ALU.subtract),
                    reads=[f"XN32_{c}", "BCB"], writes=[f"XN32_{c}"], tag=f"xns{tau}_{c}")
                tk.comp("gpsimd", lambda g, c=c: g.tensor_copy(
                    XNT[:, c * TT:(c + 1) * TT], xn32(c)),
                    reads=[f"XN32_{c}"], writes=[f"XNT_{c}"], tag=f"xnc{tau}_{c}")
                tk.comp("vector", lambda v, c=c: v.tensor_tensor(
                    TMPC[:, :], xb(c), PS[4][:, :], ALU.mult),
                    reads=[xbk, "PS4"], writes=["TMPC"], tag=f"h1m{tau}_{c}")
                tk.comp("vector", lambda v, c=c: v.tensor_tensor(
                    TMPC[:, :], TMPC[:, :], PS[5][:, :], ALU.subtract),
                    reads=["TMPC", "PS5"], writes=["TMPC"], tag=f"h1s{tau}_{c}")
                tk.comp("vector", lambda v, c=c: v.tensor_scalar(
                    h1c(c), TMPC[:, :], G1[:, c:c + 1], B1[:, c:c + 1],
                    ALU.mult, ALU.add),
                    reads=["TMPC", "VEC"], writes=[f"H1_{c}"], tag=f"h1f{tau}_{c}")
            tap_featmajor("xn32", XN32, t0, [f"XN32_{c}" for c in range(NCH)])
            if "h1" in taps:
                for c in range(NCH):
                    tk.comp("vector", lambda v, c=c: v.tensor_copy(TMPC2[:, :], h1c(c)),
                            reads=[f"H1_{c}"], writes=["TMPC2"], tag="taph1c")
                    tk.dma("sync", lambda s, c=c: s.dma_start(
                        out=taps["h1"].rearrange("(c p) t -> p c t", p=128)[:, c, t0:t0 + TT],
                        in_=TMPC2[:, :]), reads=["TMPC2"], tag="taph1")

            def p0_block(b):
                tb = b * 128
                terms = [(lambda c=c: xn32(c)[:, tb:tb + 128],
                          lambda c=c: RTB[:, c * E:(c + 1) * E], [f"XN32_{c}"])
                         for c in range(NCH)]
                mm_group(PS[4][:, 0:E], terms, "PS4", ["RTB"], f"rt{tau}_{b}")

                for proj, slot, pk in (("k", "WA", 5), ("v", "WB", 6)):
                    for hh in range(2):
                        terms = [(lambda c=c: h1c(c)[:, tb:tb + 128],
                                  lambda c=c, hh=hh, slot=slot: w_chunk(slot, c)[:, hh * 512:(hh + 1) * 512],
                                  [f"H1_{c}"])
                                 for c in range(NCH)]
                        mm_group(PS[pk][:, :], terms, f"PS{pk}", [slot],
                                 f"{proj}{tau}_{b}_{hh}")
                        if proj == "k":
                            tk.comp("scalar", lambda s, hh=hh: s.activation(
                                KE[:, hh * 512:(hh + 1) * 512], PS[5][:, :],
                                AF.Exp, scale=SCALE),
                                reads=["PS5"], writes=["KE"], tag=f"ke{tau}_{b}_{hh}")
                        else:
                            for hq in range(2):
                                h = hh * 2 + hq
                                tk.comp("vector", lambda v, hq=hq, h=h: v.tensor_copy(
                                    VA[:, h * 257:h * 257 + 256],
                                    PS[6][:, hq * 256:(hq + 1) * 256]),
                                    reads=["PS6"], writes=["VA"], tag=f"va{tau}_{b}_{h}")
                for j in range(8):
                    h, db = j // 2, j % 2
                    pb = 7 if j % 2 == 0 else 6
                    tk.comp("tensor", lambda t, h=h, db=db, pb=pb: t.matmul(
                        PS[pb][:, 0:257],
                        KE[:, h * 256 + db * 128:h * 256 + db * 128 + 128],
                        VA[:, h * 257:(h + 1) * 257],
                        start=True, stop=True),
                        reads=["KE", "VA"], writes=[f"PS{pb}"], tag=f"ctx{tau}_{b}_{j}")
                    tk.comp("vector", lambda v, j=j, pb=pb: v.tensor_tensor(
                        CTX32[:, j * 257:(j + 1) * 257],
                        CTX32[:, j * 257:(j + 1) * 257],
                        PS[pb][:, 0:257], ALU.add),
                        reads=["CTX32", f"PS{pb}"], writes=["CTX32"],
                        tag=f"ctxa{tau}_{b}_{j}")

                combine_block(tau, b)
                if "cmb" in taps:
                    tk.dma("sync", lambda s: s.dma_start(
                        out=ct_dram.ap()[:, t0 + tb:t0 + tb + 128], in_=CTS[:, :]),
                        reads=["CTS"], writes=["ct_dram"], tag=f"ctd{tau}_{b}")
                gb = 4 * tau + b
                V8 = lambda a: a
                tk.comp("vector", lambda v, gb=gb: v.tensor_copy(
                    MSB[:, gb * 8:(gb + 1) * 8], CMB[:, 16:24]),
                    reads=["CMB"], writes=["MSB"], tag=f"svm{gb}")
                tk.comp("vector", lambda v, gb=gb: v.tensor_copy(
                    M1SB[:, gb * 8:(gb + 1) * 8], CMB[:, 32:40]),
                    reads=["CMB"], writes=["M1SB"], tag=f"svm1{gb}")
                tk.comp("vector", lambda v, gb=gb: v.tensor_copy(
                    CMBSB[:, gb * 8:(gb + 1) * 8], CMB[:, 0:8]),
                    reads=["CMB"], writes=["CMBSB"], tag=f"svc{gb}")
                tk.comp("tensor", lambda t, gb=gb: t.matmul(
                    PS[7][:, 0:8], LSTR[:, :], MSB[:, gb * 8:(gb + 1) * 8],
                    start=True, stop=True),
                    reads=["LSTR", "MSB"], writes=["PS7"], tag=f"rnk{gb}")
                tk.comp("vector", lambda v, gb=gb: v.tensor_copy(
                    RNKSB[:, gb * 8:(gb + 1) * 8], PS[7][:, 0:8]),
                    reads=["PS7"], writes=["RNKSB"], tag=f"rnke{gb}")
                tk.comp("tensor", lambda t, gb=gb: t.matmul(
                    PS[7][0:8, 8:9], MSB[:, gb * 8:(gb + 1) * 8], ONESC[:, :],
                    start=True, stop=True),
                    reads=["MSB", "ONESC"], writes=["PS7"], tag=f"cnt{gb}")
                tk.comp("vector", lambda v, gb=gb: v.tensor_copy(
                    BCNTS[:, gb:gb + 1], PS[7][0:8, 8:9]),
                    reads=["PS7"], writes=["BCNTS"], tag=f"cnte{gb}")

            for b in range(NB):
                p0_block(b)

            for b in range(NB):
                rpar = b % 2
                for c in range(NCH):
                    pb = 2 + c % 2
                    tk.comp("tensor", lambda t, b=b, c=c, pb=pb: t.matmul(
                        PS[pb][:, 0:128], XNT[:, c * TT + b * 128:c * TT + b * 128 + 128],
                        IDNB[:, :], start=True, stop=True),
                        reads=[f"XNT_{c}", "IDNB"], writes=[f"PS{pb}"],
                        tag=f"xtr{tau}_{b}_{c}")
                    tk.comp("vector", lambda v, b=b, c=c, pb=pb, rpar=rpar: v.tensor_copy(
                        XROW[:, rpar * D + c * 128:rpar * D + c * 128 + 128],
                        PS[pb][:, 0:128]),
                        reads=[f"PS{pb}"], writes=[f"XROW{rpar}"],
                        tag=f"xtre{tau}_{b}_{c}")
                tk.dma("sync", lambda s, b=b, rpar=rpar: s.dma_start(
                    out=xnb_dram.ap()[t0 + b * 128:t0 + b * 128 + 128, :],
                    in_=XROW[:, rpar * D:(rpar + 1) * D]),
                    reads=[f"XROW{rpar}"], writes=["xnb_dram"], tag=f"xrow{tau}_{b}")

        for tau in range(NTILES):
            p0_tile(tau)

        def sort_post():
            tk.comp("vector", lambda v: v.tensor_reduce(
                SRT[0:8, 0:1], BCNTS[:, :], AX, ALU.add),
                reads=["BCNTS"], writes=["SRT"], tag="sp_N")
            for j in range(4):
                tk.comp("vector", lambda v, j=j: v.tensor_scalar(
                    SRT[0:8, 1 + j:2 + j], SRT[0:8, 0:1], 512.0 * j, None, ALU.is_gt),
                    reads=["SRT"], writes=["SRT"], tag=f"sp_nt{j}")
            tk.comp("vector", lambda v: v.tensor_tensor(
                SRT[0:8, 1:2], SRT[0:8, 1:2], SRT[0:8, 2:3], ALU.add),
                reads=["SRT"], writes=["SRT"], tag="sp_nta")
            tk.comp("vector", lambda v: v.tensor_tensor(
                SRT[0:8, 3:4], SRT[0:8, 3:4], SRT[0:8, 4:5], ALU.add),
                reads=["SRT"], writes=["SRT"], tag="sp_ntb")
            tk.comp("vector", lambda v: v.tensor_tensor(
                SRT[0:8, 1:2], SRT[0:8, 1:2], SRT[0:8, 3:4], ALU.add),
                reads=["SRT"], writes=["SRT"], tag="sp_ntc")
            tk.comp("tensor", lambda t: t.matmul(
                PS[7][0:8, 0:1], LIN8[:, :], SRT[0:8, 1:2], start=True, stop=True),
                reads=["LIN8", "SRT"], writes=["PS7"], tag="sp_cum")
            tk.comp("vector", lambda v: v.tensor_copy(SRT[0:8, 5:6], PS[7][0:8, 0:1]),
                    reads=["PS7"], writes=["SRT"], tag="sp_cume")
            tk.comp("vector", lambda v: v.tensor_tensor(
                SRT[0:8, 6:7], SRT[0:8, 5:6], SRT[0:8, 1:2], ALU.subtract),
                reads=["SRT"], writes=["SRT"], tag="sp_sb0")
            tk.comp("vector", lambda v: v.tensor_scalar(
                SRT[0:8, 6:7], SRT[0:8, 6:7], 512.0, None, ALU.mult),
                reads=["SRT"], writes=["SRT"], tag="sp_sb1")
            tk.comp("vector", lambda v: v.tensor_scalar(
                BCNTS[:, 0:1], BCNTS[:, 0:1], 0.0, None, ALU.mult),
                reads=["BCNTS"], writes=["BCNTS"], tag="sp_b0") if False else None
            for gb in range(15, 0, -1):
                pass
            tk.comp("vector", lambda v: v.tensor_scalar(
                SRT[0:8, 7:8], SRT[0:8, 0:1], 0.0, None, ALU.mult),
                reads=["SRT"], writes=["SRT"], tag="sp_acc0")
            for gb in range(16):
                tk.comp("vector", lambda v, gb=gb: v.tensor_copy(
                    SRT[0:8, 8:9], BCNTS[:, gb:gb + 1]),
                    reads=["BCNTS"], writes=["SRT"], tag=f"sp_sv{gb}")
                tk.comp("vector", lambda v, gb=gb: v.tensor_tensor(
                    BCNTS[:, gb:gb + 1], SRT[0:8, 7:8], SRT[0:8, 6:7], ALU.add),
                    reads=["SRT"], writes=["BCNTS"], tag=f"sp_off{gb}")
                tk.comp("vector", lambda v, gb=gb: v.tensor_tensor(
                    SRT[0:8, 7:8], SRT[0:8, 7:8], SRT[0:8, 8:9], ALU.add),
                    reads=["SRT"], writes=["SRT"], tag=f"sp_run{gb}")
            tk.dma("sync", lambda s: s.dma_start(out=off_dram.ap(), in_=BCNTS[:, :]),
                   reads=["BCNTS"], writes=["off_dram"], tag="sp_offd")
            tk.dma("sync", lambda s: s.dma_start(out=cum_dram.ap(), in_=SRT[0:8, 5:6]),
                   reads=["SRT"], writes=["cum_dram"], tag="sp_cumd")
            tk.dma("sync", lambda s: s.dma_start(
                out=SRT[0:1, 0:8], in_=cum_dram.ap().rearrange("e one -> one e")),
                reads=["cum_dram"], writes=["SRT"], tag="sp_cumr")
            for k in range(16):
                tk.comp("vector", lambda v, k=k: v.tensor_scalar(
                    SRT[0:1, 8:16], SRT[0:1, 0:8], float(k) + 0.5, None, ALU.is_lt),
                    reads=["SRT"], writes=["SRT"], tag=f"sp_ek{k}")
                tk.comp("vector", lambda v, k=k: v.tensor_reduce(
                    EXPB[0:1, k:k + 1], SRT[0:1, 8:16], AX, ALU.add),
                    reads=["SRT"], writes=["EXPB"], tag=f"sp_ekr{k}")
            tk.comp("vector", lambda v: v.tensor_scalar(
                EXPB[0:1, 0:16], EXPB[0:1, 0:16], 7.0, None, ALU.min),
                reads=["EXPB"], writes=["EXPB"], tag="sp_clamp")
            tk.comp("tensor", lambda t: t.matmul(
                PS[7][:, 0:16], ONESR[:, :], EXPB[0:1, 0:16], start=True, stop=True),
                reads=["EXPB", "ONESR"], writes=["PS7"], tag="sp_eb")
            tk.comp("vector", lambda v: v.tensor_copy(EXPB[:, :], PS[7][:, 0:16]),
                    reads=["PS7"], writes=["EXPB"], tag="sp_ebc")
            for gb in range(16):
                tk.dma("sync", lambda s, gb=gb: s.dma_start(
                    out=SRT[0:1, 0:8], in_=off_dram.ap()[:, gb:gb + 1]
                    .rearrange("e one -> one e"),
                    allow_slow_non_contiguous=True),
                    reads=["off_dram"], writes=["SRT"], tag=f"sp_ofl{gb}")
                tk.comp("tensor", lambda t, gb=gb: t.matmul(
                    PS[7][:, 0:8], ONESR[:, :], SRT[0:1, 0:8], start=True, stop=True),
                    reads=["SRT", "ONESR"], writes=["PS7"], tag=f"sp_ofb{gb}")
                tk.comp("vector", lambda v, gb=gb: v.tensor_tensor(
                    SRT[:, 0:8], PS[7][:, 0:8], RNKSB[:, gb * 8:(gb + 1) * 8], ALU.add),
                    reads=["PS7", "RNKSB"], writes=["SRT"], tag=f"sp_sl{gb}")
                for j in range(2):
                    mk = "M1SB" if j == 0 else "MSB"
                    if j == 0:
                        tk.comp("vector", lambda v, gb=gb: v.tensor_copy(
                            SRT[:, 8:16], M1SB[:, gb * 8:(gb + 1) * 8]),
                            reads=["M1SB"], writes=["SRT"], tag=f"sp_mj{gb}_{j}")
                    else:
                        tk.comp("vector", lambda v, gb=gb: v.tensor_tensor(
                            SRT[:, 8:16], MSB[:, gb * 8:(gb + 1) * 8],
                            M1SB[:, gb * 8:(gb + 1) * 8], ALU.subtract),
                            reads=["MSB", "M1SB"], writes=["SRT"], tag=f"sp_mj{gb}_{j}")
                    tk.comp("vector", lambda v: v.tensor_tensor(
                        TMPC[:, 0:8], SRT[:, 8:16], SRT[:, 0:8], ALU.mult),
                        reads=["SRT"], writes=["TMPC"], tag=f"sp_ms{gb}_{j}")
                    tk.comp("vector", lambda v: v.tensor_reduce(
                        TOKR[:, 2:3], TMPC[:, 0:8], AX, ALU.add),
                        reads=["TMPC"], writes=["TOKR"], tag=f"sp_ix{gb}_{j}")
                    tk.comp("vector", lambda v, gb=gb: v.tensor_tensor(
                        TMPC[:, 0:8], CMBSB[:, gb * 8:(gb + 1) * 8], ESCB[:, :],
                        ALU.mult),
                        reads=["CMBSB", "ESCB"], writes=["TMPC"], tag=f"sp_ce{gb}_{j}")
                    tk.comp("vector", lambda v: v.tensor_tensor(
                        TMPC[:, 0:8], TMPC[:, 0:8], SRT[:, 8:16], ALU.mult),
                        reads=["TMPC", "SRT"], writes=["TMPC"], tag=f"sp_cm{gb}_{j}")
                    tk.comp("vector", lambda v: v.tensor_reduce(
                        TOKR[:, 1:2], TMPC[:, 0:8], AX, ALU.add),
                        reads=["TMPC"], writes=["TOKR"], tag=f"sp_cer{gb}_{j}")
                    tk.comp("vector", lambda v, gb=gb: v.tensor_scalar(
                        TOKR[:, 0:1], IOTAS[:, 0:1], 1.0, 128.0 * gb, ALU.mult, ALU.add),
                        reads=["IOTAS"], writes=["TOKR"], tag=f"sp_tk{gb}_{j}")
                    tk.comp("vector", lambda v: v.tensor_copy(IDXI[:, :], TOKR[:, 2:3]),
                            reads=["TOKR"], writes=["IDXI"], tag=f"sp_ii{gb}_{j}")
                    tk.dma("gpsimd", lambda g: g.indirect_dma_start(
                        tok_dram.ap(), bass_rust.IndirectOffsetOnAxis(ap=IDXI[:, :], axis=0),
                        TOKR[:, :], None),
                        reads=["TOKR", "IDXI"], writes=["tok_dram"],
                        tag=f"sp_sc{gb}_{j}")

        sort_post()
        if "srt" in taps:
            tk.dma("sync", lambda s: s.dma_start(out=taps["srt"], in_=tok_dram.ap()),
                   reads=["tok_dram"], tag="tapsrt")

        if "cmdbg" in taps:
            tk.dma("sync", lambda s: s.dma_start(out=taps["cmdbg"][:, 0:64],
                                                 in_=CMB[:, 0:64]),
                   reads=["CMB"], tag="tapcmdbg")
        if "kv" in taps:
            tk.dma("gpsimd", lambda g: g.dma_start(out=taps["kv"][:, 0:1024],
                                                   in_=KE[:, :]),
                   reads=["KE"], tag="tapkv1")
            tk.dma("gpsimd", lambda g: g.dma_start(out=taps["kv"][:, 1024:2052],
                                                   in_=VA[:, :]),
                   reads=["VA"], tag="tapkv2")
        tk.dma("sync", lambda s: s.dma_start(out=cc_in.ap(), in_=CTX32[:, :]),
               reads=["CTX32"], writes=["cc_in"], tag="ccout")
        if "ctx" in taps:
            tk.dma("sync", lambda s: s.dma_start(out=taps["ctx"], in_=CTX32[:, :]),
                   reads=["CTX32"], tag="tapctx")
        if "cmb" in taps:
            tk.dma("sync", lambda s: s.dma_start(out=taps["cmb"], in_=ct_dram.ap()),
                   reads=["ct_dram"], tag="tapcmb")

        p0_scope.close()
        tk.barrier("p0_to_p3")
        tk.cc(lambda g: g.collective_compute(
            "AllReduce", ALU.add, replica_groups=[list(range(NCORES))],
            ins=[cc_in.ap().opt()], outs=[cc_out.ap().opt()]),
            reads=["cc_in"], writes=["cc_out"], tag="allreduce")
        p3_scope = ExitStack()
        HFULLD = p3_scope.enter_context(nc.sbuf_tensor("HFULLD", [128, 2 * NFC * TT], BF16))
        XNSD = p3_scope.enter_context(nc.sbuf_tensor("XNSD", [128, 2 * NCH * TT], BF16))
        XG = p3_scope.enter_context(nc.sbuf_tensor("XG", [128, 2 * D], BF16))
        YTOKD = p3_scope.enter_context(nc.sbuf_tensor("YTOKD", [128, 2 * 4 * D], F32))
        TOKT = p3_scope.enter_context(nc.sbuf_tensor("TOKT", [128, 2 * 4 * 8], F32))
        TKI = p3_scope.enter_context(nc.sbuf_tensor("TKI", [128, 2 * 4], mybir.dt.int32))
        WIX = p3_scope.enter_context(nc.sbuf_tensor("WIX", [128, 16], F32))
        WIXI = p3_scope.enter_context(nc.sbuf_tensor("WIXI", [128, 16], mybir.dt.int32))
        p3b_scope = ExitStack()
        CCB = p3b_scope.enter_context(nc.sbuf_tensor("CCB", [128, 8 * 257], BF16))
        RECJ = p3b_scope.enter_context(nc.sbuf_tensor("RECJ", [128, 8], F32))

        def ctx_normalize():
            tk.dma("gpsimd", lambda g: g.dma_start(out=CCB[:, :], in_=cc_out.ap()),
                   reads=["cc_out"], writes=["CCB"], tag="ccload")
            for j in range(8):
                tk.comp("vector", lambda v, j=j: v.reciprocal(
                    RECJ[:, j:j + 1], CCB[:, j * 257 + 256:j * 257 + 257]),
                    reads=["CCB"], writes=["RECJ"], tag=f"nrec{j}")
                tk.comp("vector", lambda v, j=j: v.tensor_scalar(
                    CTX_BF[:, j * 256:(j + 1) * 256], CCB[:, j * 257:j * 257 + 256],
                    RECJ[:, j:j + 1], None, ALU.mult),
                    reads=["CCB", "RECJ"], writes=["CTX_BF"], tag=f"nmul{j}")
            if "ctxn" in taps:
                tk.dma("gpsimd", lambda g: g.dma_start(
                    out=taps["ctxn"], in_=CTX_BF[:, :]),
                    reads=["CTX_BF"], tag="tapctxn")

        def sp_prologue(k):
            kp = k % 2
            XNS = XNSD[:, kp * NCH * TT:(kp + 1) * NCH * TT]
            xk = lambda c: f"XNS{kp}_{c}"
            tkk = lambda: f"TOK{kp}"
            for b in range(4):
                bq = kp * 4 + b
                tk.dma("sync", lambda s, bq=bq, b=b: s.dma_start(
                    out=TOKT[:, bq * 8:(bq + 1) * 8],
                    in_=tok_dram.ap()[k * TT + b * 128:k * TT + b * 128 + 128, :]),
                    reads=["tok_dram"], writes=[tkk()], tag=f"tok{k}_{b}")
                tk.comp("vector", lambda v, bq=bq: v.tensor_copy(
                    TKI[:, bq:bq + 1], TOKT[:, bq * 8:bq * 8 + 1]),
                    reads=[tkk()], writes=[f"TKI{kp}"], tag=f"tki{k}_{b}")
                tk.dma("gpsimd", lambda g, b=b, bq=bq: g.indirect_dma_start(
                    XG[:, (b % 2) * D:(b % 2 + 1) * D], None, xnb_dram.ap(),
                    bass_rust.IndirectOffsetOnAxis(ap=TKI[:, bq:bq + 1], axis=0)),
                    reads=["xnb_dram", f"TKI{kp}"], writes=[f"XG{b % 2}"], tag=f"xg{k}_{b}")
                for c in range(NCH):
                    pb = 6 + c % 2
                    tk.comp("tensor", lambda t, b=b, c=c, pb=pb: t.matmul(
                        PS[pb][:, 0:128],
                        XG[:, (b % 2) * D + c * 128:(b % 2) * D + c * 128 + 128],
                        IDNB[:, :], start=True, stop=True),
                        reads=[f"XG{b % 2}", "IDNB"], writes=[f"PS{pb}"],
                        tag=f"sptr{k}_{b}_{c}")
                    tk.comp("vector", lambda v, b=b, c=c, pb=pb: v.tensor_copy(
                        XNS[:, c * TT + b * 128:c * TT + b * 128 + 128],
                        PS[pb][:, 0:128]),
                        reads=[f"PS{pb}"], writes=[xk(c)],
                        tag=f"sptre{k}_{b}_{c}")

        def sp_tile(k):
            kp = k % 2
            HFULL = HFULLD[:, kp * NFC * TT:(kp + 1) * NFC * TT]
            XNS = XNSD[:, kp * NCH * TT:(kp + 1) * NCH * TT]
            YTOK = YTOKD[:, kp * 4 * D:(kp + 1) * 4 * D]
            hk = lambda fc: f"HF{kp}_{fc}"
            xk = lambda c: f"XNS{kp}_{c}"
            yk = lambda sb: f"YT{kp}_{sb}"
            tkk = lambda: f"TOK{kp}"

            def widx(col, g, tag):
                col = col + kp * 8
                tk.comp("vector", lambda v, col=col, g=g: v.scalar_tensor_tensor(
                    WIX[:, col:col + 1], EXPB[:, k:k + 1], 512.0, IOTAS[:, g:g + 1],
                    ALU.mult, ALU.add),
                    reads=["EXPB", "IOTAS"], writes=[f"WIX{col}"], tag=tag + "a")
                tk.comp("vector", lambda v, col=col: v.tensor_copy(
                    WIXI[:, col:col + 1], WIX[:, col:col + 1]),
                    reads=[f"WIX{col}"], writes=[f"WIXI{col}"], tag=tag + "b")

            for s4 in range(4):
                w1slot = WROT[(k * 8 + s4) % 3]
                widx(s4, s4, f"wi1_{k}_{s4}")
                tk.dma("gpsimd", lambda g, s4=s4, w1slot=w1slot, kp=kp: g.indirect_dma_start(
                    WSL[w1slot][:, :], None, w1g,
                    bass_rust.IndirectOffsetOnAxis(ap=WIXI[:, kp * 8 + s4:kp * 8 + s4 + 1], axis=0),
                    oob_is_err=False),
                    reads=[f"WIXI{kp * 8 + s4}"], writes=[w1slot], tag=f"w1g{k}_{s4}")
                for fb in range(8):
                    fbg = s4 * 8 + fb
                    pb = 2 + fbg % 2
                    terms = [(lambda c=c, fb=fb, w1slot=w1slot: w_chunk(w1slot, c)[:, fb * 128:(fb + 1) * 128],
                              lambda c=c, XNS=XNS: XNS[:, c * TT:(c + 1) * TT], [xk(c)])
                             for c in range(NCH)]
                    mm_group(PS[pb][:, :], terms, f"PS{pb}", [w1slot],
                             f"sph{k}_{fbg}")
                    tk.comp("scalar", lambda sc, fbg=fbg, pb=pb, HFULL=HFULL: sc.activation(
                        HFULL[:, fbg * TT:(fbg + 1) * TT], PS[pb][:, :], AF.Silu),
                        reads=[f"PS{pb}"], writes=[hk(fbg)], tag=f"sphs{k}_{fbg}")

            for dq in range(4):
                w2slot = WROT[(k * 8 + 4 + dq) % 3]
                widx(4 + dq, dq, f"wi2_{k}_{dq}")
                tk.dma("gpsimd", lambda g, dq=dq, w2slot=w2slot, kp=kp: g.indirect_dma_start(
                    WSL[w2slot][:, :], None, w2g,
                    bass_rust.IndirectOffsetOnAxis(ap=WIXI[:, kp * 8 + 4 + dq:kp * 8 + 5 + dq], axis=0),
                    oob_is_err=False),
                    reads=[f"WIXI{kp * 8 + 4 + dq}"], writes=[w2slot], tag=f"w2g{k}_{dq}")
                for sb in range(4):
                    pb = 4 + sb % 2
                    terms = [(lambda fc=fc, sb=sb, HFULL=HFULL: HFULL[:, fc * TT + sb * 128:fc * TT + sb * 128 + 128],
                              lambda fc=fc, w2slot=w2slot: WSL[w2slot][:, fc * 256:(fc + 1) * 256],
                              [hk(fc)]) for fc in range(NFC)]
                    mm_group(PS[pb][:, 0:256], terms, f"PS{pb}", [w2slot],
                             f"spy{k}_{dq}_{sb}")
                    tk.comp("vector", lambda v, dq=dq, sb=sb, pb=pb, kp=kp, YTOK=YTOK: v.tensor_scalar(
                        YTOK[:, sb * D + dq * 256:sb * D + (dq + 1) * 256],
                        PS[pb][:, 0:256], TOKT[:, (kp * 4 + sb) * 8 + 1:(kp * 4 + sb) * 8 + 2], None,
                        ALU.mult),
                        reads=[f"PS{pb}", tkk()], writes=[yk(sb)],
                        tag=f"spye{k}_{dq}_{sb}")
            for sb in range(4):
                tk.dma("gpsimd", lambda g, sb=sb, kp=kp, YTOK=YTOK: g.indirect_dma_start(
                    ys_dram.ap(),
                    bass_rust.IndirectOffsetOnAxis(ap=TKI[:, kp * 4 + sb:kp * 4 + sb + 1], axis=0),
                    YTOK[:, sb * D:(sb + 1) * D], None, compute_op=ALU.add),
                    reads=[yk(sb), f"TKI{kp}", "ys_dram"], writes=["ys_dram"],
                    tag=f"spsc{k}_{sb}")

        sp_prologue(0)
        for k in range(16):
            if k + 1 < 16:
                sp_prologue(k + 1)
            sp_tile(k)
            if DEBUG.get("spbar"):
                tk.barrier(f"spbar{k}")
            if k == 3:
                ctx_normalize()

        if "ys" in taps:
            for zz in range(8):
                tk.dma("sync", lambda s, zz=zz: s.dma_start(
                    out=taps["ys"][zz * 256:(zz + 1) * 256, :]
                    .rearrange("(p r) c -> p (r c)", p=128),
                    in_=ZZ[:, :]), reads=["ys_dram"], tag="tapysX") if False else None
            tk.dma("sync", lambda s: s.dma_start(
                out=taps["ys"].rearrange("(a p) d -> p (a d)", p=128),
                in_=ys_dram.ap().rearrange("(a p) d -> p (a d)", p=128)),
                reads=["ys_dram"], tag="tapys") if False else None
            for b16 in range(16):
                tk.dma("sync", lambda s, b16=b16: s.dma_start(
                    out=taps["ys"][b16 * 128:(b16 + 1) * 128, :],
                    in_=ys_dram.ap()[b16 * 128:(b16 + 1) * 128, :]),
                    reads=["ys_dram"], tag=f"tapys{b16}")

        p3b_scope.close()
        p3_scope.close()
        g_scope = ExitStack()
        GBF = g_scope.enter_context(nc.sbuf_tensor("GBF", [128, NCH * T], BF16,
                                                   side="right"))
        gl_scope = ExitStack()
        YSROW = gl_scope.enter_context(nc.sbuf_tensor("YSROW", [128, 2 * D], F32))
        tk.barrier("p3_to_g")

        def g_sl(db, t0):
            return GBF[:, db * T + t0:db * T + t0 + TT]

        for tb in range(16):
            rpar = tb % 2
            tk.dma("sync", lambda s, tb=tb, rpar=rpar: s.dma_start(
                out=YSROW[:, rpar * D:(rpar + 1) * D],
                in_=ys_dram.ap()[tb * 128:(tb + 1) * 128, :]),
                reads=["ys_dram"], writes=[f"YSR{rpar}"], tag=f"gld{tb}")
            for c in range(NCH):
                pb = 6 if c % 2 == 0 else 2
                tk.comp("tensor", lambda t, tb=tb, c=c, pb=pb, rpar=rpar: t.matmul(
                    PS[pb][:, 0:128],
                    YSROW[:, rpar * D + c * 128:rpar * D + c * 128 + 128],
                    IDN[:, :], start=True, stop=True),
                    reads=[f"YSR{rpar}", "IDN"], writes=[f"PS{pb}"],
                    tag=f"gtr{tb}_{c}")
                tk.comp("scalar", lambda sc, tb=tb, c=c, pb=pb: sc.activation(
                    GBF[:, c * T + tb * 128:c * T + tb * 128 + 128],
                    PS[pb][:, 0:128], AF.Silu),
                    reads=[f"PS{pb}"], writes=[f"G{c}"], tag=f"gsil{tb}_{c}")
        if "g" in taps:
            for db in range(NCH):
                for tau in range(NTILES):
                    t0g = tau * TT
                    tk.comp("vector", lambda v, db=db, t0g=t0g: v.tensor_copy(
                        TMPC[:, :], g_sl(db, t0g)),
                        reads=[f"G{db}"], writes=["TMPC"], tag="tapgc")
                    tk.dma("sync", lambda s, db=db, t0g=t0g: s.dma_start(
                        out=taps["g"].rearrange("(c p) t -> p c t", p=128)[:, db, t0g:t0g + TT],
                        in_=TMPC[:, :]), reads=["TMPC"], tag="tapg")
        gl_scope.close()

        p2s = ExitStack()
        X2D = p2s.enter_context(nc.sbuf_tensor("X2D", [128, 2 * NCH * TT], F32))
        RES = p2s.enter_context(nc.sbuf_tensor("RES", [128, NCH * TT], F32))
        H1B = p2s.enter_context(nc.sbuf_tensor("H1B", [128, NCH * TT], BF16))
        QS = p2s.enter_context(nc.sbuf_tensor("QS", [128, NCH * TT], BF16))
        UB = p2s.enter_context(nc.sbuf_tensor("UB", [128, NFC * TT], BF16))
        BCA2 = p2s.enter_context(nc.sbuf_tensor("BCA2", [128, TT], F32))
        BCB2 = p2s.enter_context(nc.sbuf_tensor("BCB2", [128, TT], F32))
        tk.barrier("p3_to_p2")

        def p2_tile(tau):
            t0 = tau * TT
            par = tau % 2
            X2 = X2D[:, par * NCH * TT:(par + 1) * NCH * TT]
            xk = lambda c: f"X2_{par}_{c}"
            rk = lambda c: f"RES_{c}"

            def x2c(c):
                return X2[:, c * TT:(c + 1) * TT]

            def resc(c):
                return RES[:, c * TT:(c + 1) * TT]

            def h1b(c):
                return H1B[:, c * TT:(c + 1) * TT]

            def qsc(c):
                return QS[:, c * TT:(c + 1) * TT]

            def ub(fb):
                return UB[:, fb * TT:(fb + 1) * TT]

            tk.dma("sync", lambda s: s.dma_start(
                out=X2[:, :].rearrange("p (c t) -> p c t", t=TT),
                in_=xT.rearrange("(c p) t -> p c t", p=128)[:, :, t0:t0 + TT]),
                writes=[xk(c) for c in range(NCH)], tag=f"x2load{tau}")
            tk.dma("sync", lambda s: s.dma_start(
                out=ROWS[0:1, TT:5 * TT], in_=rows_dram.ap()[:, t0:t0 + TT]),
                reads=["rows_dram"], writes=["ROW1", "ROW2", "ROW3", "ROW4"],
                tag=f"rload{tau}")
            bcast(6, row(1), ["ROW1"], f"p2r0_{tau}")
            bcast(7, row(2), ["ROW2"], f"p2q0_{tau}")
            bcast(4, row(3), ["ROW3"], f"p2p01_{tau}")
            bcast(5, row(4), ["ROW4"], f"p2q01_{tau}")
            tk.comp("scalar", lambda s: s.activation(
                BCA2[:, :], PS[6][:, :], AF.Identity),
                reads=["PS6"], writes=["BCA2"], tag=f"stg_r0_{tau}")
            tk.comp("scalar", lambda s: s.activation(
                BCB2[:, :], PS[7][:, :], AF.Identity),
                reads=["PS7"], writes=["BCB2"], tag=f"stg_q0_{tau}")
            for c in range(NCH):
                tk.comp("gpsimd", lambda g, c=c: g.tensor_tensor(
                    resc(c), x2c(c), BCA2[:, :], ALU.mult),
                    reads=[xk(c), "BCA2"], writes=[rk(c)], tag=f"q0m{tau}_{c}")
                tk.comp("gpsimd", lambda g, c=c: g.tensor_tensor(
                    resc(c), resc(c), BCB2[:, :], ALU.subtract),
                    reads=[rk(c), "BCB2"], writes=[rk(c)], tag=f"q0s{tau}_{c}")
                tk.comp("vector", lambda v, c=c: v.tensor_tensor(
                    TMPC[:, :], x2c(c), PS[4][:, :], ALU.mult),
                    reads=[xk(c), "PS4"], writes=["TMPC"], tag=f"qh1m{tau}_{c}")
                tk.comp("vector", lambda v, c=c: v.tensor_tensor(
                    TMPC[:, :], TMPC[:, :], PS[5][:, :], ALU.subtract),
                    reads=["TMPC", "PS5"], writes=["TMPC"], tag=f"qh1s{tau}_{c}")
                tk.comp("vector", lambda v, c=c: v.tensor_scalar(
                    h1b(c), TMPC[:, :], G1[:, c:c + 1], B1[:, c:c + 1],
                    ALU.mult, ALU.add),
                    reads=["TMPC", "VEC"], writes=[f"H1B_{c}"], tag=f"qh1f{tau}_{c}")

            wqslot = WROT[(tau * 4) % 3]
            wload(wqslot, wq.rearrange("(c p) n -> p c n", p=128), f"wq{tau}")
            for qb in range(NCH):
                pb = 2 + qb % 2
                terms = [(lambda c=c, qb=qb: w_chunk(wqslot, c)[:, qb * 128:(qb + 1) * 128],
                          lambda c=c: h1b(c), [f"H1B_{c}"]) for c in range(NCH)]
                mm_group(PS[pb][:, :], terms, f"PS{pb}", [wqslot], f"q{tau}_{qb}")
                tk.comp("scalar", lambda s, qb=qb, pb=pb: s.activation(
                    qsc(qb), PS[pb][:, :], AF.Exp, scale=SCALE),
                    reads=[f"PS{pb}"], writes=[f"QS_{qb}"], tag=f"qe{tau}_{qb}")
            for h in range(HEADS):
                terms = [(lambda: ONESB[:, :], lambda c=c: qsc(c), [f"QS_{c}"])
                         for c in (2 * h, 2 * h + 1)]
                mm_group(PS[0][0:1, :], terms, "PS0", ["ONESB"], f"qsum{tau}_{h}")
                tk.comp("vector", lambda v: v.tensor_copy(row(0), PS[0][0:1, :]),
                        reads=["PS0"], writes=["ROW0"], tag=f"qsc{tau}_{h}")
                tk.comp("vector", lambda v: v.reciprocal(row(5), row(0)),
                        reads=["ROW0"], writes=["ROW5"], tag=f"qrec{tau}_{h}")
                pbq = 6 + h % 2
                bcast(pbq, row(5), ["ROW5"], f"qb{tau}_{h}")
                for c in (2 * h, 2 * h + 1):
                    tk.comp("vector", lambda v, c=c, pbq=pbq: v.tensor_tensor(
                        qsc(c), qsc(c), PS[pbq][:, :], ALU.mult),
                        reads=[f"QS_{c}", f"PS{pbq}"], writes=[f"QS_{c}"],
                        tag=f"qn{tau}_{c}")

            for h in range(HEADS):
                for eb in range(2):
                    ac = 2 * h + eb
                    pb = 2 + ac % 2
                    terms = [(lambda dc=dc, h=h, eb=eb: CTX_BF[:, (2 * h + dc) * 256 + eb * 128:
                                                                 (2 * h + dc) * 256 + eb * 128 + 128],
                              lambda dc=dc, h=h: qsc(2 * h + dc),
                              [f"QS_{2 * h + dc}"]) for dc in range(2)]
                    mm_group(PS[pb][:, :], terms, f"PS{pb}", ["CTX_BF"],
                             f"at{tau}_{ac}")
                    tk.comp("vector", lambda v, ac=ac, pb=pb: v.tensor_copy(
                        ub(ac), PS[pb][:, :]),
                        reads=[f"PS{pb}"], writes=[f"UB_{ac}"], tag=f"atc{tau}_{ac}")

            woslot = WROT[(tau * 4 + 1) % 3]
            wload(woslot, wo.rearrange("(c p) n -> p c n", p=128), f"wo{tau}")
            for db in range(NCH):
                pb = 4 + db % 2
                terms = [(lambda ac=ac, db=db: w_chunk(woslot, ac)[:, db * 128:(db + 1) * 128],
                          lambda ac=ac: ub(ac), [f"UB_{ac}"]) for ac in range(NCH)]
                mm_group(PS[pb][:, :], terms, f"PS{pb}", [woslot], f"wo{tau}_{db}")
                tk.comp("vector", lambda v, db=db, pb=pb: v.scalar_tensor_tensor(
                    resc(db), PS[pb][:, :], BO[:, db:db + 1], resc(db),
                    ALU.add, ALU.add),
                    reads=[f"PS{pb}", "VEC", rk(db)], writes=[rk(db)],
                    tag=f"woe{tau}_{db}")
            tap_featmajor("x2", RES, t0, [rk(c) for c in range(NCH)])

            def qsq(c):
                return QS[:, c * TT:(c + 1) * TT]

            stats8(resc, rk, qsq, lambda c: f"QS_{c}", f"l2_{tau}")
            rstd_from(5, 1, f"l2r_{tau}")
            tk.comp("vector", lambda v: v.tensor_tensor(
                row(2), row(0), row(1), ALU.mult),
                reads=["ROW0", "ROW1"], writes=["ROW2"], tag=f"l2q_{tau}")
            bcast(6, row(1), ["ROW1"], f"l2P_{tau}")
            bcast(7, row(2), ["ROW2"], f"l2Q_{tau}")
            for c in range(NCH):
                tk.comp("vector", lambda v, c=c: v.tensor_tensor(
                    TMPC[:, :], resc(c), PS[6][:, :], ALU.mult),
                    reads=[rk(c), "PS6"], writes=["TMPC"], tag=f"h2m{tau}_{c}")
                tk.comp("vector", lambda v, c=c: v.tensor_tensor(
                    TMPC[:, :], TMPC[:, :], PS[7][:, :], ALU.subtract),
                    reads=["TMPC", "PS7"], writes=["TMPC"], tag=f"h2s{tau}_{c}")
                tk.comp("vector", lambda v, c=c: v.tensor_scalar(
                    h1b(c), TMPC[:, :], G2[:, c:c + 1], B2[:, c:c + 1],
                    ALU.mult, ALU.add),
                    reads=["TMPC", "VEC"], writes=[f"H1B_{c}"], tag=f"h2f{tau}_{c}")

            for s4 in range(4):
                f0 = s4 * 1024
                wfslot = WROT[(tau * 4 + 2 + s4) % 3]
                wload(wfslot,
                      wf1.rearrange("(c p) f -> p c f", p=128)[:, :, f0:f0 + 1024],
                      f"wf1_{tau}_{s4}")
                for fb in range(8):
                    fbg = s4 * 8 + fb
                    pb = 2 + fbg % 2
                    terms = [(lambda c=c, fb=fb, wfslot=wfslot: w_chunk(wfslot, c)[:, fb * 128:(fb + 1) * 128],
                              lambda c=c: h1b(c), [f"H1B_{c}"]) for c in range(NCH)]
                    mm_group(PS[pb][:, :], terms, f"PS{pb}", [wfslot],
                             f"f1_{tau}_{fbg}")
                    tk.comp("scalar", lambda s, pb=pb, fbg=fbg: s.activation(
                        TMPC2[:, :], PS[pb][:, :], AF.Identity,
                        bias=BFF1[:, fbg:fbg + 1]),
                        reads=[f"PS{pb}", "VEC"], writes=["TMPC2"],
                        tag=f"f1b{tau}_{fbg}")
                    tk.comp("scalar", lambda s: s.activation(
                        TMPC[:, :], TMPC2[:, :], AF.Square),
                        reads=["TMPC2"], writes=["TMPC"], tag=f"f1sq{tau}_{fbg}")
                    tk.comp("vector", lambda v: v.tensor_scalar(
                        TMPC[:, :], TMPC[:, :], GELU_A * GELU_B, GELU_A,
                        ALU.mult, ALU.add),
                        reads=["TMPC"], writes=["TMPC"], tag=f"f1p{tau}_{fbg}")
                    tk.comp("vector", lambda v: v.tensor_tensor(
                        TMPC[:, :], TMPC[:, :], TMPC2[:, :], ALU.mult),
                        reads=["TMPC", "TMPC2"], writes=["TMPC"],
                        tag=f"f1i{tau}_{fbg}")
                    tk.comp("scalar", lambda s: s.activation(
                        TMPC[:, :], TMPC[:, :], AF.Tanh),
                        reads=["TMPC"], writes=["TMPC"], tag=f"f1t{tau}_{fbg}")
                    tk.comp("vector", lambda v: v.tensor_scalar(
                        TMPC[:, :], TMPC[:, :], 0.5, 0.5, ALU.mult, ALU.add),
                        reads=["TMPC"], writes=["TMPC"], tag=f"f1q{tau}_{fbg}")
                    tk.comp("vector", lambda v, fbg=fbg: v.tensor_tensor(
                        ub(fbg), TMPC[:, :], TMPC2[:, :], ALU.mult),
                        reads=["TMPC", "TMPC2"], writes=[f"UB_{fbg}"],
                        tag=f"f1u{tau}_{fbg}")

            for dbp in range(4):
                wfslot2 = WROT[(tau * 4 + dbp) % 3]
                dst = WSL[wfslot2][:, 0:NFC * 256].rearrange("p (c w) -> p c w", w=256)
                tk.dma("gpsimd", lambda g, dbp=dbp, dst=dst, wfslot2=wfslot2: g.dma_start(
                    out=dst,
                    in_=wf2.rearrange("(c p) d -> p c d", p=128)[:, :, dbp * 256:(dbp + 1) * 256]),
                    writes=[wfslot2], tag=f"wf2_{tau}_{dbp}")
                for dq in range(2):
                    db = dbp * 2 + dq
                    pb = 4 + db % 2
                    terms = [(lambda fc=fc, dq=dq, wfslot2=wfslot2:
                              WSL[wfslot2][:, fc * 256 + dq * 128:fc * 256 + dq * 128 + 128],
                              lambda fc=fc: ub(fc), [f"UB_{fc}"]) for fc in range(NFC)]
                    mm_group(PS[pb][:, :], terms, f"PS{pb}", [wfslot2],
                             f"f2_{tau}_{db}")
                    tk.comp("vector", lambda v, db=db, pb=pb: v.scalar_tensor_tensor(
                        resc(db), PS[pb][:, :], BF2[:, db:db + 1], resc(db),
                        ALU.add, ALU.add),
                        reads=[f"PS{pb}", "VEC", rk(db)], writes=[rk(db)],
                        tag=f"f2e{tau}_{db}")

            for c in range(NCH):
                tk.comp("vector", lambda v, c=c: v.tensor_copy(qsc(c), resc(c)),
                        reads=[rk(c)], writes=[f"QS_{c}"], tag=f"x3c{tau}_{c}")
            wdslot = WROT[(tau * 4 + 3) % 3]
            wload(wdslot, wd.rearrange("(c p) n -> p c n", p=128), f"wd{tau}")
            for db in range(NCH):
                pb = 2 + db % 2
                terms = [(lambda c=c, db=db: w_chunk(wdslot, c)[:, db * 128:(db + 1) * 128],
                          lambda c=c: qsc(c), [f"QS_{c}"]) for c in range(NCH)]
                mm_group(PS[pb][:, :], terms, f"PS{pb}", [wdslot],
                         f"sd{tau}_{db}")
                if "s" in taps:
                    tk.comp("vector", lambda v, db=db, pb=pb: v.tensor_scalar(
                        TMPC[:, :], PS[pb][:, :], BD[:, db:db + 1], None, ALU.add),
                        reads=[f"PS{pb}", "VEC"], writes=["TMPC"], tag="taps1")
                    tk.dma("sync", lambda s, db=db: s.dma_start(
                        out=taps["s"].rearrange("(c p) t -> p c t", p=128)[:, db, t0:t0 + TT],
                        in_=TMPC[:, :]), reads=["TMPC"], tag="taps")
                tk.comp("scalar", lambda sc, db=db, pb=pb: sc.activation(
                    TMPC2[:, :], PS[pb][:, :], AF.Identity, bias=BD[:, db:db + 1]),
                    reads=[f"PS{pb}", "VEC"], writes=["TMPC2"], tag=f"sb{tau}_{db}")
                tk.comp("vector", lambda v, db=db: v.tensor_tensor(
                    TMPC[:, :], TMPC2[:, :], g_sl(db, t0), ALU.mult),
                    reads=["TMPC2", f"G{db}"], writes=["TMPC"], tag=f"sg{tau}_{db}")
                tk.comp("gpsimd", lambda g, db=db: g.tensor_tensor(
                    x2c(db), x2c(db), TMPC[:, :], ALU.add),
                    reads=[xk(db), "TMPC"], writes=[xk(db)], tag=f"out{tau}_{db}")
            tk.dma("sync", lambda s: s.dma_start(
                out=outT.rearrange("(c p) t -> p c t", p=128)[:, :, t0:t0 + TT],
                in_=X2[:, :].rearrange("p (c t) -> p c t", t=TT)),
                reads=[xk(c) for c in range(NCH)], writes=["outT"], tag=f"store{tau}")

        for tau in range(NTILES):
            p2_tile(tau)

        p2s.close()
        g_scope.close()

        tk.barrier("final")

        tk.resolve()
        sems = {}
        for e in ENGINES:
            sems[("ev", e)] = stack.enter_context(nc.semaphore(f"ev_{e}"))
        for e in ("sync", "gpsimd"):
            for ln in range(NDMALANES):
                sems[("dma", e, ln)] = stack.enter_context(
                    nc.semaphore(f"dma_{e}_{ln}"))
        sems[("cc", "cc")] = stack.enter_context(nc.semaphore("ccsem"))
        block = stack.enter_context(nc.Block())
        tk.emit(block, sems)

    return nc, taps, tk.stats



def kernel(L0_attn_x, L0_attn_ln1_g, L0_attn_ln1_b, L0_attn_wq, L0_attn_wk,
           L0_attn_wv, L0_attn_wo, L0_attn_bo, L0_attn_ln2_g, L0_attn_ln2_b,
           L0_attn_w_ff1, L0_attn_b_ff1, L0_attn_w_ff2, L0_attn_b_ff2,
           L0_attn_w_d, L0_attn_b_d, L0_ffn_router_w, L0_ffn_w1, L0_ffn_w2,
           L0_ffn_expert_scale):
    f32 = np.float32

    def arr(a):
        return np.ascontiguousarray(np.asarray(a), dtype=f32)

    x = arr(L0_attn_x).reshape(TTOT, D)
    xTh = np.ascontiguousarray(x.T)
    w1_ = arr(L0_ffn_w1)
    w1g = np.ascontiguousarray(
        w1_.reshape(E, 4, 1024, NCH, 128).transpose(0, 1, 4, 3, 2)
    ).reshape(E * 4 * 128, NCH * 1024)
    w2_ = arr(L0_ffn_w2)
    w2g = np.ascontiguousarray(
        w2_.reshape(E, 4, 256, NFC, 128).transpose(0, 1, 4, 3, 2)
    ).reshape(E * 4 * 128, NFC * 256)
    rTh = np.ascontiguousarray(arr(L0_ffn_router_w).T)
    iotas = (np.arange(128, dtype=f32)[:, None]
             + 128.0 * np.arange(8, dtype=f32)[None, :])
    lstrict = np.tril(np.ones((128, 128), f32), -1).T
    lincl8 = np.tril(np.ones((8, 8), f32), 0).T
    eidx = np.broadcast_to(np.arange(8, dtype=f32)[None, :], (128, 8)).copy()

    def vecr(a, n):
        return np.ascontiguousarray(arr(a).reshape(n, 128).T)

    shared = {
        "wq": arr(L0_attn_wq), "wk": arr(L0_attn_wk), "wv": arr(L0_attn_wv),
        "wo": arr(L0_attn_wo), "wf1": arr(L0_attn_w_ff1), "wf2": arr(L0_attn_w_ff2),
        "wd": arr(L0_attn_w_d), "w1g": w1g, "w2g": w2g, "rT": rTh,
        "iotas": iotas, "lstrict": lstrict, "lincl8": lincl8, "eidx": eidx,
        "vg1": vecr(L0_attn_ln1_g, NCH), "vb1": vecr(L0_attn_ln1_b, NCH),
        "vg2": vecr(L0_attn_ln2_g, NCH), "vb2": vecr(L0_attn_ln2_b, NCH),
        "vbo": vecr(L0_attn_bo, NCH), "vbf2": vecr(L0_attn_b_ff2, NCH),
        "vbd": vecr(L0_attn_b_d, NCH), "vbf1": vecr(L0_attn_b_ff1, NFC),
        "vesc": arr(L0_ffn_expert_scale).reshape(1, E),
        "idn": np.eye(128, dtype=f32),
    }
    in_maps = []
    for c in range(NCORES):
        m = dict(shared)
        m["xT"] = np.ascontiguousarray(xTh[:, c * T:(c + 1) * T])
        in_maps.append(m)

    nc, taps, stats = build_program()
    res = run_bass_kernel_spmd(nc, in_maps, core_ids=list(range(NCORES)))

    out = np.empty((TTOT, D), dtype=f32)
    for c in range(NCORES):
        out[c * T:(c + 1) * T, :] = np.asarray(res.results[c]["outT"]).T
    if taps:
        kernel.tap_results = [
            {k: np.asarray(res.results[c]["tap_" + k]) for k in taps}
            for c in range(NCORES)]
        kernel.build_stats = stats
    return out.reshape(1, TTOT, D)


# version 07 (final)
# speedup vs baseline: 1.0289x; 1.0018x over previous
import numpy as np
import bass_rust
from contextlib import ExitStack
import concourse.bass as bass
import concourse.mybir as mybir
from concourse.bass_utils import run_bass_kernel_spmd

F32 = mybir.dt.float32
BF16 = mybir.dt.bfloat16
AF = mybir.ActivationFunctionType
ALU = mybir.AluOpType
AX = mybir.AxisListType.X

NCORES = 8
D = 1024
TTOT = 16384
T = TTOT // NCORES
TT = 512
NTILES = T // TT
NB = TT // 128
HEADS = 4
DH = 256
FF = 4096
E = 8
EPS = 1e-5
SCALE = DH ** (-0.25)
NCH = D // 128
NFC = FF // 128
GELU_A = 0.7978845608028654
GELU_B = 0.044715

DEBUG = {}

ENGINES = ("sync", "gpsimd", "scalar", "vector", "tensor")


NDMALANES = 8


class Op:
    __slots__ = ("idx", "eng", "fn", "reads", "writes", "kind", "tag",
                 "needs_event", "event_no", "waits", "bar_state", "lane",
                 "lane_req")

    def __init__(self, idx, eng, fn, reads, writes, kind, tag):
        self.idx = idx
        self.eng = eng
        self.fn = fn
        self.reads = tuple(reads)
        self.writes = tuple(writes)
        self.kind = kind
        self.tag = tag
        self.needs_event = False
        self.event_no = None
        self.waits = []
        self.lane = None
        self.lane_req = None


class TK:

    def __init__(self):
        self.ops = []

    def comp(self, eng, fn, reads=(), writes=(), tag=""):
        op = Op(len(self.ops), eng, fn, reads, writes, "comp", tag)
        self.ops.append(op)
        return op

    def dma(self, queue, fn, reads=(), writes=(), tag="", lane=None):
        op = Op(len(self.ops), queue, fn, reads, writes, "dma", tag)
        op.lane_req = lane
        self.ops.append(op)
        return op

    def cc(self, fn, reads=(), writes=(), tag=""):
        op = Op(len(self.ops), "gpsimd", fn, reads, writes, "cc", tag)
        self.ops.append(op)
        return op

    def barrier(self, tag="bar"):
        op = Op(len(self.ops), None, None, (), (), "barrier", tag)
        self.ops.append(op)
        return op

    def resolve(self):
        last_writer = {}
        readers = {}
        edges = []
        last_op_per_eng = {}
        for op in self.ops:
            if op.kind == "barrier":
                for e, lop in last_op_per_eng.items():
                    if lop.kind == "comp":
                        lop.needs_event = True
                continue
            deps = set()
            for r in op.reads:
                w = last_writer.get(r)
                if w is not None:
                    deps.add(w)
            for w in op.writes:
                lw = last_writer.get(w)
                if lw is not None:
                    deps.add(lw)
                for rd in readers.get(w, {}).values():
                    deps.add(rd)
            for d in deps:
                if d is op:
                    continue
                if d.eng == op.eng == "tensor" and d.kind == "comp" and op.kind == "comp":
                    continue
                edges.append((d, op))
            for r in op.reads:
                readers.setdefault(r, {})[op.eng] = op
            for w in op.writes:
                last_writer[w] = op
                readers[w] = {}
            last_op_per_eng[op.eng] = op
        for p, c in edges:
            if p.kind == "comp":
                p.needs_event = True
        ev_count = {e: 0 for e in ENGINES}
        dma_lane_count = {}
        dma_queue_idx = {e: 0 for e in ENGINES}
        cc_count = 0
        chan_of, val_of = {}, {}
        reserved = {}
        for op in self.ops:
            if op.kind == "dma" and op.lane_req is not None:
                reserved.setdefault(op.eng, set()).add(op.lane_req)
        auto_lanes = {e: [ln for ln in range(NDMALANES)
                          if ln not in reserved.get(e, set())] for e in ENGINES}
        for op in self.ops:
            if op.kind == "comp":
                if op.needs_event:
                    ev_count[op.eng] += 1
                    op.event_no = ev_count[op.eng]
                chan_of[op] = ("ev", op.eng)
                val_of[op] = op.event_no
            elif op.kind == "dma":
                if op.lane_req is not None:
                    lane = op.lane_req
                else:
                    pool = auto_lanes[op.eng]
                    lane = pool[dma_queue_idx[op.eng] % len(pool)]
                dma_queue_idx[op.eng] += 1
                op.lane = lane
                key = (op.eng, lane)
                dma_lane_count[key] = dma_lane_count.get(key, 0) + 16
                op.event_no = dma_lane_count[key]
                chan_of[op] = ("dma", op.eng, lane)
                val_of[op] = op.event_no
            elif op.kind == "cc":
                cc_count += 1
                op.event_no = cc_count
                chan_of[op] = ("cc", "cc")
                val_of[op] = op.event_no
        edges_by_consumer = {}
        for p, c in edges:
            edges_by_consumer.setdefault(c.idx, []).append(p)
        frontier = {e: {} for e in ENGINES}
        run_ev = {e: 0 for e in ENGINES}
        run_dma = {}
        run_cc = 0
        for op in self.ops:
            if op.kind == "barrier":
                op.bar_state = (dict(run_ev), dict(run_dma), run_cc)
                continue
            need = {}
            for p in edges_by_consumer.get(op.idx, ()):
                v = val_of[p]
                if v is None:
                    continue
                ch = chan_of[p]
                if need.get(ch, 0) < v:
                    need[ch] = v
            if op.kind == "dma" and op.event_no > 16:
                ch = chan_of[op]
                if need.get(ch, 0) < op.event_no - 16:
                    need[ch] = op.event_no - 16
            fr = frontier[op.eng]
            for ch, v in sorted(need.items(), key=lambda kv: str(kv[0])):
                if fr.get(ch, 0) >= v:
                    continue
                op.waits.append((ch, v))
                fr[ch] = v
            if op.kind == "comp":
                if op.needs_event:
                    run_ev[op.eng] = op.event_no
            elif op.kind == "dma":
                run_dma[chan_of[op]] = op.event_no
            elif op.kind == "cc":
                run_cc = op.event_no
        self.stats = dict(n_ops=len(self.ops), ev=dict(ev_count),
                          dma=dict(dma_queue_idx), edges=len(edges))

    def emit(self, block, sems):
        per_eng = {e: [] for e in ENGINES}
        for op in self.ops:
            if op.kind == "barrier":
                ev, dma, cc = op.bar_state
                for e in ENGINES:
                    items = []
                    for e2 in ENGINES:
                        if ev[e2] > 0:
                            items.append((("ev", e2), ev[e2]))
                    for ch, v in dma.items():
                        items.append((ch, v))
                    if cc > 0:
                        items.append((("cc", "cc"), cc))
                    per_eng[e].append(("waits", items))
                continue
            per_eng[op.eng].append(("op", op))

        def emitter(eng_name, eng):
            frontier = {}
            for kind, payload in per_eng[eng_name]:
                if kind == "waits":
                    for ch, v in payload:
                        if frontier.get(ch, 0) >= v:
                            continue
                        eng.wait_ge(sems[ch], v)
                        frontier[ch] = v
                    continue
                op = payload
                for ch, v in op.waits:
                    if frontier.get(ch, 0) >= v:
                        continue
                    eng.wait_ge(sems[ch], v)
                    frontier[ch] = v
                inst = op.fn(eng)
                if op.kind == "dma":
                    inst.then_inc(sems[("dma", op.eng, op.lane)], 16)
                elif op.kind == "cc":
                    inst.then_inc(sems[("cc", "cc")])
                elif op.needs_event:
                    inst.then_inc(sems[("ev", op.eng)], 1)

        @block.sync
        def _(eng):
            emitter("sync", eng)

        @block.gpsimd
        def _(eng):
            emitter("gpsimd", eng)

        @block.scalar
        def _(eng):
            emitter("scalar", eng)

        @block.vector
        def _(eng):
            emitter("vector", eng)

        @block.tensor
        def _(eng):
            emitter("tensor", eng)



def build_program():
    nc = bass.Bass("TRN2", target_bir_lowering=False)
    tk = TK()

    xT = nc.dram_tensor("xT", [D, T], F32, kind="ExternalInput").ap()
    wq = nc.dram_tensor("wq", [D, D], F32, kind="ExternalInput").ap()
    wk = nc.dram_tensor("wk", [D, D], F32, kind="ExternalInput").ap()
    wv = nc.dram_tensor("wv", [D, D], F32, kind="ExternalInput").ap()
    wo = nc.dram_tensor("wo", [D, D], F32, kind="ExternalInput").ap()
    wf1 = nc.dram_tensor("wf1", [D, FF], F32, kind="ExternalInput").ap()
    wf2 = nc.dram_tensor("wf2", [FF, D], F32, kind="ExternalInput").ap()
    wd = nc.dram_tensor("wd", [D, D], F32, kind="ExternalInput").ap()
    w1g = nc.dram_tensor("w1g", [E * 4 * 128, NCH * 1024], F32, kind="ExternalInput").ap()
    w2g = nc.dram_tensor("w2g", [E * 4 * 128, NFC * 256], F32, kind="ExternalInput").ap()
    iotas_in = nc.dram_tensor("iotas", [128, 8], F32, kind="ExternalInput").ap()
    lstr_in = nc.dram_tensor("lstrict", [128, 128], F32, kind="ExternalInput").ap()
    lincl_in = nc.dram_tensor("lincl8", [8, 8], F32, kind="ExternalInput").ap()
    eidx_in = nc.dram_tensor("eidx", [128, 8], F32, kind="ExternalInput").ap()
    rT = nc.dram_tensor("rT", [D, E], F32, kind="ExternalInput").ap()
    vg1 = nc.dram_tensor("vg1", [128, NCH], F32, kind="ExternalInput").ap()
    vb1 = nc.dram_tensor("vb1", [128, NCH], F32, kind="ExternalInput").ap()
    vg2 = nc.dram_tensor("vg2", [128, NCH], F32, kind="ExternalInput").ap()
    vb2 = nc.dram_tensor("vb2", [128, NCH], F32, kind="ExternalInput").ap()
    vbo = nc.dram_tensor("vbo", [128, NCH], F32, kind="ExternalInput").ap()
    vbf2 = nc.dram_tensor("vbf2", [128, NCH], F32, kind="ExternalInput").ap()
    vbd = nc.dram_tensor("vbd", [128, NCH], F32, kind="ExternalInput").ap()
    vbf1 = nc.dram_tensor("vbf1", [128, NFC], F32, kind="ExternalInput").ap()
    vesc = nc.dram_tensor("vesc", [1, E], F32, kind="ExternalInput").ap()
    idn = nc.dram_tensor("idn", [128, 128], F32, kind="ExternalInput").ap()
    outT = nc.dram_tensor("outT", [D, T], F32, kind="ExternalOutput").ap()

    ct_dram = nc.dram_tensor("ct_dram", [E, T], F32)
    rows_dram = nc.dram_tensor("rows_dram", [4, T], F32)
    xnb_dram = nc.dram_tensor("xnb_dram", [T, D], BF16)
    tok_dram = nc.dram_tensor("tok_dram", [16 * TT, 8], F32)
    ys_dram = nc.dram_tensor("ys_dram", [T, D], F32)
    off_dram = nc.dram_tensor("off_dram", [8, 16], F32)
    cum_dram = nc.dram_tensor("cum_dram", [8, 1], F32)
    cc_in = nc.dram_tensor("cc_in", [128, 8 * 257], F32)
    cc_out = nc.dram_tensor("cc_out", [128, 8 * 257], F32)

    taps = {}
    for name, shape in [("xn32", [D, T]), ("ctx", [128, 8 * 257]), ("cmb", [E, T]),
                        ("ctxn", [128, 8 * 256]), ("g", [D, T]), ("h1", [D, T]),
                        ("x2", [D, T]), ("s", [D, T]), ("moe", [D, T]),
                        ("cmdbg", [128, 64]), ("kv", [128, 1024 + 1028]),
                        ("hsb", [128, 2 * 8 * TT]), ("srt", [16 * TT, 8]),
                        ("ys", [T, D])]:
        if DEBUG.get(name):
            taps[name] = nc.dram_tensor("tap_" + name, shape, F32, kind="ExternalOutput").ap()

    stack = ExitStack()
    with stack:
        VEC = stack.enter_context(nc.sbuf_tensor("VEC", [128, 7 * NCH + NFC], F32))
        ESC = stack.enter_context(nc.sbuf_tensor("ESCb", [1, E], F32))
        ONESC = stack.enter_context(nc.sbuf_tensor("ONESC", [128, 1], F32))
        ONESB = stack.enter_context(nc.sbuf_tensor("ONESB", [128, 1], BF16))
        ONESR = stack.enter_context(nc.sbuf_tensor("ONESR", [1, 128], F32))
        EPSR = stack.enter_context(nc.sbuf_tensor("EPSR", [1, 1], F32))
        IDN = stack.enter_context(nc.sbuf_tensor("IDN", [128, 128], F32))
        CTX_BF = stack.enter_context(nc.sbuf_tensor("CTX_BF", [128, 8 * 256], BF16))
        WSL = {
            "WA": stack.enter_context(nc.sbuf_tensor("WA", [128, 8 * 1024], BF16)),
            "WB": stack.enter_context(nc.sbuf_tensor("WB", [128, 8 * 1024], BF16)),
            "WC": stack.enter_context(nc.sbuf_tensor("WC", [128, 8 * 1024], BF16)),
        }
        WROT = ["WA", "WB", "WC"]
        ROWS = stack.enter_context(nc.sbuf_tensor("ROWS", [1, 6 * TT], F32))
        CMB = stack.enter_context(nc.sbuf_tensor("CMBt", [128, 64], F32))
        EXPB = stack.enter_context(nc.sbuf_tensor("EXPB", [128, 16], F32))
        ESCB = stack.enter_context(nc.sbuf_tensor("ESCB", [128, 8], F32))
        IOTAS = stack.enter_context(nc.sbuf_tensor("IOTAS", [128, 8], F32))
        EIDX = stack.enter_context(nc.sbuf_tensor("EIDXs", [128, 8], F32))
        IDNB = stack.enter_context(nc.sbuf_tensor("IDNB", [128, 128], BF16))
        CTS = stack.enter_context(nc.sbuf_tensor("CTS", [8, 128], F32))
        TMPC = stack.enter_context(nc.sbuf_tensor("TMPC", [128, TT], F32))
        TMPC2 = stack.enter_context(nc.sbuf_tensor("TMPC2", [128, TT], F32))
        PS = [stack.enter_context(nc.psum_tensor(f"PS{i}", [128, 512], F32))
              for i in range(8)]

        def row(i):
            return ROWS[:, i * TT:(i + 1) * TT]

        G1 = VEC[:, 0 * NCH:1 * NCH]
        B1 = VEC[:, 1 * NCH:2 * NCH]
        G2 = VEC[:, 2 * NCH:3 * NCH]
        B2 = VEC[:, 3 * NCH:4 * NCH]
        BO = VEC[:, 4 * NCH:5 * NCH]
        BF2 = VEC[:, 5 * NCH:6 * NCH]
        BD = VEC[:, 6 * NCH:7 * NCH]
        BFF1 = VEC[:, 7 * NCH:7 * NCH + NFC]

        tk.comp("vector", lambda v: v.memset(ONESC[:, :], 1.0), writes=["ONESC"])
        tk.comp("vector", lambda v: v.memset(ONESB[:, :], 1.0), writes=["ONESB"])
        tk.comp("vector", lambda v: v.memset(ONESR[:, :], 1.0), writes=["ONESR"])
        tk.comp("vector", lambda v: v.memset(EPSR[:, :], EPS), writes=["EPSR"])
        tk.dma("sync", lambda s: s.dma_start(out=IDN[:, :], in_=idn), writes=["IDN"])
        for i, src in enumerate([vg1, vb1, vg2, vb2, vbo, vbf2, vbd]):
            tk.dma("sync", lambda s, i=i, src=src: s.dma_start(
                out=VEC[:, i * NCH:(i + 1) * NCH], in_=src), writes=["VEC"])
        tk.dma("sync", lambda s: s.dma_start(out=VEC[:, 7 * NCH:7 * NCH + NFC], in_=vbf1),
               writes=["VEC"])
        tk.dma("sync", lambda s: s.dma_start(out=ESC[:, :], in_=vesc), writes=["ESC"])
        tk.dma("sync", lambda s: s.dma_start(out=IOTAS[:, :], in_=iotas_in), writes=["IOTAS"])
        tk.dma("sync", lambda s: s.dma_start(out=EIDX[:, :], in_=eidx_in), writes=["EIDX"])
        tk.comp("scalar", lambda sc: sc.activation(IDNB[:, :], IDN[:, :], AF.Identity),
                reads=["IDN"], writes=["IDNB"], tag="idnb")
        tk.comp("tensor", lambda t: t.matmul(PS[0][:, 0:8], ONESR[:, :], ESC[:, :],
                                             start=True, stop=True),
                reads=["ONESR", "ESC"], writes=["PS0"], tag="escb_mm")
        tk.comp("vector", lambda v: v.tensor_copy(ESCB[:, :], PS[0][:, 0:8]),
                reads=["PS0"], writes=["ESCB"], tag="escb_cp")

        def wload(slot, src_ap3, tag, nchunks=8, width=1024):
            dst = WSL[slot][:, 0:nchunks * width].rearrange("p (c w) -> p c w", w=width)
            tk.dma("gpsimd", lambda g: g.dma_start(out=dst, in_=src_ap3),
                   writes=[slot], tag=tag)

        def w_chunk(slot, c, width=1024):
            return WSL[slot][:, c * width:(c + 1) * width]

        def mm_group(bankslice, terms, psname, reads, tag):
            nt = len(terms)
            for i, term in enumerate(terms):
                lf, rf = term[0], term[1]
                extra = list(term[2]) if len(term) > 2 else []
                tk.comp("tensor",
                        lambda t, lf=lf, rf=rf, i=i: t.matmul(
                            bankslice, lf(), rf(), start=(i == 0), stop=(i == nt - 1)),
                        reads=list(reads) + extra, writes=[psname], tag=tag)

        def bcast(psbank, src_row, src_reads, tag):
            tk.comp("tensor", lambda t: t.matmul(
                PS[psbank][:, :], ONESR[:, :], src_row, start=True, stop=True),
                reads=list(src_reads) + ["ONESR"], writes=[f"PS{psbank}"], tag=tag)

        def stats8(xf32_fn, src_key_fn, sq_fn, sq_key_fn, tag):
            terms = [(lambda c=c: ONESC[:, :], lambda c=c: xf32_fn(c), [src_key_fn(c)])
                     for c in range(NCH)]
            mm_group(PS[0][0:1, :], terms, "PS0", ["ONESC"], tag + "_sumx")
            for c in range(NCH):
                tk.comp("scalar", lambda s, c=c: s.activation(
                    sq_fn(c), xf32_fn(c), AF.Square),
                    reads=[src_key_fn(c)], writes=[sq_key_fn(c)], tag=tag + "_sq")
            terms = [(lambda c=c: ONESB[:, :], lambda c=c: sq_fn(c), [sq_key_fn(c)])
                     for c in range(NCH)]
            mm_group(PS[1][0:1, :], terms, "PS1", ["ONESB"], tag + "_sumsq")
            tk.comp("vector", lambda v: v.tensor_scalar(
                row(0), PS[0][0:1, :], 1.0 / D, None, ALU.mult),
                reads=["PS0"], writes=["ROW0"], tag=tag + "_mu")
            tk.comp("vector", lambda v: v.tensor_scalar(
                row(5), PS[1][0:1, :], 1.0 / D, None, ALU.mult),
                reads=["PS1"], writes=["ROW5"], tag=tag + "_msq")
            tk.comp("vector", lambda v: v.tensor_tensor(
                row(4), row(0), row(0), ALU.mult),
                reads=["ROW0"], writes=["ROW4"], tag=tag + "_mu2")
            tk.comp("vector", lambda v: v.tensor_tensor(
                row(5), row(5), row(4), ALU.subtract),
                reads=["ROW5", "ROW4"], writes=["ROW5"], tag=tag + "_var")

        def rstd_from(var_slot, out_slot, tag):
            tk.comp("scalar", lambda s: s.activation(
                row(4), row(var_slot), AF.Ln, bias=EPSR[0:1, 0:1]),
                reads=[f"ROW{var_slot}", "EPSR"], writes=["ROW4"], tag=tag + "_ln")
            tk.comp("scalar", lambda s: s.activation(
                row(out_slot), row(4), AF.Exp, scale=-0.5),
                reads=["ROW4"], writes=[f"ROW{out_slot}"], tag=tag + "_rstd")

        def ln01_rows(tag):
            rstd_from(5, 1, tag + "_r0")
            tk.comp("vector", lambda v: v.tensor_tensor(
                row(2), row(0), row(1), ALU.mult),
                reads=["ROW0", "ROW1"], writes=["ROW2"], tag=tag + "_Q0")
            tk.comp("vector", lambda v: v.tensor_tensor(
                row(3), row(5), row(1), ALU.mult),
                reads=["ROW5", "ROW1"], writes=["ROW3"], tag=tag + "_v1a")
            tk.comp("vector", lambda v: v.tensor_tensor(
                row(3), row(3), row(1), ALU.mult),
                reads=["ROW3", "ROW1"], writes=["ROW3"], tag=tag + "_v1b")
            rstd_from(3, 3, tag + "_r1")
            tk.comp("vector", lambda v: v.tensor_tensor(
                row(3), row(3), row(1), ALU.mult),
                reads=["ROW3", "ROW1"], writes=["ROW3"], tag=tag + "_P01")
            tk.comp("vector", lambda v: v.tensor_tensor(
                row(4), row(0), row(3), ALU.mult),
                reads=["ROW0", "ROW3"], writes=["ROW4"], tag=tag + "_Q01")

        def tap_featmajor(name, src_tensor, t0, reads, f32_chunks=NCH):
            if name not in taps:
                return
            tk.dma("sync", lambda s: s.dma_start(
                out=taps[name].rearrange("(c p) t -> p c t", p=128)[:, :, t0:t0 + TT],
                in_=src_tensor[:, 0:f32_chunks * TT].rearrange("p (c t) -> p c t", t=TT)),
                reads=list(reads), tag="tap_" + name)

        p0_scope = ExitStack()
        XBD = p0_scope.enter_context(nc.sbuf_tensor("XBD", [128, 2 * NCH * TT], F32))
        XN32 = p0_scope.enter_context(nc.sbuf_tensor("XN32", [128, NCH * TT], F32))
        H1P = p0_scope.enter_context(nc.sbuf_tensor("H1P", [128, NCH * TT], BF16))
        XSQ = p0_scope.enter_context(nc.sbuf_tensor("XSQ", [128, NCH * TT], BF16))
        KE = p0_scope.enter_context(nc.sbuf_tensor("KE", [128, 1024], BF16))
        VA = p0_scope.enter_context(nc.sbuf_tensor("VA", [128, 4 * 257], BF16))
        CTX32 = p0_scope.enter_context(nc.sbuf_tensor("CTX32", [128, 8 * 257], F32))
        RTB = p0_scope.enter_context(nc.sbuf_tensor("RTB", [128, NCH * E], F32))
        BCA = p0_scope.enter_context(nc.sbuf_tensor("BCA", [128, TT], F32))
        BCB = p0_scope.enter_context(nc.sbuf_tensor("BCB", [128, TT], F32))
        XNT = p0_scope.enter_context(nc.sbuf_tensor("XNT", [128, NCH * TT], BF16))
        XROW = p0_scope.enter_context(nc.sbuf_tensor("XROW", [128, 2 * D], BF16))
        LSTR = p0_scope.enter_context(nc.sbuf_tensor("LSTR", [128, 128], F32))
        LIN8 = p0_scope.enter_context(nc.sbuf_tensor("LIN8", [8, 8], F32))
        BCNTS = p0_scope.enter_context(nc.sbuf_tensor("BCNTS", [8, 16], F32))
        RNKSB = p0_scope.enter_context(nc.sbuf_tensor("RNKSB", [128, 16 * 8], F32))
        MSB = p0_scope.enter_context(nc.sbuf_tensor("MSB", [128, 16 * 8], F32))
        M1SB = p0_scope.enter_context(nc.sbuf_tensor("M1SB", [128, 16 * 8], F32))
        CMBSB = p0_scope.enter_context(nc.sbuf_tensor("CMBSB", [128, 16 * 8], F32))
        SRT = p0_scope.enter_context(nc.sbuf_tensor("SRT", [128, 16], F32))
        TOKR = p0_scope.enter_context(nc.sbuf_tensor("TOKR", [128, 8], F32))
        IDXI = p0_scope.enter_context(nc.sbuf_tensor("IDXI", [128, 1], mybir.dt.int32))
        ZZ = p0_scope.enter_context(nc.sbuf_tensor("ZZ", [128, 2048], F32))

        tk.comp("vector", lambda v: v.memset(VA[:, :], 1.0), writes=["VA"])
        tk.comp("vector", lambda v: v.memset(CTX32[:, :], 0.0), writes=["CTX32"])
        tk.comp("vector", lambda v: v.memset(ZZ[:, :], 0.0), writes=["ZZ"])
        tk.comp("vector", lambda v: v.memset(TOKR[:, :], 0.0), writes=["TOKR"])
        tk.dma("sync", lambda s: s.dma_start(out=LSTR[:, :], in_=lstr_in), writes=["LSTR"])
        tk.dma("sync", lambda s: s.dma_start(out=LIN8[:, :], in_=lincl_in), writes=["LIN8"])
        tk.dma("sync", lambda s: s.dma_start(
            out=tok_dram.ap().rearrange("(p r) c -> p (r c)", p=128), in_=ZZ[:, 0:512]),
            reads=["ZZ"], writes=["tok_dram"], tag="tokzero")
        for zz in range(8):
            tk.dma("sync", lambda s, zz=zz: s.dma_start(
                out=ys_dram.ap()[zz * 256:(zz + 1) * 256, :]
                .rearrange("(p r) c -> p (r c)", p=128), in_=ZZ[:, :]),
                reads=["ZZ"], writes=["ys_dram"], tag=f"yszero{zz}")

        wload("WA", wk.rearrange("(c p) n -> p c n", p=128), "wk")
        wload("WB", wv.rearrange("(c p) n -> p c n", p=128), "wv")
        tk.dma("sync", lambda s: s.dma_start(
            out=RTB[:, :].rearrange("p (c e) -> p c e", e=E),
            in_=rT.rearrange("(c p) e -> p c e", p=128)), writes=["RTB"], tag="rt")

        def combine_block(tau, b):
            def V(fn, reads, writes, tag):
                tk.comp("vector", fn, reads=reads, writes=writes, tag=f"{tag}{tau}_{b}")

            V(lambda v: v.tensor_reduce(CMB[:, 8:9], PS[4][:, 0:E], AX, ALU.max),
              ["PS4"], ["CMB"], "cm_max")
            V(lambda v: v.tensor_scalar(CMB[:, 0:E], PS[4][:, 0:E], CMB[:, 8:9],
                                        None, ALU.subtract),
              ["PS4", "CMB"], ["CMB"], "cm_sh")
            tk.comp("scalar", lambda s: s.activation(CMB[:, 0:E], CMB[:, 0:E], AF.Exp),
                    reads=["CMB"], writes=["CMB"], tag=f"cm_exp{tau}_{b}")
            V(lambda v: v.tensor_reduce(CMB[:, 9:10], CMB[:, 0:E], AX, ALU.add),
              ["CMB"], ["CMB"], "cm_sum")
            V(lambda v: v.reciprocal(CMB[:, 10:11], CMB[:, 9:10]),
              ["CMB"], ["CMB"], "cm_rcp")
            V(lambda v: v.tensor_scalar(CMB[:, 0:E], CMB[:, 0:E], CMB[:, 10:11],
                                        None, ALU.mult),
              ["CMB"], ["CMB"], "cm_p")
            V(lambda v: v.tensor_reduce(CMB[:, 11:12], CMB[:, 0:E], AX, ALU.max),
              ["CMB"], ["CMB"], "cm_m1")
            V(lambda v: v.tensor_scalar(CMB[:, 32:32 + E], CMB[:, 0:E], CMB[:, 11:12],
                                        None, ALU.is_ge),
              ["CMB"], ["CMB"], "cm_g1")
            V(lambda v: v.tensor_scalar(CMB[:, 40:40 + E], CMB[:, 32:32 + E], 2.0,
                                        None, ALU.mult),
              ["CMB"], ["CMB"], "cm_g2")
            V(lambda v: v.tensor_tensor(CMB[:, 24:24 + E], CMB[:, 0:E],
                                        CMB[:, 40:40 + E], ALU.subtract),
              ["CMB"], ["CMB"], "cm_q8")
            V(lambda v: v.tensor_reduce(CMB[:, 12:13], CMB[:, 24:24 + E], AX, ALU.max),
              ["CMB"], ["CMB"], "cm_m2")
            V(lambda v: v.tensor_scalar(CMB[:, 16:16 + E], CMB[:, 0:E], CMB[:, 12:13],
                                        None, ALU.is_ge),
              ["CMB"], ["CMB"], "cm_msk")
            tk.comp("scalar", lambda s: s.activation(CMB[:, 0:E], CMB[:, 0:E], AF.Exp),
                    reads=["CMB"], writes=["CMB"], tag=f"cm_ep{tau}_{b}")
            V(lambda v: v.tensor_tensor(CMB[:, 0:E], CMB[:, 0:E], CMB[:, 16:16 + E],
                                        ALU.mult),
              ["CMB"], ["CMB"], "cm_em")
            V(lambda v: v.tensor_reduce(CMB[:, 13:14], CMB[:, 0:E], AX, ALU.add),
              ["CMB"], ["CMB"], "cm_ds")
            V(lambda v: v.reciprocal(CMB[:, 14:15], CMB[:, 13:14]),
              ["CMB"], ["CMB"], "cm_dr")
            V(lambda v: v.tensor_scalar(CMB[:, 0:E], CMB[:, 0:E], CMB[:, 14:15],
                                        None, ALU.mult),
              ["CMB"], ["CMB"], "cm_c")
            tk.comp("tensor", lambda t: t.matmul(
                PS[5][0:E, 0:128], CMB[:, 0:E], IDN[:, :], start=True, stop=True),
                reads=["CMB", "IDN"], writes=["PS5"], tag=f"cm_tr{tau}_{b}")
            V(lambda v: v.tensor_copy(CTS[:, :], PS[5][0:E, 0:128]),
              ["PS5"], ["CTS"], "cm_ct")

        def p0_tile(tau):
            t0 = tau * TT
            par = tau % 2
            xbk = f"XB{par}"
            XBt = XBD[:, par * NCH * TT:(par + 1) * NCH * TT]

            def xb(c):
                return XBt[:, c * TT:(c + 1) * TT]

            def xn32(c):
                return XN32[:, c * TT:(c + 1) * TT]

            def h1c(c):
                return H1P[:, c * TT:(c + 1) * TT]

            def xsq(c):
                return XSQ[:, c * TT:(c + 1) * TT]

            tk.dma("sync", lambda s: s.dma_start(
                out=XBt.rearrange("p (c t) -> p c t", t=TT),
                in_=xT.rearrange("(c p) t -> p c t", p=128)[:, :, t0:t0 + TT]),
                writes=[xbk], tag=f"xload{tau}")

            stats8(xb, lambda c: xbk, xsq, lambda c: f"XSQ_{c}", f"ln0_{tau}")
            ln01_rows(f"ln01_{tau}")
            tk.dma("sync", lambda s: s.dma_start(
                out=rows_dram.ap()[:, t0:t0 + TT], in_=ROWS[0:1, TT:5 * TT]),
                reads=["ROW1", "ROW2", "ROW3", "ROW4"], writes=["rows_dram"],
                tag=f"rsave{tau}")
            bcast(2, row(1), ["ROW1"], f"b_r0_{tau}")
            bcast(3, row(2), ["ROW2"], f"b_q0_{tau}")
            bcast(4, row(3), ["ROW3"], f"b_p01_{tau}")
            bcast(5, row(4), ["ROW4"], f"b_q01_{tau}")
            tk.comp("scalar", lambda s: s.activation(
                BCA[:, :], PS[2][:, :], AF.Identity),
                reads=["PS2"], writes=["BCA"], tag=f"stg_r0_{tau}")
            tk.comp("scalar", lambda s: s.activation(
                BCB[:, :], PS[3][:, :], AF.Identity),
                reads=["PS3"], writes=["BCB"], tag=f"stg_q0_{tau}")
            for c in range(NCH):
                tk.comp("gpsimd", lambda g, c=c: g.tensor_tensor(
                    xn32(c), xb(c), BCA[:, :], ALU.mult),
                    reads=[xbk, "BCA"], writes=[f"XN32_{c}"], tag=f"xnm{tau}_{c}")
                tk.comp("gpsimd", lambda g, c=c: g.tensor_tensor(
                    xn32(c), xn32(c), BCB[:, :], ALU.subtract),
                    reads=[f"XN32_{c}", "BCB"], writes=[f"XN32_{c}"], tag=f"xns{tau}_{c}")
                tk.comp("gpsimd", lambda g, c=c: g.tensor_copy(
                    XNT[:, c * TT:(c + 1) * TT], xn32(c)),
                    reads=[f"XN32_{c}"], writes=[f"XNT_{c}"], tag=f"xnc{tau}_{c}")
                tk.comp("vector", lambda v, c=c: v.tensor_tensor(
                    TMPC[:, :], xb(c), PS[4][:, :], ALU.mult),
                    reads=[xbk, "PS4"], writes=["TMPC"], tag=f"h1m{tau}_{c}")
                tk.comp("vector", lambda v, c=c: v.tensor_tensor(
                    TMPC[:, :], TMPC[:, :], PS[5][:, :], ALU.subtract),
                    reads=["TMPC", "PS5"], writes=["TMPC"], tag=f"h1s{tau}_{c}")
                tk.comp("vector", lambda v, c=c: v.tensor_scalar(
                    h1c(c), TMPC[:, :], G1[:, c:c + 1], B1[:, c:c + 1],
                    ALU.mult, ALU.add),
                    reads=["TMPC", "VEC"], writes=[f"H1_{c}"], tag=f"h1f{tau}_{c}")
            tap_featmajor("xn32", XN32, t0, [f"XN32_{c}" for c in range(NCH)])
            if "h1" in taps:
                for c in range(NCH):
                    tk.comp("vector", lambda v, c=c: v.tensor_copy(TMPC2[:, :], h1c(c)),
                            reads=[f"H1_{c}"], writes=["TMPC2"], tag="taph1c")
                    tk.dma("sync", lambda s, c=c: s.dma_start(
                        out=taps["h1"].rearrange("(c p) t -> p c t", p=128)[:, c, t0:t0 + TT],
                        in_=TMPC2[:, :]), reads=["TMPC2"], tag="taph1")

            def p0_block(b):
                tb = b * 128
                terms = [(lambda c=c: xn32(c)[:, tb:tb + 128],
                          lambda c=c: RTB[:, c * E:(c + 1) * E], [f"XN32_{c}"])
                         for c in range(NCH)]
                mm_group(PS[4][:, 0:E], terms, "PS4", ["RTB"], f"rt{tau}_{b}")

                for proj, slot, pk in (("k", "WA", 5), ("v", "WB", 6)):
                    for hh in range(2):
                        terms = [(lambda c=c: h1c(c)[:, tb:tb + 128],
                                  lambda c=c, hh=hh, slot=slot: w_chunk(slot, c)[:, hh * 512:(hh + 1) * 512],
                                  [f"H1_{c}"])
                                 for c in range(NCH)]
                        mm_group(PS[pk][:, :], terms, f"PS{pk}", [slot],
                                 f"{proj}{tau}_{b}_{hh}")
                        if proj == "k":
                            tk.comp("scalar", lambda s, hh=hh: s.activation(
                                KE[:, hh * 512:(hh + 1) * 512], PS[5][:, :],
                                AF.Exp, scale=SCALE),
                                reads=["PS5"], writes=["KE"], tag=f"ke{tau}_{b}_{hh}")
                        else:
                            for hq in range(2):
                                h = hh * 2 + hq
                                tk.comp("vector", lambda v, hq=hq, h=h: v.tensor_copy(
                                    VA[:, h * 257:h * 257 + 256],
                                    PS[6][:, hq * 256:(hq + 1) * 256]),
                                    reads=["PS6"], writes=["VA"], tag=f"va{tau}_{b}_{h}")
                for j in range(8):
                    h, db = j // 2, j % 2
                    pb = 7 if j % 2 == 0 else 6
                    tk.comp("tensor", lambda t, h=h, db=db, pb=pb: t.matmul(
                        PS[pb][:, 0:257],
                        KE[:, h * 256 + db * 128:h * 256 + db * 128 + 128],
                        VA[:, h * 257:(h + 1) * 257],
                        start=True, stop=True),
                        reads=["KE", "VA"], writes=[f"PS{pb}"], tag=f"ctx{tau}_{b}_{j}")
                    tk.comp("vector", lambda v, j=j, pb=pb: v.tensor_tensor(
                        CTX32[:, j * 257:(j + 1) * 257],
                        CTX32[:, j * 257:(j + 1) * 257],
                        PS[pb][:, 0:257], ALU.add),
                        reads=["CTX32", f"PS{pb}"], writes=["CTX32"],
                        tag=f"ctxa{tau}_{b}_{j}")

                combine_block(tau, b)
                if "cmb" in taps:
                    tk.dma("sync", lambda s: s.dma_start(
                        out=ct_dram.ap()[:, t0 + tb:t0 + tb + 128], in_=CTS[:, :]),
                        reads=["CTS"], writes=["ct_dram"], tag=f"ctd{tau}_{b}")
                gb = 4 * tau + b
                V8 = lambda a: a
                tk.comp("vector", lambda v, gb=gb: v.tensor_copy(
                    MSB[:, gb * 8:(gb + 1) * 8], CMB[:, 16:24]),
                    reads=["CMB"], writes=["MSB"], tag=f"svm{gb}")
                tk.comp("vector", lambda v, gb=gb: v.tensor_copy(
                    M1SB[:, gb * 8:(gb + 1) * 8], CMB[:, 32:40]),
                    reads=["CMB"], writes=["M1SB"], tag=f"svm1{gb}")
                tk.comp("vector", lambda v, gb=gb: v.tensor_copy(
                    CMBSB[:, gb * 8:(gb + 1) * 8], CMB[:, 0:8]),
                    reads=["CMB"], writes=["CMBSB"], tag=f"svc{gb}")
                tk.comp("tensor", lambda t, gb=gb: t.matmul(
                    PS[7][:, 0:8], LSTR[:, :], MSB[:, gb * 8:(gb + 1) * 8],
                    start=True, stop=True),
                    reads=["LSTR", "MSB"], writes=["PS7"], tag=f"rnk{gb}")
                tk.comp("vector", lambda v, gb=gb: v.tensor_copy(
                    RNKSB[:, gb * 8:(gb + 1) * 8], PS[7][:, 0:8]),
                    reads=["PS7"], writes=["RNKSB"], tag=f"rnke{gb}")
                tk.comp("tensor", lambda t, gb=gb: t.matmul(
                    PS[7][0:8, 8:9], MSB[:, gb * 8:(gb + 1) * 8], ONESC[:, :],
                    start=True, stop=True),
                    reads=["MSB", "ONESC"], writes=["PS7"], tag=f"cnt{gb}")
                tk.comp("vector", lambda v, gb=gb: v.tensor_copy(
                    BCNTS[:, gb:gb + 1], PS[7][0:8, 8:9]),
                    reads=["PS7"], writes=["BCNTS"], tag=f"cnte{gb}")

            for b in range(NB):
                p0_block(b)

            for b in range(NB):
                rpar = b % 2
                for c in range(NCH):
                    pb = 2 + c % 2
                    tk.comp("tensor", lambda t, b=b, c=c, pb=pb: t.matmul(
                        PS[pb][:, 0:128], XNT[:, c * TT + b * 128:c * TT + b * 128 + 128],
                        IDNB[:, :], start=True, stop=True),
                        reads=[f"XNT_{c}", "IDNB"], writes=[f"PS{pb}"],
                        tag=f"xtr{tau}_{b}_{c}")
                    tk.comp("vector", lambda v, b=b, c=c, pb=pb, rpar=rpar: v.tensor_copy(
                        XROW[:, rpar * D + c * 128:rpar * D + c * 128 + 128],
                        PS[pb][:, 0:128]),
                        reads=[f"PS{pb}"], writes=[f"XROW{rpar}"],
                        tag=f"xtre{tau}_{b}_{c}")
                tk.dma("sync", lambda s, b=b, rpar=rpar: s.dma_start(
                    out=xnb_dram.ap()[t0 + b * 128:t0 + b * 128 + 128, :],
                    in_=XROW[:, rpar * D:(rpar + 1) * D]),
                    reads=[f"XROW{rpar}"], writes=["xnb_dram"], tag=f"xrow{tau}_{b}")

        for tau in range(NTILES):
            p0_tile(tau)

        def sort_post():
            tk.comp("vector", lambda v: v.tensor_reduce(
                SRT[0:8, 0:1], BCNTS[:, :], AX, ALU.add),
                reads=["BCNTS"], writes=["SRT"], tag="sp_N")
            for j in range(4):
                tk.comp("vector", lambda v, j=j: v.tensor_scalar(
                    SRT[0:8, 1 + j:2 + j], SRT[0:8, 0:1], 512.0 * j, None, ALU.is_gt),
                    reads=["SRT"], writes=["SRT"], tag=f"sp_nt{j}")
            tk.comp("vector", lambda v: v.tensor_tensor(
                SRT[0:8, 1:2], SRT[0:8, 1:2], SRT[0:8, 2:3], ALU.add),
                reads=["SRT"], writes=["SRT"], tag="sp_nta")
            tk.comp("vector", lambda v: v.tensor_tensor(
                SRT[0:8, 3:4], SRT[0:8, 3:4], SRT[0:8, 4:5], ALU.add),
                reads=["SRT"], writes=["SRT"], tag="sp_ntb")
            tk.comp("vector", lambda v: v.tensor_tensor(
                SRT[0:8, 1:2], SRT[0:8, 1:2], SRT[0:8, 3:4], ALU.add),
                reads=["SRT"], writes=["SRT"], tag="sp_ntc")
            tk.comp("tensor", lambda t: t.matmul(
                PS[7][0:8, 0:1], LIN8[:, :], SRT[0:8, 1:2], start=True, stop=True),
                reads=["LIN8", "SRT"], writes=["PS7"], tag="sp_cum")
            tk.comp("vector", lambda v: v.tensor_copy(SRT[0:8, 5:6], PS[7][0:8, 0:1]),
                    reads=["PS7"], writes=["SRT"], tag="sp_cume")
            tk.comp("vector", lambda v: v.tensor_tensor(
                SRT[0:8, 6:7], SRT[0:8, 5:6], SRT[0:8, 1:2], ALU.subtract),
                reads=["SRT"], writes=["SRT"], tag="sp_sb0")
            tk.comp("vector", lambda v: v.tensor_scalar(
                SRT[0:8, 6:7], SRT[0:8, 6:7], 512.0, None, ALU.mult),
                reads=["SRT"], writes=["SRT"], tag="sp_sb1")
            tk.comp("vector", lambda v: v.tensor_scalar(
                BCNTS[:, 0:1], BCNTS[:, 0:1], 0.0, None, ALU.mult),
                reads=["BCNTS"], writes=["BCNTS"], tag="sp_b0") if False else None
            for gb in range(15, 0, -1):
                pass
            tk.comp("vector", lambda v: v.tensor_scalar(
                SRT[0:8, 7:8], SRT[0:8, 0:1], 0.0, None, ALU.mult),
                reads=["SRT"], writes=["SRT"], tag="sp_acc0")
            for gb in range(16):
                tk.comp("vector", lambda v, gb=gb: v.tensor_copy(
                    SRT[0:8, 8:9], BCNTS[:, gb:gb + 1]),
                    reads=["BCNTS"], writes=["SRT"], tag=f"sp_sv{gb}")
                tk.comp("vector", lambda v, gb=gb: v.tensor_tensor(
                    BCNTS[:, gb:gb + 1], SRT[0:8, 7:8], SRT[0:8, 6:7], ALU.add),
                    reads=["SRT"], writes=["BCNTS"], tag=f"sp_off{gb}")
                tk.comp("vector", lambda v, gb=gb: v.tensor_tensor(
                    SRT[0:8, 7:8], SRT[0:8, 7:8], SRT[0:8, 8:9], ALU.add),
                    reads=["SRT"], writes=["SRT"], tag=f"sp_run{gb}")
            tk.dma("sync", lambda s: s.dma_start(out=off_dram.ap(), in_=BCNTS[:, :]),
                   reads=["BCNTS"], writes=["off_dram"], tag="sp_offd")
            tk.dma("sync", lambda s: s.dma_start(out=cum_dram.ap(), in_=SRT[0:8, 5:6]),
                   reads=["SRT"], writes=["cum_dram"], tag="sp_cumd")
            tk.dma("sync", lambda s: s.dma_start(
                out=SRT[0:1, 0:8], in_=cum_dram.ap().rearrange("e one -> one e")),
                reads=["cum_dram"], writes=["SRT"], tag="sp_cumr")
            for k in range(16):
                tk.comp("vector", lambda v, k=k: v.tensor_scalar(
                    SRT[0:1, 8:16], SRT[0:1, 0:8], float(k) + 0.5, None, ALU.is_lt),
                    reads=["SRT"], writes=["SRT"], tag=f"sp_ek{k}")
                tk.comp("vector", lambda v, k=k: v.tensor_reduce(
                    EXPB[0:1, k:k + 1], SRT[0:1, 8:16], AX, ALU.add),
                    reads=["SRT"], writes=["EXPB"], tag=f"sp_ekr{k}")
            tk.comp("vector", lambda v: v.tensor_scalar(
                EXPB[0:1, 0:16], EXPB[0:1, 0:16], 7.0, None, ALU.min),
                reads=["EXPB"], writes=["EXPB"], tag="sp_clamp")
            tk.comp("tensor", lambda t: t.matmul(
                PS[7][:, 0:16], ONESR[:, :], EXPB[0:1, 0:16], start=True, stop=True),
                reads=["EXPB", "ONESR"], writes=["PS7"], tag="sp_eb")
            tk.comp("vector", lambda v: v.tensor_copy(EXPB[:, :], PS[7][:, 0:16]),
                    reads=["PS7"], writes=["EXPB"], tag="sp_ebc")
            for gb in range(16):
                tk.dma("sync", lambda s, gb=gb: s.dma_start(
                    out=SRT[0:1, 0:8], in_=off_dram.ap()[:, gb:gb + 1]
                    .rearrange("e one -> one e"),
                    allow_slow_non_contiguous=True),
                    reads=["off_dram"], writes=["SRT"], tag=f"sp_ofl{gb}")
                tk.comp("tensor", lambda t, gb=gb: t.matmul(
                    PS[7][:, 0:8], ONESR[:, :], SRT[0:1, 0:8], start=True, stop=True),
                    reads=["SRT", "ONESR"], writes=["PS7"], tag=f"sp_ofb{gb}")
                tk.comp("vector", lambda v, gb=gb: v.tensor_tensor(
                    SRT[:, 0:8], PS[7][:, 0:8], RNKSB[:, gb * 8:(gb + 1) * 8], ALU.add),
                    reads=["PS7", "RNKSB"], writes=["SRT"], tag=f"sp_sl{gb}")
                for j in range(2):
                    mk = "M1SB" if j == 0 else "MSB"
                    if j == 0:
                        tk.comp("vector", lambda v, gb=gb: v.tensor_copy(
                            SRT[:, 8:16], M1SB[:, gb * 8:(gb + 1) * 8]),
                            reads=["M1SB"], writes=["SRT"], tag=f"sp_mj{gb}_{j}")
                    else:
                        tk.comp("vector", lambda v, gb=gb: v.tensor_tensor(
                            SRT[:, 8:16], MSB[:, gb * 8:(gb + 1) * 8],
                            M1SB[:, gb * 8:(gb + 1) * 8], ALU.subtract),
                            reads=["MSB", "M1SB"], writes=["SRT"], tag=f"sp_mj{gb}_{j}")
                    tk.comp("vector", lambda v: v.tensor_tensor(
                        TMPC[:, 0:8], SRT[:, 8:16], SRT[:, 0:8], ALU.mult),
                        reads=["SRT"], writes=["TMPC"], tag=f"sp_ms{gb}_{j}")
                    tk.comp("vector", lambda v: v.tensor_reduce(
                        TOKR[:, 2:3], TMPC[:, 0:8], AX, ALU.add),
                        reads=["TMPC"], writes=["TOKR"], tag=f"sp_ix{gb}_{j}")
                    tk.comp("vector", lambda v, gb=gb: v.tensor_tensor(
                        TMPC[:, 0:8], CMBSB[:, gb * 8:(gb + 1) * 8], ESCB[:, :],
                        ALU.mult),
                        reads=["CMBSB", "ESCB"], writes=["TMPC"], tag=f"sp_ce{gb}_{j}")
                    tk.comp("vector", lambda v: v.tensor_tensor(
                        TMPC[:, 0:8], TMPC[:, 0:8], SRT[:, 8:16], ALU.mult),
                        reads=["TMPC", "SRT"], writes=["TMPC"], tag=f"sp_cm{gb}_{j}")
                    tk.comp("vector", lambda v: v.tensor_reduce(
                        TOKR[:, 1:2], TMPC[:, 0:8], AX, ALU.add),
                        reads=["TMPC"], writes=["TOKR"], tag=f"sp_cer{gb}_{j}")
                    tk.comp("vector", lambda v, gb=gb: v.tensor_scalar(
                        TOKR[:, 0:1], IOTAS[:, 0:1], 1.0, 128.0 * gb, ALU.mult, ALU.add),
                        reads=["IOTAS"], writes=["TOKR"], tag=f"sp_tk{gb}_{j}")
                    tk.comp("vector", lambda v: v.tensor_copy(IDXI[:, :], TOKR[:, 2:3]),
                            reads=["TOKR"], writes=["IDXI"], tag=f"sp_ii{gb}_{j}")
                    tk.dma("gpsimd", lambda g: g.indirect_dma_start(
                        tok_dram.ap(), bass_rust.IndirectOffsetOnAxis(ap=IDXI[:, :], axis=0),
                        TOKR[:, :], None),
                        reads=["TOKR", "IDXI"], writes=["tok_dram"],
                        tag=f"sp_sc{gb}_{j}")

        sort_post()
        if "srt" in taps:
            tk.dma("sync", lambda s: s.dma_start(out=taps["srt"], in_=tok_dram.ap()),
                   reads=["tok_dram"], tag="tapsrt")

        if "cmdbg" in taps:
            tk.dma("sync", lambda s: s.dma_start(out=taps["cmdbg"][:, 0:64],
                                                 in_=CMB[:, 0:64]),
                   reads=["CMB"], tag="tapcmdbg")
        if "kv" in taps:
            tk.dma("gpsimd", lambda g: g.dma_start(out=taps["kv"][:, 0:1024],
                                                   in_=KE[:, :]),
                   reads=["KE"], tag="tapkv1")
            tk.dma("gpsimd", lambda g: g.dma_start(out=taps["kv"][:, 1024:2052],
                                                   in_=VA[:, :]),
                   reads=["VA"], tag="tapkv2")
        tk.dma("sync", lambda s: s.dma_start(out=cc_in.ap(), in_=CTX32[:, :]),
               reads=["CTX32"], writes=["cc_in"], tag="ccout")
        if "ctx" in taps:
            tk.dma("sync", lambda s: s.dma_start(out=taps["ctx"], in_=CTX32[:, :]),
                   reads=["CTX32"], tag="tapctx")
        if "cmb" in taps:
            tk.dma("sync", lambda s: s.dma_start(out=taps["cmb"], in_=ct_dram.ap()),
                   reads=["ct_dram"], tag="tapcmb")

        p0_scope.close()
        tk.barrier("p0_to_p3")
        tk.cc(lambda g: g.collective_compute(
            "AllReduce", ALU.add, replica_groups=[list(range(NCORES))],
            ins=[cc_in.ap().opt()], outs=[cc_out.ap().opt()]),
            reads=["cc_in"], writes=["cc_out"], tag="allreduce")
        p3_scope = ExitStack()
        HFULLD = p3_scope.enter_context(nc.sbuf_tensor("HFULLD", [128, 2 * NFC * TT], BF16))
        XNSD = p3_scope.enter_context(nc.sbuf_tensor("XNSD", [128, 2 * NCH * TT], BF16))
        XG = p3_scope.enter_context(nc.sbuf_tensor("XG", [128, 2 * D], BF16))
        YTOKD = p3_scope.enter_context(nc.sbuf_tensor("YTOKD", [128, 2 * 4 * D], F32))
        TOKT = p3_scope.enter_context(nc.sbuf_tensor("TOKT", [128, 2 * 4 * 8], F32))
        TKI = p3_scope.enter_context(nc.sbuf_tensor("TKI", [128, 2 * 4], mybir.dt.int32))
        WIX = p3_scope.enter_context(nc.sbuf_tensor("WIX", [128, 16], F32))
        WIXI = p3_scope.enter_context(nc.sbuf_tensor("WIXI", [128, 16], mybir.dt.int32))
        p3b_scope = ExitStack()
        CCB = p3b_scope.enter_context(nc.sbuf_tensor("CCB", [128, 8 * 257], BF16))
        RECJ = p3b_scope.enter_context(nc.sbuf_tensor("RECJ", [128, 8], F32))

        def ctx_normalize():
            tk.dma("gpsimd", lambda g: g.dma_start(out=CCB[:, :], in_=cc_out.ap()),
                   reads=["cc_out"], writes=["CCB"], tag="ccload")
            for j in range(8):
                tk.comp("vector", lambda v, j=j: v.reciprocal(
                    RECJ[:, j:j + 1], CCB[:, j * 257 + 256:j * 257 + 257]),
                    reads=["CCB"], writes=["RECJ"], tag=f"nrec{j}")
                tk.comp("vector", lambda v, j=j: v.tensor_scalar(
                    CTX_BF[:, j * 256:(j + 1) * 256], CCB[:, j * 257:j * 257 + 256],
                    RECJ[:, j:j + 1], None, ALU.mult),
                    reads=["CCB", "RECJ"], writes=["CTX_BF"], tag=f"nmul{j}")
            if "ctxn" in taps:
                tk.dma("gpsimd", lambda g: g.dma_start(
                    out=taps["ctxn"], in_=CTX_BF[:, :]),
                    reads=["CTX_BF"], tag="tapctxn")

        def sp_prologue(k):
            kp = k % 2
            XNS = XNSD[:, kp * NCH * TT:(kp + 1) * NCH * TT]
            xk = lambda c: f"XNS{kp}_{c}"
            tkk = lambda: f"TOK{kp}"
            for b in range(4):
                bq = kp * 4 + b
                tk.dma("sync", lambda s, bq=bq, b=b: s.dma_start(
                    out=TOKT[:, bq * 8:(bq + 1) * 8],
                    in_=tok_dram.ap()[k * TT + b * 128:k * TT + b * 128 + 128, :]),
                    reads=["tok_dram"], writes=[tkk()], tag=f"tok{k}_{b}")
                tk.comp("vector", lambda v, bq=bq: v.tensor_copy(
                    TKI[:, bq:bq + 1], TOKT[:, bq * 8:bq * 8 + 1]),
                    reads=[tkk()], writes=[f"TKI{kp}"], tag=f"tki{k}_{b}")
                tk.dma("gpsimd", lambda g, b=b, bq=bq: g.indirect_dma_start(
                    XG[:, (b % 2) * D:(b % 2 + 1) * D], None, xnb_dram.ap(),
                    bass_rust.IndirectOffsetOnAxis(ap=TKI[:, bq:bq + 1], axis=0)),
                    reads=["xnb_dram", f"TKI{kp}"], writes=[f"XG{b % 2}"],
                    tag=f"xg{k}_{b}", lane=6 + b % 2)
                for c in range(NCH):
                    pb = 6 + c % 2
                    tk.comp("tensor", lambda t, b=b, c=c, pb=pb: t.matmul(
                        PS[pb][:, 0:128],
                        XG[:, (b % 2) * D + c * 128:(b % 2) * D + c * 128 + 128],
                        IDNB[:, :], start=True, stop=True),
                        reads=[f"XG{b % 2}", "IDNB"], writes=[f"PS{pb}"],
                        tag=f"sptr{k}_{b}_{c}")
                    tk.comp("vector", lambda v, b=b, c=c, pb=pb: v.tensor_copy(
                        XNS[:, c * TT + b * 128:c * TT + b * 128 + 128],
                        PS[pb][:, 0:128]),
                        reads=[f"PS{pb}"], writes=[xk(c)],
                        tag=f"sptre{k}_{b}_{c}")

        def sp_tile(k):
            kp = k % 2
            HFULL = HFULLD[:, kp * NFC * TT:(kp + 1) * NFC * TT]
            XNS = XNSD[:, kp * NCH * TT:(kp + 1) * NCH * TT]
            YTOK = YTOKD[:, kp * 4 * D:(kp + 1) * 4 * D]
            hk = lambda fc: f"HF{kp}_{fc}"
            xk = lambda c: f"XNS{kp}_{c}"
            yk = lambda sb: f"YT{kp}_{sb}"
            tkk = lambda: f"TOK{kp}"

            def widx(col, g, tag):
                col = col + kp * 8
                tk.comp("vector", lambda v, col=col, g=g: v.scalar_tensor_tensor(
                    WIX[:, col:col + 1], EXPB[:, k:k + 1], 512.0, IOTAS[:, g:g + 1],
                    ALU.mult, ALU.add),
                    reads=["EXPB", "IOTAS"], writes=[f"WIX{col}"], tag=tag + "a")
                tk.comp("vector", lambda v, col=col: v.tensor_copy(
                    WIXI[:, col:col + 1], WIX[:, col:col + 1]),
                    reads=[f"WIX{col}"], writes=[f"WIXI{col}"], tag=tag + "b")

            for s4 in range(4):
                w1slot = WROT[(k * 8 + s4) % 3]
                widx(s4, s4, f"wi1_{k}_{s4}")
                tk.dma("gpsimd", lambda g, s4=s4, w1slot=w1slot, kp=kp: g.indirect_dma_start(
                    WSL[w1slot][:, :], None, w1g,
                    bass_rust.IndirectOffsetOnAxis(ap=WIXI[:, kp * 8 + s4:kp * 8 + s4 + 1], axis=0),
                    oob_is_err=False),
                    reads=[f"WIXI{kp * 8 + s4}"], writes=[w1slot], tag=f"w1g{k}_{s4}")
                for fb in range(8):
                    fbg = s4 * 8 + fb
                    pb = 2 + fbg % 2
                    terms = [(lambda c=c, fb=fb, w1slot=w1slot: w_chunk(w1slot, c)[:, fb * 128:(fb + 1) * 128],
                              lambda c=c, XNS=XNS: XNS[:, c * TT:(c + 1) * TT], [xk(c)])
                             for c in range(NCH)]
                    mm_group(PS[pb][:, :], terms, f"PS{pb}", [w1slot],
                             f"sph{k}_{fbg}")
                    tk.comp("scalar", lambda sc, fbg=fbg, pb=pb, HFULL=HFULL: sc.activation(
                        HFULL[:, fbg * TT:(fbg + 1) * TT], PS[pb][:, :], AF.Silu),
                        reads=[f"PS{pb}"], writes=[hk(fbg)], tag=f"sphs{k}_{fbg}")

            for dq in range(4):
                w2slot = WROT[(k * 8 + 4 + dq) % 3]
                widx(4 + dq, dq, f"wi2_{k}_{dq}")
                tk.dma("gpsimd", lambda g, dq=dq, w2slot=w2slot, kp=kp: g.indirect_dma_start(
                    WSL[w2slot][:, :], None, w2g,
                    bass_rust.IndirectOffsetOnAxis(ap=WIXI[:, kp * 8 + 4 + dq:kp * 8 + 5 + dq], axis=0),
                    oob_is_err=False),
                    reads=[f"WIXI{kp * 8 + 4 + dq}"], writes=[w2slot], tag=f"w2g{k}_{dq}")
                for sb in range(4):
                    pb = 4 + sb % 2
                    terms = [(lambda fc=fc, sb=sb, HFULL=HFULL: HFULL[:, fc * TT + sb * 128:fc * TT + sb * 128 + 128],
                              lambda fc=fc, w2slot=w2slot: WSL[w2slot][:, fc * 256:(fc + 1) * 256],
                              [hk(fc)]) for fc in range(NFC)]
                    mm_group(PS[pb][:, 0:256], terms, f"PS{pb}", [w2slot],
                             f"spy{k}_{dq}_{sb}")
                    tk.comp("vector", lambda v, dq=dq, sb=sb, pb=pb, kp=kp, YTOK=YTOK: v.tensor_scalar(
                        YTOK[:, sb * D + dq * 256:sb * D + (dq + 1) * 256],
                        PS[pb][:, 0:256], TOKT[:, (kp * 4 + sb) * 8 + 1:(kp * 4 + sb) * 8 + 2], None,
                        ALU.mult),
                        reads=[f"PS{pb}", tkk()], writes=[yk(sb)],
                        tag=f"spye{k}_{dq}_{sb}")
            for sb in range(4):
                tk.dma("gpsimd", lambda g, sb=sb, kp=kp, YTOK=YTOK: g.indirect_dma_start(
                    ys_dram.ap(),
                    bass_rust.IndirectOffsetOnAxis(ap=TKI[:, kp * 4 + sb:kp * 4 + sb + 1], axis=0),
                    YTOK[:, sb * D:(sb + 1) * D], None, compute_op=ALU.add),
                    reads=[yk(sb), f"TKI{kp}", "ys_dram"], writes=["ys_dram"],
                    tag=f"spsc{k}_{sb}")

        sp_prologue(0)
        for k in range(16):
            if k + 1 < 16:
                sp_prologue(k + 1)
            sp_tile(k)
            if DEBUG.get("spbar"):
                tk.barrier(f"spbar{k}")
            if k == 3:
                ctx_normalize()

        if "ys" in taps:
            for zz in range(8):
                tk.dma("sync", lambda s, zz=zz: s.dma_start(
                    out=taps["ys"][zz * 256:(zz + 1) * 256, :]
                    .rearrange("(p r) c -> p (r c)", p=128),
                    in_=ZZ[:, :]), reads=["ys_dram"], tag="tapysX") if False else None
            tk.dma("sync", lambda s: s.dma_start(
                out=taps["ys"].rearrange("(a p) d -> p (a d)", p=128),
                in_=ys_dram.ap().rearrange("(a p) d -> p (a d)", p=128)),
                reads=["ys_dram"], tag="tapys") if False else None
            for b16 in range(16):
                tk.dma("sync", lambda s, b16=b16: s.dma_start(
                    out=taps["ys"][b16 * 128:(b16 + 1) * 128, :],
                    in_=ys_dram.ap()[b16 * 128:(b16 + 1) * 128, :]),
                    reads=["ys_dram"], tag=f"tapys{b16}")

        p3b_scope.close()
        p3_scope.close()
        g_scope = ExitStack()
        GBF = g_scope.enter_context(nc.sbuf_tensor("GBF", [128, NCH * T], BF16,
                                                   side="right"))
        gl_scope = ExitStack()
        YSROW = gl_scope.enter_context(nc.sbuf_tensor("YSROW", [128, 2 * D], F32))
        tk.barrier("p3_to_g")

        def g_sl(db, t0):
            return GBF[:, db * T + t0:db * T + t0 + TT]

        for tb in range(16):
            rpar = tb % 2
            tk.dma("sync", lambda s, tb=tb, rpar=rpar: s.dma_start(
                out=YSROW[:, rpar * D:(rpar + 1) * D],
                in_=ys_dram.ap()[tb * 128:(tb + 1) * 128, :]),
                reads=["ys_dram"], writes=[f"YSR{rpar}"], tag=f"gld{tb}")
            for c in range(NCH):
                pb = 6 if c % 2 == 0 else 2
                tk.comp("tensor", lambda t, tb=tb, c=c, pb=pb, rpar=rpar: t.matmul(
                    PS[pb][:, 0:128],
                    YSROW[:, rpar * D + c * 128:rpar * D + c * 128 + 128],
                    IDN[:, :], start=True, stop=True),
                    reads=[f"YSR{rpar}", "IDN"], writes=[f"PS{pb}"],
                    tag=f"gtr{tb}_{c}")
                tk.comp("scalar", lambda sc, tb=tb, c=c, pb=pb: sc.activation(
                    GBF[:, c * T + tb * 128:c * T + tb * 128 + 128],
                    PS[pb][:, 0:128], AF.Silu),
                    reads=[f"PS{pb}"], writes=[f"G{c}"], tag=f"gsil{tb}_{c}")
        if "g" in taps:
            for db in range(NCH):
                for tau in range(NTILES):
                    t0g = tau * TT
                    tk.comp("vector", lambda v, db=db, t0g=t0g: v.tensor_copy(
                        TMPC[:, :], g_sl(db, t0g)),
                        reads=[f"G{db}"], writes=["TMPC"], tag="tapgc")
                    tk.dma("sync", lambda s, db=db, t0g=t0g: s.dma_start(
                        out=taps["g"].rearrange("(c p) t -> p c t", p=128)[:, db, t0g:t0g + TT],
                        in_=TMPC[:, :]), reads=["TMPC"], tag="tapg")
        gl_scope.close()

        p2s = ExitStack()
        X2D = p2s.enter_context(nc.sbuf_tensor("X2D", [128, 2 * NCH * TT], F32))
        RES = p2s.enter_context(nc.sbuf_tensor("RES", [128, NCH * TT], F32))
        H1B = p2s.enter_context(nc.sbuf_tensor("H1B", [128, NCH * TT], BF16))
        QS = p2s.enter_context(nc.sbuf_tensor("QS", [128, NCH * TT], BF16))
        UB = p2s.enter_context(nc.sbuf_tensor("UB", [128, NFC * TT], BF16))
        BCA2 = p2s.enter_context(nc.sbuf_tensor("BCA2", [128, TT], F32))
        BCB2 = p2s.enter_context(nc.sbuf_tensor("BCB2", [128, TT], F32))
        tk.barrier("p3_to_p2")

        def p2_tile(tau):
            t0 = tau * TT
            par = tau % 2
            X2 = X2D[:, par * NCH * TT:(par + 1) * NCH * TT]
            xk = lambda c: f"X2_{par}_{c}"
            rk = lambda c: f"RES_{c}"

            def x2c(c):
                return X2[:, c * TT:(c + 1) * TT]

            def resc(c):
                return RES[:, c * TT:(c + 1) * TT]

            def h1b(c):
                return H1B[:, c * TT:(c + 1) * TT]

            def qsc(c):
                return QS[:, c * TT:(c + 1) * TT]

            def ub(fb):
                return UB[:, fb * TT:(fb + 1) * TT]

            tk.dma("sync", lambda s: s.dma_start(
                out=X2[:, :].rearrange("p (c t) -> p c t", t=TT),
                in_=xT.rearrange("(c p) t -> p c t", p=128)[:, :, t0:t0 + TT]),
                writes=[xk(c) for c in range(NCH)], tag=f"x2load{tau}")
            tk.dma("sync", lambda s: s.dma_start(
                out=ROWS[0:1, TT:5 * TT], in_=rows_dram.ap()[:, t0:t0 + TT]),
                reads=["rows_dram"], writes=["ROW1", "ROW2", "ROW3", "ROW4"],
                tag=f"rload{tau}")
            bcast(6, row(1), ["ROW1"], f"p2r0_{tau}")
            bcast(7, row(2), ["ROW2"], f"p2q0_{tau}")
            bcast(4, row(3), ["ROW3"], f"p2p01_{tau}")
            bcast(5, row(4), ["ROW4"], f"p2q01_{tau}")
            tk.comp("scalar", lambda s: s.activation(
                BCA2[:, :], PS[6][:, :], AF.Identity),
                reads=["PS6"], writes=["BCA2"], tag=f"stg_r0_{tau}")
            tk.comp("scalar", lambda s: s.activation(
                BCB2[:, :], PS[7][:, :], AF.Identity),
                reads=["PS7"], writes=["BCB2"], tag=f"stg_q0_{tau}")
            for c in range(NCH):
                tk.comp("gpsimd", lambda g, c=c: g.tensor_tensor(
                    resc(c), x2c(c), BCA2[:, :], ALU.mult),
                    reads=[xk(c), "BCA2"], writes=[rk(c)], tag=f"q0m{tau}_{c}")
                tk.comp("gpsimd", lambda g, c=c: g.tensor_tensor(
                    resc(c), resc(c), BCB2[:, :], ALU.subtract),
                    reads=[rk(c), "BCB2"], writes=[rk(c)], tag=f"q0s{tau}_{c}")
                tk.comp("vector", lambda v, c=c: v.tensor_tensor(
                    TMPC[:, :], x2c(c), PS[4][:, :], ALU.mult),
                    reads=[xk(c), "PS4"], writes=["TMPC"], tag=f"qh1m{tau}_{c}")
                tk.comp("vector", lambda v, c=c: v.tensor_tensor(
                    TMPC[:, :], TMPC[:, :], PS[5][:, :], ALU.subtract),
                    reads=["TMPC", "PS5"], writes=["TMPC"], tag=f"qh1s{tau}_{c}")
                tk.comp("vector", lambda v, c=c: v.tensor_scalar(
                    h1b(c), TMPC[:, :], G1[:, c:c + 1], B1[:, c:c + 1],
                    ALU.mult, ALU.add),
                    reads=["TMPC", "VEC"], writes=[f"H1B_{c}"], tag=f"qh1f{tau}_{c}")

            wqslot = WROT[(tau * 4) % 3]
            wload(wqslot, wq.rearrange("(c p) n -> p c n", p=128), f"wq{tau}")
            for qb in range(NCH):
                pb = 2 + qb % 2
                terms = [(lambda c=c, qb=qb: w_chunk(wqslot, c)[:, qb * 128:(qb + 1) * 128],
                          lambda c=c: h1b(c), [f"H1B_{c}"]) for c in range(NCH)]
                mm_group(PS[pb][:, :], terms, f"PS{pb}", [wqslot], f"q{tau}_{qb}")
                tk.comp("scalar", lambda s, qb=qb, pb=pb: s.activation(
                    qsc(qb), PS[pb][:, :], AF.Exp, scale=SCALE),
                    reads=[f"PS{pb}"], writes=[f"QS_{qb}"], tag=f"qe{tau}_{qb}")
            for h in range(HEADS):
                terms = [(lambda: ONESB[:, :], lambda c=c: qsc(c), [f"QS_{c}"])
                         for c in (2 * h, 2 * h + 1)]
                mm_group(PS[0][0:1, :], terms, "PS0", ["ONESB"], f"qsum{tau}_{h}")
                tk.comp("vector", lambda v: v.tensor_copy(row(0), PS[0][0:1, :]),
                        reads=["PS0"], writes=["ROW0"], tag=f"qsc{tau}_{h}")
                tk.comp("vector", lambda v: v.reciprocal(row(5), row(0)),
                        reads=["ROW0"], writes=["ROW5"], tag=f"qrec{tau}_{h}")
                pbq = 6 + h % 2
                bcast(pbq, row(5), ["ROW5"], f"qb{tau}_{h}")
                for c in (2 * h, 2 * h + 1):
                    tk.comp("vector", lambda v, c=c, pbq=pbq: v.tensor_tensor(
                        qsc(c), qsc(c), PS[pbq][:, :], ALU.mult),
                        reads=[f"QS_{c}", f"PS{pbq}"], writes=[f"QS_{c}"],
                        tag=f"qn{tau}_{c}")

            for h in range(HEADS):
                for eb in range(2):
                    ac = 2 * h + eb
                    pb = 2 + ac % 2
                    terms = [(lambda dc=dc, h=h, eb=eb: CTX_BF[:, (2 * h + dc) * 256 + eb * 128:
                                                                 (2 * h + dc) * 256 + eb * 128 + 128],
                              lambda dc=dc, h=h: qsc(2 * h + dc),
                              [f"QS_{2 * h + dc}"]) for dc in range(2)]
                    mm_group(PS[pb][:, :], terms, f"PS{pb}", ["CTX_BF"],
                             f"at{tau}_{ac}")
                    tk.comp("vector", lambda v, ac=ac, pb=pb: v.tensor_copy(
                        ub(ac), PS[pb][:, :]),
                        reads=[f"PS{pb}"], writes=[f"UB_{ac}"], tag=f"atc{tau}_{ac}")

            woslot = WROT[(tau * 4 + 1) % 3]
            wload(woslot, wo.rearrange("(c p) n -> p c n", p=128), f"wo{tau}")
            for db in range(NCH):
                pb = 4 + db % 2
                terms = [(lambda ac=ac, db=db: w_chunk(woslot, ac)[:, db * 128:(db + 1) * 128],
                          lambda ac=ac: ub(ac), [f"UB_{ac}"]) for ac in range(NCH)]
                mm_group(PS[pb][:, :], terms, f"PS{pb}", [woslot], f"wo{tau}_{db}")
                tk.comp("vector", lambda v, db=db, pb=pb: v.scalar_tensor_tensor(
                    resc(db), PS[pb][:, :], BO[:, db:db + 1], resc(db),
                    ALU.add, ALU.add),
                    reads=[f"PS{pb}", "VEC", rk(db)], writes=[rk(db)],
                    tag=f"woe{tau}_{db}")
            tap_featmajor("x2", RES, t0, [rk(c) for c in range(NCH)])

            def qsq(c):
                return QS[:, c * TT:(c + 1) * TT]

            stats8(resc, rk, qsq, lambda c: f"QS_{c}", f"l2_{tau}")
            rstd_from(5, 1, f"l2r_{tau}")
            tk.comp("vector", lambda v: v.tensor_tensor(
                row(2), row(0), row(1), ALU.mult),
                reads=["ROW0", "ROW1"], writes=["ROW2"], tag=f"l2q_{tau}")
            bcast(6, row(1), ["ROW1"], f"l2P_{tau}")
            bcast(7, row(2), ["ROW2"], f"l2Q_{tau}")
            for c in range(NCH):
                tk.comp("vector", lambda v, c=c: v.tensor_tensor(
                    TMPC[:, :], resc(c), PS[6][:, :], ALU.mult),
                    reads=[rk(c), "PS6"], writes=["TMPC"], tag=f"h2m{tau}_{c}")
                tk.comp("vector", lambda v, c=c: v.tensor_tensor(
                    TMPC[:, :], TMPC[:, :], PS[7][:, :], ALU.subtract),
                    reads=["TMPC", "PS7"], writes=["TMPC"], tag=f"h2s{tau}_{c}")
                tk.comp("vector", lambda v, c=c: v.tensor_scalar(
                    h1b(c), TMPC[:, :], G2[:, c:c + 1], B2[:, c:c + 1],
                    ALU.mult, ALU.add),
                    reads=["TMPC", "VEC"], writes=[f"H1B_{c}"], tag=f"h2f{tau}_{c}")

            for s4 in range(4):
                f0 = s4 * 1024
                wfslot = WROT[(tau * 4 + 2 + s4) % 3]
                wload(wfslot,
                      wf1.rearrange("(c p) f -> p c f", p=128)[:, :, f0:f0 + 1024],
                      f"wf1_{tau}_{s4}")
                for fb in range(8):
                    fbg = s4 * 8 + fb
                    pb = 2 + fbg % 2
                    terms = [(lambda c=c, fb=fb, wfslot=wfslot: w_chunk(wfslot, c)[:, fb * 128:(fb + 1) * 128],
                              lambda c=c: h1b(c), [f"H1B_{c}"]) for c in range(NCH)]
                    mm_group(PS[pb][:, :], terms, f"PS{pb}", [wfslot],
                             f"f1_{tau}_{fbg}")
                    tk.comp("scalar", lambda s, pb=pb, fbg=fbg: s.activation(
                        TMPC2[:, :], PS[pb][:, :], AF.Identity,
                        bias=BFF1[:, fbg:fbg + 1]),
                        reads=[f"PS{pb}", "VEC"], writes=["TMPC2"],
                        tag=f"f1b{tau}_{fbg}")
                    tk.comp("scalar", lambda s: s.activation(
                        TMPC[:, :], TMPC2[:, :], AF.Square),
                        reads=["TMPC2"], writes=["TMPC"], tag=f"f1sq{tau}_{fbg}")
                    tk.comp("vector", lambda v: v.tensor_scalar(
                        TMPC[:, :], TMPC[:, :], GELU_A * GELU_B, GELU_A,
                        ALU.mult, ALU.add),
                        reads=["TMPC"], writes=["TMPC"], tag=f"f1p{tau}_{fbg}")
                    tk.comp("vector", lambda v: v.tensor_tensor(
                        TMPC[:, :], TMPC[:, :], TMPC2[:, :], ALU.mult),
                        reads=["TMPC", "TMPC2"], writes=["TMPC"],
                        tag=f"f1i{tau}_{fbg}")
                    tk.comp("scalar", lambda s: s.activation(
                        TMPC[:, :], TMPC[:, :], AF.Tanh),
                        reads=["TMPC"], writes=["TMPC"], tag=f"f1t{tau}_{fbg}")
                    tk.comp("vector", lambda v: v.tensor_scalar(
                        TMPC[:, :], TMPC[:, :], 0.5, 0.5, ALU.mult, ALU.add),
                        reads=["TMPC"], writes=["TMPC"], tag=f"f1q{tau}_{fbg}")
                    tk.comp("vector", lambda v, fbg=fbg: v.tensor_tensor(
                        ub(fbg), TMPC[:, :], TMPC2[:, :], ALU.mult),
                        reads=["TMPC", "TMPC2"], writes=[f"UB_{fbg}"],
                        tag=f"f1u{tau}_{fbg}")

            for dbp in range(4):
                wfslot2 = WROT[(tau * 4 + dbp) % 3]
                dst = WSL[wfslot2][:, 0:NFC * 256].rearrange("p (c w) -> p c w", w=256)
                tk.dma("gpsimd", lambda g, dbp=dbp, dst=dst, wfslot2=wfslot2: g.dma_start(
                    out=dst,
                    in_=wf2.rearrange("(c p) d -> p c d", p=128)[:, :, dbp * 256:(dbp + 1) * 256]),
                    writes=[wfslot2], tag=f"wf2_{tau}_{dbp}")
                for dq in range(2):
                    db = dbp * 2 + dq
                    pb = 4 + db % 2
                    terms = [(lambda fc=fc, dq=dq, wfslot2=wfslot2:
                              WSL[wfslot2][:, fc * 256 + dq * 128:fc * 256 + dq * 128 + 128],
                              lambda fc=fc: ub(fc), [f"UB_{fc}"]) for fc in range(NFC)]
                    mm_group(PS[pb][:, :], terms, f"PS{pb}", [wfslot2],
                             f"f2_{tau}_{db}")
                    tk.comp("vector", lambda v, db=db, pb=pb: v.scalar_tensor_tensor(
                        resc(db), PS[pb][:, :], BF2[:, db:db + 1], resc(db),
                        ALU.add, ALU.add),
                        reads=[f"PS{pb}", "VEC", rk(db)], writes=[rk(db)],
                        tag=f"f2e{tau}_{db}")

            for c in range(NCH):
                tk.comp("vector", lambda v, c=c: v.tensor_copy(qsc(c), resc(c)),
                        reads=[rk(c)], writes=[f"QS_{c}"], tag=f"x3c{tau}_{c}")
            wdslot = WROT[(tau * 4 + 3) % 3]
            wload(wdslot, wd.rearrange("(c p) n -> p c n", p=128), f"wd{tau}")
            for db in range(NCH):
                pb = 2 + db % 2
                terms = [(lambda c=c, db=db: w_chunk(wdslot, c)[:, db * 128:(db + 1) * 128],
                          lambda c=c: qsc(c), [f"QS_{c}"]) for c in range(NCH)]
                mm_group(PS[pb][:, :], terms, f"PS{pb}", [wdslot],
                         f"sd{tau}_{db}")
                if "s" in taps:
                    tk.comp("vector", lambda v, db=db, pb=pb: v.tensor_scalar(
                        TMPC[:, :], PS[pb][:, :], BD[:, db:db + 1], None, ALU.add),
                        reads=[f"PS{pb}", "VEC"], writes=["TMPC"], tag="taps1")
                    tk.dma("sync", lambda s, db=db: s.dma_start(
                        out=taps["s"].rearrange("(c p) t -> p c t", p=128)[:, db, t0:t0 + TT],
                        in_=TMPC[:, :]), reads=["TMPC"], tag="taps")
                tk.comp("scalar", lambda sc, db=db, pb=pb: sc.activation(
                    TMPC2[:, :], PS[pb][:, :], AF.Identity, bias=BD[:, db:db + 1]),
                    reads=[f"PS{pb}", "VEC"], writes=["TMPC2"], tag=f"sb{tau}_{db}")
                tk.comp("vector", lambda v, db=db: v.tensor_tensor(
                    TMPC[:, :], TMPC2[:, :], g_sl(db, t0), ALU.mult),
                    reads=["TMPC2", f"G{db}"], writes=["TMPC"], tag=f"sg{tau}_{db}")
                tk.comp("gpsimd", lambda g, db=db: g.tensor_tensor(
                    x2c(db), x2c(db), TMPC[:, :], ALU.add),
                    reads=[xk(db), "TMPC"], writes=[xk(db)], tag=f"out{tau}_{db}")
            tk.dma("sync", lambda s: s.dma_start(
                out=outT.rearrange("(c p) t -> p c t", p=128)[:, :, t0:t0 + TT],
                in_=X2[:, :].rearrange("p (c t) -> p c t", t=TT)),
                reads=[xk(c) for c in range(NCH)], writes=["outT"], tag=f"store{tau}")

        for tau in range(NTILES):
            p2_tile(tau)

        p2s.close()
        g_scope.close()

        tk.barrier("final")

        tk.resolve()
        sems = {}
        for e in ENGINES:
            sems[("ev", e)] = stack.enter_context(nc.semaphore(f"ev_{e}"))
        for e in ("sync", "gpsimd"):
            for ln in range(NDMALANES):
                sems[("dma", e, ln)] = stack.enter_context(
                    nc.semaphore(f"dma_{e}_{ln}"))
        sems[("cc", "cc")] = stack.enter_context(nc.semaphore("ccsem"))
        block = stack.enter_context(nc.Block())
        tk.emit(block, sems)

    return nc, taps, tk.stats



def kernel(L0_attn_x, L0_attn_ln1_g, L0_attn_ln1_b, L0_attn_wq, L0_attn_wk,
           L0_attn_wv, L0_attn_wo, L0_attn_bo, L0_attn_ln2_g, L0_attn_ln2_b,
           L0_attn_w_ff1, L0_attn_b_ff1, L0_attn_w_ff2, L0_attn_b_ff2,
           L0_attn_w_d, L0_attn_b_d, L0_ffn_router_w, L0_ffn_w1, L0_ffn_w2,
           L0_ffn_expert_scale):
    f32 = np.float32

    def arr(a):
        return np.ascontiguousarray(np.asarray(a), dtype=f32)

    x = arr(L0_attn_x).reshape(TTOT, D)
    xTh = np.ascontiguousarray(x.T)
    w1_ = arr(L0_ffn_w1)
    w1g = np.ascontiguousarray(
        w1_.reshape(E, 4, 1024, NCH, 128).transpose(0, 1, 4, 3, 2)
    ).reshape(E * 4 * 128, NCH * 1024)
    w2_ = arr(L0_ffn_w2)
    w2g = np.ascontiguousarray(
        w2_.reshape(E, 4, 256, NFC, 128).transpose(0, 1, 4, 3, 2)
    ).reshape(E * 4 * 128, NFC * 256)
    rTh = np.ascontiguousarray(arr(L0_ffn_router_w).T)
    iotas = (np.arange(128, dtype=f32)[:, None]
             + 128.0 * np.arange(8, dtype=f32)[None, :])
    lstrict = np.tril(np.ones((128, 128), f32), -1).T
    lincl8 = np.tril(np.ones((8, 8), f32), 0).T
    eidx = np.broadcast_to(np.arange(8, dtype=f32)[None, :], (128, 8)).copy()

    def vecr(a, n):
        return np.ascontiguousarray(arr(a).reshape(n, 128).T)

    shared = {
        "wq": arr(L0_attn_wq), "wk": arr(L0_attn_wk), "wv": arr(L0_attn_wv),
        "wo": arr(L0_attn_wo), "wf1": arr(L0_attn_w_ff1), "wf2": arr(L0_attn_w_ff2),
        "wd": arr(L0_attn_w_d), "w1g": w1g, "w2g": w2g, "rT": rTh,
        "iotas": iotas, "lstrict": lstrict, "lincl8": lincl8, "eidx": eidx,
        "vg1": vecr(L0_attn_ln1_g, NCH), "vb1": vecr(L0_attn_ln1_b, NCH),
        "vg2": vecr(L0_attn_ln2_g, NCH), "vb2": vecr(L0_attn_ln2_b, NCH),
        "vbo": vecr(L0_attn_bo, NCH), "vbf2": vecr(L0_attn_b_ff2, NCH),
        "vbd": vecr(L0_attn_b_d, NCH), "vbf1": vecr(L0_attn_b_ff1, NFC),
        "vesc": arr(L0_ffn_expert_scale).reshape(1, E),
        "idn": np.eye(128, dtype=f32),
    }
    in_maps = []
    for c in range(NCORES):
        m = dict(shared)
        m["xT"] = np.ascontiguousarray(xTh[:, c * T:(c + 1) * T])
        in_maps.append(m)

    nc, taps, stats = build_program()
    res = run_bass_kernel_spmd(nc, in_maps, core_ids=list(range(NCORES)))

    out = np.empty((TTOT, D), dtype=f32)
    for c in range(NCORES):
        out[c * T:(c + 1) * T, :] = np.asarray(res.results[c]["outT"]).T
    if taps:
        kernel.tap_results = [
            {k: np.asarray(res.results[c]["tap_" + k]) for k in taps}
            for c in range(NCORES)]
        kernel.build_stats = stats
    return out.reshape(1, TTOT, D)
